# version 11
# baseline (speedup 1.0000x reference)
import os
import sys
sys.path.insert(0, "/opt/trn_rl_repo")
os.environ.setdefault("TRN_ALLOW_ICI_ALL_TO_ALL", "1")
from contextlib import ExitStack

import numpy as np
import concourse.bass as bass
import concourse.mybir as mybir
from concourse.bass_utils import run_bass_kernel_spmd

F32 = mybir.dt.float32
BF16 = mybir.dt.bfloat16
I32 = mybir.dt.int32
AF = mybir.ActivationFunctionType
ALU = mybir.AluOpType
AXX = mybir.AxisListType.X

L = 4096
DM = 1024
DI = 2048
DH = 1024
NST = 16
DTR = 64
DFF = 4096
TOWN = 2048
TC = 512
NCH = L // TC
NJ = NCH * 8
TC2 = 256
NF5 = TOWN // TC
NF6 = TOWN // TC2
NPC = 128
PCW = 512
RG = [[0, 1], [2, 3], [4, 5], [6, 7]]

DEBUG_OUT = False


def wpiece(w_up_t, w_dn_t, j):
    if j < 64:
        sj, q = divmod(j, 8)
        return w_up_t[sj * 128:(sj + 1) * 128, q * PCW:(q + 1) * PCW]
    js, q = divmod(j - 64, 2)
    return w_dn_t[js * 128:(js + 1) * 128, q * PCW:(q + 1) * PCW]


def wtdst(d_wtu, d_wtd, j):
    if j < 64:
        sj, q = divmod(j, 8)
        return d_wtu[sj * 128:(sj + 1) * 128, q * PCW:(q + 1) * PCW]
    js, q = divmod(j - 64, 2)
    return d_wtd[js * 128:(js + 1) * 128, q * PCW:(q + 1) * PCW]


def build():
    nc = bass.Bass("TRN2", target_bir_lowering=False,
                   detect_race_conditions=False)

    TAU_P = max(1, NCH // 2)
    Q1 = -(-NPC // TAU_P)
    NB2 = NCH - TAU_P
    Q2 = -(-NPC // NB2)
    Q2J = -(-NPC // (NB2 * 8))

    x_seq = nc.dram_tensor("x_seq", [L, DM], F32, kind="ExternalInput")
    x_own = nc.dram_tensor("x_own", [TOWN, DM], F32, kind="ExternalInput")
    w_in = nc.dram_tensor("w_in", [DM, 2 * DH], F32, kind="ExternalInput")
    w_xp = nc.dram_tensor("w_xp", [DH, 96], F32, kind="ExternalInput")
    w_dt = nc.dram_tensor("w_dt", [DTR, DH], F32, kind="ExternalInput")
    b_dt = nc.dram_tensor("b_dt", [128, 8], F32, kind="ExternalInput")
    a_log = nc.dram_tensor("a_log", [128, 128], F32, kind="ExternalInput")
    conv_w4 = nc.dram_tensor("conv_w4", [128, 32], F32, kind="ExternalInput")
    conv_b8 = nc.dram_tensor("conv_b8", [128, 8], F32, kind="ExternalInput")
    dsk8 = nc.dram_tensor("dsk8", [128, 8], F32, kind="ExternalInput")
    w_out = nc.dram_tensor("w_out", [DI, DM], F32, kind="ExternalInput")
    w_up_t = nc.dram_tensor("w_up_t", [DM, DFF], F32, kind="ExternalInput")
    w_dn_t = nc.dram_tensor("w_dn_t", [DFF, DM], F32, kind="ExternalInput")
    eye = nc.dram_tensor("eye", [128, 128], F32, kind="ExternalInput")
    onesm = nc.dram_tensor("onesm", [128, 128], F32, kind="ExternalInput")
    selbc = nc.dram_tensor("selbc", [32, 32 * 128], F32, kind="ExternalInput")
    roff = nc.dram_tensor("roff", [1, 2], I32, kind="ExternalInput")
    cvals = nc.dram_tensor("cvals", [128, 8], F32, kind="ExternalInput")

    out = nc.dram_tensor("out", [TOWN, DM], F32, kind="ExternalOutput")

    dk = "ExternalOutput" if DEBUG_OUT else "Internal"
    d_xbld = nc.dram_tensor("d_xbld", [DH, L], BF16, kind=dk)
    d_gate = nc.dram_tensor("d_gate", [DH, L], BF16, kind=dk)
    d_x2 = nc.dram_tensor("d_x2", [DM, TOWN], F32, kind=dk)
    d_xn = nc.dram_tensor("d_xn", [DM, TOWN], BF16, kind=dk)
    d_wtu = nc.dram_tensor("d_wtu", [DM, DFF], BF16, kind=dk)
    d_wtd = nc.dram_tensor("d_wtd", [DFF, DM], BF16, kind=dk)
    d_arin = nc.dram_tensor("d_arin", [NCH, 96, TC], F32)
    d_arout = nc.dram_tensor("d_arout", [NCH, 96, TC], F32)
    d_ystore = nc.dram_tensor("d_ystore", [2, DH, TOWN], BF16)
    d_agy = nc.dram_tensor("d_agy", [4, DH, TOWN], BF16)

    with ExitStack() as eg:
        SBg = lambda n, s, d: eg.enter_context(nc.sbuf_tensor(n, s, d))
        blk = eg.enter_context(nc.Block())
        sem = {}
        for n in ["lc", "sq1", "vln", "tp", "xlf", "ipm", "ipe",
                  "cv", "sil", "xpm", "xpe",
                  "tnA", "tnV", "arv", "arl", "dcp", "p2v", "p2s", "p2t",
                  "lnv", "lns", "nmv",
                  "bcm", "bce", "mmd", "spd", "dlv", "dAc", "scn", "cmv",
                  "gat", "a2a", "wol",
                  "opm", "tp5", "tpe5", "x2e", "sq5", "st5",
                  "vst5", "ast5", "nb5", "opb", "sse", "xne",
                  "wtl", "upm", "gel", "hsq", "h2m", "s2v", "sbe",
                  "dnm", "x3e", "tp6", "tpe6"]:
            sem[n] = eg.enter_context(nc.semaphore("c_" + n))
        for n in ["xt", "slab", "ug", "ytl", "x5", "xnl", "xbst", "gtst",
                  "xds", "ygst", "x2st", "xnst", "ost", "wtst"]:
            ns = 4 if n == "xt" else 2
            sem[n] = [eg.enter_context(nc.semaphore("c_" + n + str(i)))
                      for i in range(ns)]

        identf = SBg("identf", [128, 128], F32)
        identb = SBg("identb", [128, 128], BF16)
        ones_f = SBg("ones_f", [128, 128], F32)
        ones_b = SBg("ones_b", [128, 128], BF16)
        bdt_s = SBg("bdt_s", [128, 8], F32)
        b2_s = SBg("b2_s", [128, 8], F32)
        alog_s = SBg("alog_s", [128, 128], F32)
        aneg_s = SBg("aneg_s", [128, 128], F32)
        cw_s = SBg("cw_s", [128, 32], F32)
        cb_s = SBg("cb_s", [128, 8], F32)
        dsk_s = SBg("dsk_s", [128, 8], F32)
        hcarry = SBg("hcarry", [128, 128], F32)
        asums = SBg("asums", [128, NPC], F32)
        tau_b = SBg("tau_b", [128, 2], F32)
        tauv = SBg("tauv", [128, 8], F32)
        cval_s = SBg("cval_s", [128, 8], F32)
        for _i, _v in enumerate([1.0, 1e-5, 1e-35, -1.0, 0.5, -0.5]):
            nc.const_aps.register(F32, _v, cval_s[:, _i:_i + 1])

        with ExitStack() as e13:
            SB = lambda n, s, d: e13.enter_context(nc.sbuf_tensor(n, s, d))
            PS = lambda n, s, d: e13.enter_context(nc.psum_tensor(n, s, d))
            slab = SB("slab", [128, 2 * PCW], F32)
            tscr = SB("tscr", [128, PCW], F32)
            sgn_s = SB("sgn_s", [128, PCW], BF16)
            msk_s = SB("msk_s", [128, PCW], BF16)
            wt_s = SB("wt_s", [128, 2 * PCW], BF16)
            w_dt_b = SB("w_dt_b", [64, 1024], BF16)
            sel_b = SB("sel_b", [32, 32 * 128], BF16)
            w_in_b = SB("w_in_b", [128, 8 * 2048], BF16)
            w_xp_b = SB("w_xp_b", [128, 8 * 96], BF16)
            xt_s = SB("xt_s", [128, 4 * DM], F32)
            sq_s = SB("sq_s", [128, DM], F32)
            s1_s = SB("s1_s", [128, 4], F32)
            ssq_s = SB("ssq_s", [128, 4], F32)
            mva = SB("mva", [128, 24], F32)
            xln_tm = SB("xln_tm", [128, 4 * DM], BF16)
            xln_fm = SB("xln_fm", [128, 8 * TC], BF16)
            xc_buf = SB("xc_buf", [128, 8 * (TC + 6)], BF16)
            cacc = SB("cacc", [128, 2 * TC], F32)
            xbld_c = SB("xbld_c", [128, 8 * TC], BF16)
            gate_c = SB("gate_c", [128, 4 * TC], BF16)
            xdp = SB("xdp", [96, TC], F32)
            xdbl = SB("xdbl", [96, 2 * TC], F32)
            dtl_b = SB("dtl_b", [64, 2 * TC], BF16)
            bc_b = SB("bc_b", [32, 2 * TC], BF16)
            Bb = SB("Bb", [128, NST * TC], BF16)
            Cb = SB("Cb", [128, NST * TC], BF16)
            spA = SB("spA", [128, TC], F32)
            spB = SB("spB", [128, TC], F32)
            dlt_s = SB("dlt_s", [128, 2 * TC], BF16)
            du_s = SB("du_s", [128, 2 * TC], BF16)
            u_s = SB("u_s", [128, 2 * TC], BF16)
            gt_s = SB("gt_s", [128, 2 * TC], BF16)
            dA_s = SB("dA_s", [128, 4 * TC], F32)
            bta_s = SB("bta_s", [128, TC], BF16)
            h_s = SB("h_s", [128, NST * TC], BF16)
            yg_s = SB("yg_s", [128, 2 * TC], BF16)
            y1_s = SB("y1_s", [128, TC], F32)

            ps_tr = PS("ps_tr", [128, 4 * TC], BF16)
            ps_ip = PS("ps_ip", [128, 2 * TC], F32)
            ps_bc = PS("ps_bc", [128, 2 * TC], F32)
            ps_dl = PS("ps_dl", [128, 2 * TC], F32)

            def slabv(j):
                return slab[:, (j % 2) * PCW:(j % 2) * PCW + PCW]

            def wtv(j):
                return wt_s[:, (j % 2) * PCW:(j % 2) * PCW + PCW]

            def xtv(gi):
                return xt_s[:, (gi % 4) * DM:(gi % 4) * DM + DM]

            def xlntv(i):
                return xln_tm[:, i * DM:(i + 1) * DM]

            def xlnfv(k):
                return xln_fm[:, k * TC:(k + 1) * TC]

            def xcv(dt):
                return xc_buf[:, dt * (TC + 6):(dt + 1) * (TC + 6)]

            def caccv(dt):
                return cacc[:, (dt % 2) * TC:(dt % 2) * TC + TC]

            def xbldv(k):
                return xbld_c[:, k * TC:(k + 1) * TC]

            def gatev(dt):
                return gate_c[:, (dt % 4) * TC:(dt % 4) * TC + TC]

            def ipsv(i):
                return ps_ip[:, (i % 2) * TC:(i % 2) * TC + TC]

            def trv(dt):
                return ps_tr[:, (dt % 2) * 2 * TC:(dt % 2) * 2 * TC + TC]

            def b2v(t_, j):
                return t_[:, (j % 2) * TC:(j % 2) * TC + TC]

            def xdblv(tc, r0, r1):
                return xdbl[r0:r1, (tc % 2) * TC:(tc % 2) * TC + TC]

            LC_N = (9 + 8 + 8 + 1 + 1 + 1) * 16

            def sy_all(sy):
                def ystore(pj):
                    ptc, pdt = divmod(pj, 8)
                    sy.wait_ge(sem["gat"], pj + 1)
                    chh = NCH // 2
                    dst = d_ystore[ptc // chh][
                        pdt * 128:(pdt + 1) * 128,
                        (ptc % chh) * TC:(ptc % chh + 1) * TC]
                    sy.dma_start(dst, b2v(yg_s, pj)).then_inc(
                        sem["ygst"][pj % 2], 16)

                for a in range(NCH + 1):
                    if a >= 1:
                        sy.wait_ge(sem["xpe"], a)
                        if a >= 3:
                            sy.wait_ge(sem["xds"][(a - 1) % 2],
                                       16 * ((a - 1) // 2))
                        sy.dma_start(
                            d_arin[a - 1][:, :],
                            xdp[:, :]).then_inc(sem["xds"][(a - 1) % 2], 16)
                    if a < NCH:
                        if a >= 1:
                            sy.wait_ge(sem["vln"], 4 * a)
                        for i in range(4):
                            gi = 4 * a + i
                            sy.dma_start(
                                xtv(gi),
                                x_seq[gi * 128:(gi + 1) * 128, :]
                            ).then_inc(sem["xt"][i], 16)
                        for dt in range(8):
                            sy.wait_ge(sem["ipe"], 16 * a + 8 + dt + 1)
                            sy.dma_start(
                                d_gate[dt * 128:(dt + 1) * 128,
                                       a * TC:(a + 1) * TC],
                                gatev(dt)).then_inc(sem["gtst"][a % 2], 16)
                    for dt in range(8):
                        if a >= 1:
                            tc = a - 1
                            j = 8 * tc + dt
                            if dt >= 2:
                                sy.wait_ge(sem["gat"], j - 1)
                                sy.dma_start(
                                    b2v(u_s, j),
                                    d_xbld[dt * 128:(dt + 1) * 128,
                                           tc * TC:(tc + 1) * TC]
                                ).then_inc(sem["ug"][j % 2], 16)
                                sy.dma_start(
                                    b2v(gt_s, j),
                                    d_gate[dt * 128:(dt + 1) * 128,
                                           tc * TC:(tc + 1) * TC]
                                ).then_inc(sem["ug"][j % 2], 16)
                            if j >= 1:
                                ystore(j - 1)
                        if a < NCH:
                            sy.wait_ge(sem["sil"], 8 * a + dt + 1)
                            sy.dma_start(
                                d_xbld[dt * 128:(dt + 1) * 128,
                                       a * TC:(a + 1) * TC],
                                xbldv(dt)).then_inc(sem["xbst"][a % 2], 16)
                    if a < NCH:
                        for dt in (0, 1):
                            j = 8 * a + dt
                            if j >= 2:
                                sy.wait_ge(sem["gat"], j - 1)
                            sy.wait_ge(sem["xbst"][a % 2],
                                       128 * (a // 2 + 1))
                            sy.dma_start(
                                b2v(u_s, j),
                                d_xbld[dt * 128:(dt + 1) * 128,
                                       a * TC:(a + 1) * TC]
                            ).then_inc(sem["ug"][j % 2], 16)
                            sy.wait_ge(sem["gtst"][a % 2],
                                       128 * (a // 2 + 1))
                            sy.dma_start(
                                b2v(gt_s, j),
                                d_gate[dt * 128:(dt + 1) * 128,
                                       a * TC:(a + 1) * TC]
                            ).then_inc(sem["ug"][j % 2], 16)
                ystore(NJ - 1)

            def g_all(g):
                for dst, src in [(identf, eye), (identb, eye),
                                 (ones_f, onesm), (ones_b, onesm),
                                 (bdt_s, b_dt), (alog_s, a_log),
                                 (cw_s, conv_w4), (cb_s, conv_b8),
                                 (dsk_s, dsk8)]:
                    g.dma_start(dst[:, :], src[:, :]).then_inc(sem["lc"], 16)
                for k in range(8):
                    g.dma_start(w_in_b[:, k * 2048:(k + 1) * 2048],
                                w_in[k * 128:(k + 1) * 128, :]
                                ).then_inc(sem["lc"], 16)
                for k in range(8):
                    g.dma_start(w_xp_b[:, k * 96:(k + 1) * 96],
                                w_xp[k * 128:(k + 1) * 128, :]
                                ).then_inc(sem["lc"], 16)
                g.dma_start(w_dt_b[:, :], w_dt[:, :]).then_inc(sem["lc"], 16)
                g.dma_start(sel_b[:, :], selbc[:, :]).then_inc(sem["lc"], 16)
                g.dma_start(cval_s[:, :], cvals[:, :]).then_inc(sem["lc"], 16)

                p1i = [0]
                p2i = [0]

                def pass1(n):
                    for _ in range(n):
                        j = p1i[0]
                        if j >= NPC:
                            return
                        if j >= 2:
                            g.wait_ge(sem["tnA"], j - 1)
                        g.dma_start(slabv(j),
                                    wpiece(w_up_t, w_dn_t, j)
                                    ).then_inc(sem["slab"][j % 2], 16)
                        p1i[0] += 1

                def pass2(n):
                    for _ in range(n):
                        j = p2i[0]
                        if j >= NPC:
                            return
                        if j == 0:
                            g.wait_ge(sem["p2s"], 2)
                        if j >= 2:
                            g.wait_ge(sem["tnV"], j - 1)
                        g.dma_start(slabv(j),
                                    wpiece(w_up_t, w_dn_t, j)
                                    ).then_inc(sem["slab"][j % 2], 16)
                        if j >= 1:
                            g.wait_ge(sem["tnV"], j)
                            g.dma_start(wtdst(d_wtu, d_wtd, j - 1),
                                        wtv(j - 1)
                                        ).then_inc(sem["wtst"][(j - 1) % 2],
                                                   16)
                        p2i[0] += 1

                for a in range(NCH + 1):
                    if a >= 1:
                        tc = a - 1
                        g.wait_ge(sem["xds"][tc % 2], 16 * (tc // 2 + 1))
                        g.collective_compute(
                            "AllReduce", ALU.add, replica_groups=RG,
                            ins=[d_arin[tc][:, :].opt()],
                            outs=[d_arout[tc][:, :].opt()],
                        ).then_inc(sem["arv"], 1)
                        g.wait_ge(sem["arv"], a)
                        if tc >= 2:
                            g.wait_ge(sem["dcp"], tc - 1)
                        g.dma_start(xdblv(tc, 0, 96), d_arout[tc][:, :]
                                    ).then_inc(sem["arl"], 16)
                    if a < TAU_P:
                        pass1(Q1)
                    if a >= 1 and a - 1 >= TAU_P:
                        pass2(Q2)
                while p2i[0] < NPC:
                    pass2(1)
                g.wait_ge(sem["tnV"], NPC)
                g.dma_start(wtdst(d_wtu, d_wtd, NPC - 1),
                            wtv(NPC - 1)).then_inc(sem["wtst"][(NPC - 1) % 2],
                                                   16)

            def s_all(s):
                s.wait_ge(sem["lc"], LC_N)
                s.activation(aneg_s[:, :], alog_s[:, :], AF.Exp,
                             bias=0.0, scale=1.0).then_inc(sem["p2s"], 1)
                w1 = [0]
                w2 = [0]

                def wslab1(n):
                    for _ in range(n):
                        j = w1[0]
                        if j >= NPC:
                            return
                        s.wait_ge(sem["slab"][j % 2], 16 * (j // 2 + 1))
                        s.activation(tscr[:, :], slabv(j), AF.Abs,
                                     bias=0.0, scale=1.0,
                                     accum_out=asums[:, j:j + 1]
                                     ).then_inc(sem["tnA"], 1)
                        w1[0] += 1

                def wslab2(n):
                    for _ in range(n):
                        j = w2[0]
                        if j >= NPC:
                            return
                        s.wait_ge(sem["slab"][j % 2],
                                  16 * (NPC // 2 + j // 2 + 1))
                        if j >= 1:
                            s.wait_ge(sem["tnV"], j)
                        s.activation(tscr[:, :], slabv(j), AF.Abs,
                                     bias=0.0, scale=1.0)
                        s.activation(sgn_s[:, :], slabv(j), AF.Sign,
                                     bias=0.0,
                                     scale=1.0).then_inc(sem["tnA"], 1)
                        w2[0] += 1

                def softplus(j):
                    tc_, dt_ = divmod(j, 8)
                    s.wait_ge(sem["mmd"], j + 1)
                    if j >= 1:
                        s.wait_ge(sem["dlv"], j)
                    b2c = b2_s[:, dt_:dt_ + 1]
                    s.activation(spA[:, :], b2v(ps_dl, j),
                                 AF.Abs, bias=b2c, scale=1.0)
                    s.activation(spB[:, :], spA[:, :],
                                 AF.Exp, bias=0.0, scale=-1.0)
                    s.activation(spA[:, :], spB[:, :],
                                 AF.Ln, bias=1.0, scale=1.0)
                    s.activation(spB[:, :], b2v(ps_dl, j),
                                 AF.Relu, bias=b2c,
                                 scale=1.0).then_inc(sem["spd"], 1)

                def xbsilu(a_, dt):
                    s.wait_ge(sem["cv"], 2 * (8 * a_ + dt) + 1)
                    if a_ >= 1:
                        s.wait_ge(sem["xbst"][(a_ - 1) % 2],
                                  128 * ((a_ - 1) // 2) + 16 * (dt + 1))
                        s.wait_ge(sem["xpm"], a_)
                    s.activation(xbldv(dt), caccv(dt), AF.Silu,
                                 bias=cb_s[:, dt:dt + 1], scale=1.0
                                 ).then_inc(sem["sil"], 1)

                def xdpev(a_):
                    s.wait_ge(sem["xpm"], a_ + 1)
                    if a_ >= 1:
                        s.wait_ge(sem["xds"][(a_ - 1) % 2],
                                  16 * ((a_ - 1) // 2 + 1))
                    s.copy(xdp[:, :],
                           ps_ip[0:96, TC:2 * TC]).then_inc(sem["xpe"], 1)

                def front(tc):
                    if tc == TAU_P:
                        s.wait_ge(sem["p2t"], 2)
                        s.copy(tau_b[:, 0:2],
                               ps_dl[:, TC + 4:TC + 6]).then_inc(sem["p2s"],
                                                                 1)
                    if tc == 0:
                        s.wait_ge(sem["p2v"], 2)
                    s.wait_ge(sem["arl"], 16 * (tc + 1))
                    if tc >= 2:
                        s.wait_ge(sem["mmd"], 8 * (tc - 1))
                        s.wait_ge(sem["bcm"], 32 * (tc - 1))
                    s.copy(b2v(dtl_b, tc), xdblv(tc, 0, 64))
                    s.copy(b2v(bc_b, tc),
                           xdblv(tc, 64, 96)).then_inc(sem["dcp"], 1)
                    if tc >= 1:
                        s.wait_ge(sem["cmv"], 8 * tc)
                    for n in range(32):
                        i = 32 * tc + n
                        s.wait_ge(sem["bcm"], i + 1)
                        row = n // 2
                        dstb = Bb if (n % 2) == 0 else Cb
                        s.copy(dstb[:, row * TC:(row + 1) * TC],
                               b2v(ps_bc, i)).then_inc(sem["bce"], 1)
                    softplus(8 * tc)

                for a in range(NCH + 1):
                    if a < NCH:
                        for i in range(4):
                            gi = 4 * a + i
                            s.wait_ge(sem["xt"][i], 16 * (a + 1))
                            s.activation(sq_s[:, :], xtv(gi), AF.Square,
                                         bias=0.0, scale=1.0,
                                         accum_out=ssq_s[:, i:i + 1]
                                         ).then_inc(sem["sq1"], 1)
                            s.wait_ge(sem["lnv"], 4 * a + i + 1)
                            var = mva[:, 4 * i + 1:4 * i + 2]
                            rstd = mva[:, 4 * i + 2:4 * i + 3]
                            t1 = mva[:, 4 * i + 3:4 * i + 4]
                            s.activation(t1, var, AF.Ln, bias=0.0, scale=1.0)
                            s.activation(rstd, t1, AF.Exp, bias=0.0,
                                         scale=-0.5).then_inc(sem["lns"], 1)
                        if a >= 1:
                            front(a - 1)
                        for dt in range(8):
                            s.wait_ge(sem["tp"], 8 * a + dt + 1)
                            if a >= 1:
                                s.wait_ge(sem["ipm"], 16 * (a - 1) + 16)
                            s.copy(xlnfv(dt), trv(dt)).then_inc(sem["xlf"], 1)
                        for f in range(16):
                            s.wait_ge(sem["ipm"], 16 * a + f + 1)
                            if f < 8:
                                if a >= 1:
                                    s.wait_ge(sem["cv"],
                                              2 * (8 * (a - 1) + f) + 2)
                                s.copy(xcv(f)[:, 6:TC + 6],
                                       ipsv(f)).then_inc(sem["ipe"], 1)
                            else:
                                dt = f - 8
                                if dt >= 4:
                                    s.wait_ge(sem["gtst"][a % 2],
                                              128 * (a // 2) + 16 * (dt - 3))
                                elif a >= 1:
                                    s.wait_ge(sem["gtst"][(a - 1) % 2],
                                              128 * ((a - 1) // 2)
                                              + 16 * (dt + 5))
                                s.activation(gatev(dt), ipsv(f),
                                             AF.Silu, bias=0.0, scale=1.0
                                             ).then_inc(sem["ipe"], 1)
                        if a == 0:
                            for dt in range(8):
                                xbsilu(0, dt)
                            xdpev(0)
                    if a >= 1:
                        tc = a - 1
                        if a == NCH:
                            front(tc)
                        for dt in range(8):
                            j = 8 * tc + dt
                            if dt >= 1:
                                softplus(j)
                            for n in range(NST):
                                idx = 16 * j + n
                                s.wait_ge(sem["dlv"], j + 1)
                                if idx >= 4:
                                    s.wait_ge(sem["scn"], idx - 3)
                                col = dt * 16 + n
                                s.activation(
                                    dA_s[:, (idx % 4) * TC:
                                         (idx % 4) * TC + TC],
                                    b2v(dlt_s, j), AF.Exp, bias=0.0,
                                    scale=aneg_s[:, col:col + 1]
                                ).then_inc(sem["dAc"], 1)
                            if a < NCH:
                                xbsilu(a, dt)
                            if tc >= TAU_P:
                                wslab2(Q2J)
                        if a < NCH:
                            xdpev(a)
                    if a < TAU_P:
                        wslab1(Q1)
                while w1[0] < NPC:
                    wslab1(1)
                while w2[0] < NPC:
                    wslab2(1)

            def v_all(v):
                for dt in range(8):
                    v.memset(xcv(dt)[:, 0:6], 0.0)
                v.wait_ge(sem["lc"], LC_N)
                v.tensor_tensor(b2_s[:, :], bdt_s[:, :], bdt_s[:, :],
                                ALU.add)
                v.memset(hcarry[:, :], 0.0).then_inc(sem["p2v"], 1)
                v.wait_ge(sem["p2s"], 1)
                v.tensor_single_scalar(aneg_s[:, :], aneg_s[:, :], -1.0,
                                       ALU.mult).then_inc(sem["p2v"], 1)
                t2 = [0]

                def tern2(n):
                    for _ in range(n):
                        j = t2[0]
                        if j >= NPC:
                            return
                        v.wait_ge(sem["tnA"], NPC + j + 1)
                        if j >= 2:
                            v.wait_ge(sem["wtst"][j % 2],
                                      16 * ((j - 2) // 2 + 1))
                        tcol = (tau_b[:, 0:1] if j < 64
                                else tau_b[:, 1:2])
                        v.tensor_scalar(msk_s[:, :], tscr[:, :],
                                        tcol, None, ALU.is_gt)
                        v.tensor_tensor(wtv(j), msk_s[:, :],
                                        sgn_s[:, :],
                                        ALU.mult).then_inc(sem["tnV"], 1)
                        t2[0] += 1

                for a in range(NCH + 1):
                    if a == TAU_P:
                        v.wait_ge(sem["tnA"], NPC)
                        v.reduce_sum(tauv[:, 0:1], asums[:, 0:64], axis=AXX)
                        v.reduce_sum(tauv[:, 1:2], asums[:, 64:NPC],
                                     axis=AXX).then_inc(sem["p2v"], 1)
                    if a == TAU_P + 1:
                        v.wait_ge(sem["p2t"], 1)
                        v.tensor_single_scalar(tauv[0:1, 2:3],
                                               ps_dl[0:1, TC:TC + 1],
                                               0.5 / (DM * DFF), ALU.mult)
                        v.tensor_single_scalar(tauv[0:1, 3:4],
                                               ps_dl[0:1, TC + 1:TC + 2],
                                               0.5 / (DM * DFF),
                                               ALU.mult).then_inc(sem["p2v"],
                                                                  1)
                    def conv(a_, dt):
                        v.wait_ge(sem["ipe"], 16 * a_ + dt + 1)
                        if a_ >= 1 or dt >= 2:
                            v.wait_ge(sem["sil"], 8 * a_ + dt - 1)
                        acc = caccv(dt)
                        xc = xcv(dt)
                        v.tensor_scalar(acc, xc[:, 0:TC],
                                        cw_s[:, 4 * dt:4 * dt + 1], None,
                                        ALU.mult)
                        for k in range(1, 4):
                            ins = v.scalar_tensor_tensor(
                                acc, xc[:, k:k + TC],
                                cw_s[:, 4 * dt + k:4 * dt + k + 1],
                                acc, ALU.mult, ALU.add)
                        ins.then_inc(sem["cv"], 1)
                        v.tensor_copy(xc[:, 0:6],
                                      xc[:, TC:TC + 6]).then_inc(sem["cv"], 1)

                    if a < NCH:
                        for i in range(4):
                            gi = 4 * a + i
                            v.wait_ge(sem["xt"][i], 16 * (a + 1))
                            v.reduce_sum(s1_s[:, i:i + 1], xtv(gi), axis=AXX)
                        for i in range(4):
                            v.wait_ge(sem["sq1"], 4 * a + i + 1)
                            m = mva[:, 4 * i:4 * i + 1]
                            var = mva[:, 4 * i + 1:4 * i + 2]
                            t1 = mva[:, 4 * i + 3:4 * i + 4]
                            v.tensor_single_scalar(m, s1_s[:, i:i + 1],
                                                   1.0 / DM, ALU.mult)
                            v.tensor_single_scalar(var, ssq_s[:, i:i + 1],
                                                   1.0 / DM, ALU.mult)
                            v.tensor_tensor(t1, m, m, ALU.mult)
                            v.tensor_tensor(var, var, t1, ALU.subtract)
                            v.tensor_single_scalar(var, var, 1e-5,
                                                   ALU.add
                                                   ).then_inc(sem["lnv"], 1)
                        for i in range(4):
                            gi = 4 * a + i
                            v.wait_ge(sem["lns"], 4 * a + i + 1)
                            if i == 0 and a >= 1:
                                v.wait_ge(sem["tp"], 8 * (a - 1) + 8)
                            m = mva[:, 4 * i:4 * i + 1]
                            rstd = mva[:, 4 * i + 2:4 * i + 3]
                            v.tensor_scalar(xlntv(i), xtv(gi), m[:, 0:1],
                                            rstd[:, 0:1], ALU.subtract,
                                            ALU.mult).then_inc(sem["vln"], 1)
                        if a == 0:
                            for dt in range(8):
                                conv(0, dt)
                    if a >= 1:
                        tc = a - 1
                        for dt in range(8):
                            j = 8 * tc + dt
                            v.wait_ge(sem["spd"], j + 1)
                            if j >= 2:
                                v.wait_ge(sem["dAc"], 16 * (j - 2) + 16)
                            v.tensor_tensor(b2v(dlt_s, j), spB[:, :],
                                            spA[:, :],
                                            ALU.add).then_inc(sem["dlv"], 1)
                            v.wait_ge(sem["ug"][j % 2], 32 * (j // 2 + 1))
                            v.tensor_tensor(b2v(du_s, j), b2v(dlt_s, j),
                                            b2v(u_s, j), ALU.mult)
                            if dt == 0:
                                v.wait_ge(sem["bce"], 32 * (tc + 1))
                            for n in range(NST):
                                idx = 16 * j + n
                                v.wait_ge(sem["dAc"], idx + 1)
                                v.tensor_tensor(
                                    bta_s[:, :], b2v(du_s, j),
                                    Bb[:, n * TC:(n + 1) * TC], ALU.mult)
                                v.tensor_tensor_scan(
                                    h_s[:, n * TC:(n + 1) * TC],
                                    dA_s[:, (idx % 4) * TC:
                                         (idx % 4) * TC + TC],
                                    bta_s[:, :],
                                    hcarry[:, dt * 16 + n:dt * 16 + n + 1],
                                    ALU.mult, ALU.add).then_inc(sem["scn"], 1)
                            v.wait_ge(sem["scn"], 16 * (j + 1))
                            v.tensor_copy(
                                hcarry[:, dt * 16:(dt + 1) * 16],
                                bass.AP(h_s, TC - 1,
                                        [[NST * TC, 128], [TC, NST]]))
                            v.tensor_tensor(h_s[:, :], h_s[:, :],
                                            Cb[:, :], ALU.mult)
                            v.tensor_tensor(h_s[:, 0:8 * TC],
                                            h_s[:, 0:8 * TC],
                                            h_s[:, 8 * TC:16 * TC], ALU.add)
                            v.tensor_tensor(h_s[:, 0:4 * TC],
                                            h_s[:, 0:4 * TC],
                                            h_s[:, 4 * TC:8 * TC], ALU.add)
                            v.tensor_tensor(h_s[:, 0:2 * TC],
                                            h_s[:, 0:2 * TC],
                                            h_s[:, 2 * TC:4 * TC], ALU.add)
                            v.tensor_tensor(h_s[:, 0:TC], h_s[:, 0:TC],
                                            h_s[:, TC:2 * TC],
                                            ALU.add).then_inc(sem["cmv"], 1)
                            if j >= 2:
                                v.wait_ge(sem["ygst"][j % 2],
                                          16 * ((j - 2) // 2 + 1))
                            v.tensor_scalar(y1_s[:, :], b2v(u_s, j),
                                            dsk_s[:, dt:dt + 1], None,
                                            ALU.mult)
                            v.tensor_tensor(y1_s[:, :], y1_s[:, :],
                                            h_s[:, 0:TC], ALU.add)
                            v.tensor_tensor(b2v(yg_s, j), y1_s[:, :],
                                            b2v(gt_s, j),
                                            ALU.mult).then_inc(sem["gat"], 1)
                            if a < NCH:
                                conv(a, dt)
                            if tc >= TAU_P:
                                tern2(Q2J)
                while t2[0] < NPC:
                    tern2(1)

            def t_all(t):
                t.wait_ge(sem["lc"], LC_N)

                def dmm(tc, dt):
                    j = 8 * tc + dt
                    if j >= 2:
                        t.wait_ge(sem["spd"], j - 1)
                    if j == 8 * TAU_P + 1:
                        t.wait_ge(sem["p2s"], 2)
                    t.matmul(b2v(ps_dl, j),
                             w_dt_b[:, dt * 128:(dt + 1) * 128],
                             b2v(dtl_b, tc),
                             start=True, stop=True).then_inc(sem["mmd"], 1)

                for a in range(NCH + 1):
                    if a >= 1:
                        tc = a - 1
                        if tc == TAU_P:
                            t.wait_ge(sem["p2v"], 4)
                            t.matmul(ps_dl[:, TC + 4:TC + 6],
                                     ones_f[0:1, :], tauv[0:1, 2:4],
                                     start=True,
                                     stop=True).then_inc(sem["p2t"], 1)
                        t.wait_ge(sem["dcp"], tc + 1)
                        for n in range(32):
                            i = 32 * tc + n
                            if i >= 2:
                                t.wait_ge(sem["bce"], i - 1)
                            row = (n // 2) + (16 if (n % 2) else 0)
                            t.matmul(b2v(ps_bc, i),
                                     sel_b[:, row * 128:(row + 1) * 128],
                                     b2v(bc_b, tc),
                                     start=True,
                                     stop=True).then_inc(sem["bcm"], 1)
                        dmm(tc, 0)
                        dmm(tc, 1)
                    if a < NCH:
                        for dt in range(8):
                            idx = 8 * a + dt
                            t.wait_ge(sem["vln"], 4 * a + 4)
                            if idx >= 2:
                                t.wait_ge(sem["xlf"], idx - 1)
                            for i in range(4):
                                tr = t.transpose(
                                    trv(dt)[:, i * 128:(i + 1) * 128],
                                    xlntv(i)[:, dt * 128:(dt + 1) * 128],
                                    identb[:, :])
                            tr.then_inc(sem["tp"], 1)
                        for f in range(16):
                            idx = 16 * a + f
                            if f == 0:
                                t.wait_ge(sem["xlf"], 8 * a + 8)
                            if f == 1 and a >= 1:
                                t.wait_ge(sem["xpe"], a)
                            if idx >= 2:
                                t.wait_ge(sem["ipe"], idx - 1)
                            for k in range(8):
                                mm = t.matmul(
                                    ipsv(f),
                                    w_in_b[:, k * 2048 + f * 128:
                                           k * 2048 + (f + 1) * 128],
                                    xlnfv(k),
                                    start=(k == 0), stop=(k == 7))
                            mm.then_inc(sem["ipm"], 1)
                    if a >= 1:
                        for dt in range(2, 8):
                            dmm(a - 1, dt)
                    if a < NCH:
                        t.wait_ge(sem["ipe"], 16 * a + 16)
                        for k in range(8):
                            t.wait_ge(sem["sil"], 8 * a + k + 1)
                            mm = t.matmul(ps_ip[0:96, TC:2 * TC],
                                          w_xp_b[:, k * 96:(k + 1) * 96],
                                          xbldv(k),
                                          start=(k == 0), stop=(k == 7))
                        mm.then_inc(sem["xpm"], 1)
                    if a == TAU_P:
                        t.wait_ge(sem["p2v"], 3)
                        t.wait_ge(sem["spd"], 8 * TAU_P)
                        t.matmul(ps_dl[0:1, TC:TC + 2], ones_f[:, 0:1],
                                 tauv[:, 0:2], start=True,
                                 stop=True).then_inc(sem["p2t"], 1)

            blk.sync(sy_all)
            blk.gpsimd(g_all)
            blk.scalar(s_all)
            blk.vector(v_all)
            blk.tensor(t_all)

        TC5 = 256
        NF5B = TOWN // TC5
        with ExitStack() as e56:
            SB56 = lambda n, s, d: e56.enter_context(nc.sbuf_tensor(n, s, d))
            wtu_r = SB56("wtu_r", [128, 8 * DFF], BF16)

            def g_p45(g):
                g.wait_ge(sem["ygst"][0], 16 * (NJ // 2))
                g.wait_ge(sem["ygst"][1], 16 * (NJ // 2))
                g.wait_ge(sem["wtst"][0], 16 * (NPC // 2))
                g.wait_ge(sem["wtst"][1], 16 * (NPC // 2))
                g.collective_compute(
                    "AllGather", ALU.bypass, replica_groups=RG,
                    ins=[d_ystore.ap().opt()], outs=[d_agy.ap().opt()],
                ).then_inc(sem["a2a"], 1)
                for k in range(16):
                    g.dma_start(wob[:, k * DM:(k + 1) * DM],
                                w_out[k * 128:(k + 1) * 128, :]
                                ).then_inc(sem["wol"], 16)
                with g.register("rsec0") as r0, g.register("rsec1") as r1:
                    g.reg_load(r0, roff[0:1, 0:1])
                    g.reg_load(r1, roff[0:1, 1:2])
                    sec0 = g.snap(r0)
                    sec1 = g.snap(r1)
                    g.wait_ge(sem["a2a"], 1)
                    for fc in range(NF5B):
                        if fc >= 2:
                            g.wait_ge(sem["opm"], 8 * (fc - 1))
                        for half, sec in ((0, sec0), (1, sec1)):
                            srcap = bass.AP(
                                d_agy,
                                sec * (DH * TOWN) + fc * TC5,
                                [[TOWN, 128], [128 * TOWN, 8], [1, TC5]])
                            dstap = bass.AP(
                                ytl,
                                ((fc % 2) * 16 + half * 8) * TC5,
                                [[32 * TC5, 128], [TC5, 8], [1, TC5]])
                            g.dma_start(dstap,
                                        srcap).then_inc(sem["ytl"][fc % 2], 16)

            with ExitStack() as e5:
                SB5 = lambda n, s, d: e5.enter_context(nc.sbuf_tensor(n, s, d))
                PS5 = lambda n, s, d: e5.enter_context(nc.psum_tensor(n, s, d))
                wob = SB5("wob", [128, 16 * DM], BF16)
                ytl = SB5("ytl", [128, 32 * TC5], BF16)
                xt5 = SB5("xt5", [128, 4 * DM], F32)
                xres = SB5("xres", [128, 2 * TC5], F32)
                x2c = SB5("x2c", [128, 16 * TC5], F32)
                sqt = SB5("sqt", [128, 2 * TC5], F32)
                vm = SB5("vm", [1, TC5], F32)
                vv = SB5("vv", [1, TC5], F32)
                vq = SB5("vq", [1, TC5], F32)
                vt1 = SB5("vt1", [1, TC5], F32)
                vt2 = SB5("vt2", [1, TC5], F32)
                vrstd = SB5("vrstd", [1, TC5], F32)
                vrms2 = SB5("vrms2", [1, TC5], F32)
                sv_s = SB5("sv_s", [1, TC5], F32)
                sv_ms = SB5("sv_ms", [1, TC5], F32)
                sb5 = SB5("sb5", [128, 2 * TC5], F32)
                xnc = SB5("xnc", [128, 2 * TC5], BF16)
                xtm5 = SB5("xtm5", [128, TC5], F32)
                ps_o = PS5("ps_o", [128, 2 * TC], F32)
                ps_t5 = PS5("ps_t5", [128, 2 * TC], F32)
                ps_s15 = PS5("ps_s15", [1, TC], F32)
                ps_s25 = PS5("ps_s25", [1, TC], F32)
                ps_b5 = PS5("ps_b5", [128, 2 * TC], F32)

                def ytlv(fc, k):
                    o = ((fc % 2) * 16 + k) * TC5
                    return ytl[:, o:o + TC5]

                def xt5v(fc, i):
                    o = ((fc % 2) * 2 + i) * DM
                    return xt5[:, o:o + DM]

                def x2cv(fc, f):
                    o = ((fc % 2) * 8 + f) * TC5
                    return x2c[:, o:o + TC5]

                def i2v(t_, i):
                    return t_[:, (i % 2) * TC5:(i % 2) * TC5 + TC5]

                def p2v_(ps, i):
                    return ps[:, (i % 2) * TC:(i % 2) * TC + TC5]

                blk.gpsimd(g_p45)

                def sy_p5(sy):
                    sy.wait_ge(sem["wtst"][0], 16 * (NPC // 2))
                    sy.wait_ge(sem["wtst"][1], 16 * (NPC // 2))
                    sy.wait_ge(sem["ygst"][0], 16 * (NJ // 2))
                    sy.wait_ge(sem["ygst"][1], 16 * (NJ // 2))
                    for j in range(8):
                        sy.dma_start(wtu_r[:, j * DFF:(j + 1) * DFF],
                                     d_wtu[j * 128:(j + 1) * 128, :]
                                     ).then_inc(sem["wtl"], 16)

                    def stores(fc):
                        for f in range(8):
                            sy.wait_ge(sem["x2e"], 8 * fc + f + 1)
                            sy.dma_start(
                                d_x2[f * 128:(f + 1) * 128,
                                     fc * TC5:(fc + 1) * TC5],
                                x2cv(fc, f)).then_inc(sem["x2st"][fc % 2], 16)
                            sy.wait_ge(sem["xne"], 8 * fc + f + 1)
                            sy.dma_start(
                                d_xn[f * 128:(f + 1) * 128,
                                     fc * TC5:(fc + 1) * TC5],
                                i2v(xnc, f)).then_inc(sem["xnst"][f % 2], 16)
                    for fc in range(NF5B):
                        if fc >= 2:
                            sy.wait_ge(sem["tp5"], 8 * (fc - 1))
                        for i in range(2):
                            row0 = fc * TC5 + i * 128
                            sy.dma_start(xt5v(fc, i),
                                         x_own[row0:row0 + 128, :]
                                         ).then_inc(sem["x5"][fc % 2], 16)
                        if fc >= 1:
                            stores(fc - 1)
                    stores(NF5B - 1)

                def t_p5(t):
                    t.wait_ge(sem["wol"], 16 * 16)
                    for fc in range(NF5B):
                        t.wait_ge(sem["ytl"][fc % 2], 32 * (fc // 2 + 1))
                        t.wait_ge(sem["x5"][fc % 2], 32 * (fc // 2 + 1))
                        for f in range(8):
                            idx = 8 * fc + f
                            if idx >= 2:
                                t.wait_ge(sem["tpe5"], idx - 1)
                            for i in range(2):
                                tr = t.transpose(
                                    p2v_(ps_t5, f)[:, i * 128:(i + 1) * 128],
                                    xt5v(fc, i)[:, f * 128:(f + 1) * 128],
                                    identf[:, :])
                            tr.then_inc(sem["tp5"], 1)
                            if idx >= 2:
                                t.wait_ge(sem["x2e"], idx - 1)
                            for k in range(16):
                                mm = t.matmul(p2v_(ps_o, f),
                                              wob[:, k * DM + f * 128:
                                                  k * DM + (f + 1) * 128],
                                              ytlv(fc, k),
                                              start=(k == 0), stop=(k == 15))
                            mm.then_inc(sem["opm"], 1)
                        if fc >= 1:
                            t.wait_ge(sem["nb5"], fc)
                        for k in range(8):
                            idx = 8 * fc + k
                            t.wait_ge(sem["x2e"], idx + 1)
                            t.wait_ge(sem["sq5"], idx + 1)
                            t.matmul(ps_s15[0:1, 0:TC5], ones_f[:, 0:1],
                                     x2cv(fc, k),
                                     start=(k == 0), stop=(k == 7))
                            t.matmul(ps_s25[0:1, 0:TC5], ones_f[:, 0:1],
                                     i2v(sqt, k),
                                     start=(k == 0),
                                     stop=(k == 7)).then_inc(sem["st5"], 1)
                        t.wait_ge(sem["nb5"], fc + 1)
                        if fc >= 1:
                            t.wait_ge(sem["sse"], 2 * fc)
                        t.matmul(ps_b5[:, 0:TC5], ones_f[0:1, :],
                                 sv_s[0:1, :], start=True, stop=True)
                        t.matmul(ps_b5[:, TC:TC + TC5], ones_f[0:1, :],
                                 sv_ms[0:1, :], start=True,
                                 stop=True).then_inc(sem["opb"], 1)

                def s_p5(s):
                    for fc in range(NF5B):
                        for dt in range(8):
                            idx = 8 * fc + dt
                            s.wait_ge(sem["tp5"], idx + 1)
                            if idx >= 2:
                                s.wait_ge(sem["x2e"], idx - 1)
                            s.copy(i2v(xres, dt),
                                   p2v_(ps_t5, dt)).then_inc(sem["tpe5"], 1)
                        for k in range(8):
                            idx = 8 * fc + k
                            s.wait_ge(sem["x2e"], idx + 1)
                            if idx >= 2:
                                s.wait_ge(sem["st5"], idx - 1)
                            s.activation(i2v(sqt, k), x2cv(fc, k), AF.Square,
                                         bias=0.0,
                                         scale=1.0).then_inc(sem["sq5"], 1)
                        s.wait_ge(sem["vst5"], fc + 1)
                        s.activation(vt1[0:1, :], vv[0:1, :], AF.Ln,
                                     bias=1e-5, scale=1.0)
                        s.activation(vrstd[0:1, :], vt1[0:1, :], AF.Exp,
                                     bias=0.0, scale=-0.5)
                        s.activation(vt1[0:1, :], vq[0:1, :], AF.Ln,
                                     bias=1e-35, scale=1.0)
                        s.activation(vrms2[0:1, :], vt1[0:1, :], AF.Exp,
                                     bias=0.0, scale=0.5).then_inc(sem["ast5"],
                                                                   1)
                        s.wait_ge(sem["opb"], fc + 1)
                        if fc >= 1:
                            s.wait_ge(sem["xne"], 8 * fc)
                        s.copy(sb5[:, 0:TC5], ps_b5[:, 0:TC5])
                        s.copy(sb5[:, TC5:2 * TC5],
                               ps_b5[:, TC:TC + TC5]).then_inc(sem["sse"], 2)

                def v_p5(v):
                    for fc in range(NF5B):
                        for f in range(8):
                            idx = 8 * fc + f
                            v.wait_ge(sem["tpe5"], idx + 1)
                            v.wait_ge(sem["opm"], idx + 1)
                            if fc >= 2:
                                v.wait_ge(sem["x2st"][fc % 2],
                                          128 * ((fc - 2) // 2 + 1))
                                v.wait_ge(sem["st5"], 8 * (fc - 1))
                                v.wait_ge(sem["xne"], 8 * (fc - 1))
                            v.tensor_tensor(x2cv(fc, f), i2v(xres, f),
                                            p2v_(ps_o, f),
                                            ALU.add).then_inc(sem["x2e"], 1)
                        v.wait_ge(sem["st5"], 8 * fc + 8)
                        v.tensor_copy(vm[0:1, :], ps_s15[0:1, 0:TC5])
                        v.tensor_copy(vv[0:1, :], ps_s25[0:1, 0:TC5])
                        v.tensor_single_scalar(vm[0:1, :], vm[0:1, :],
                                               1.0 / DM, ALU.mult)
                        v.tensor_single_scalar(vv[0:1, :], vv[0:1, :],
                                               1.0 / DM, ALU.mult)
                        v.tensor_tensor(vt1[0:1, :], vm[0:1, :], vm[0:1, :],
                                        ALU.mult)
                        v.tensor_tensor(vv[0:1, :], vv[0:1, :], vt1[0:1, :],
                                        ALU.subtract)
                        v.tensor_single_scalar(vt1[0:1, :], vv[0:1, :], 1e-5,
                                               ALU.add)
                        v.reciprocal(vt2[0:1, :], vt1[0:1, :])
                        v.tensor_tensor(vq[0:1, :], vv[0:1, :], vt2[0:1, :],
                                        ALU.mult).then_inc(sem["vst5"], 1)
                        v.wait_ge(sem["ast5"], fc + 1)
                        v.tensor_single_scalar(vt1[0:1, :], vrms2[0:1, :],
                                               1e-8, ALU.add)
                        v.reciprocal(vt2[0:1, :], vt1[0:1, :])
                        v.tensor_tensor(sv_s[0:1, :], vrstd[0:1, :],
                                        vt2[0:1, :], ALU.mult)
                        v.tensor_tensor(sv_ms[0:1, :], vm[0:1, :],
                                        sv_s[0:1, :],
                                        ALU.mult).then_inc(sem["nb5"], 1)
                        v.wait_ge(sem["sse"], 2 * fc + 2)
                        for k in range(8):
                            idx = 8 * fc + k
                            if idx >= 2:
                                v.wait_ge(sem["xnst"][k % 2],
                                          16 * (4 * fc + k // 2))
                            v.tensor_tensor(xtm5[:, :], x2cv(fc, k),
                                            sb5[:, 0:TC5], ALU.mult)
                            v.tensor_tensor(i2v(xnc, k), xtm5[:, :],
                                            sb5[:, TC5:2 * TC5],
                                            ALU.subtract).then_inc(sem["xne"],
                                                                   1)

                blk.sync(sy_p5)
                blk.tensor(t_p5)
                blk.scalar(s_p5)
                blk.vector(v_p5)

            with ExitStack() as e6:
                SB6 = lambda n, s, d: e6.enter_context(nc.sbuf_tensor(n, s, d))
                PS6 = lambda n, s, d: e6.enter_context(nc.psum_tensor(n, s, d))
                wtd_r = SB6("wtd_r", [128, 32 * DM], BF16)
                xn6 = SB6("xn6", [128, 16 * TC2], BF16)
                x26 = SB6("x26", [128, 16 * TC2], F32)
                h6 = SB6("h6", [128, 32 * TC2], BF16)
                sqh = SB6("sqh", [128, 2 * TC2], BF16)
                w6a = SB6("w6a", [1, TC2], F32)
                w6b = SB6("w6b", [1, TC2], F32)
                w6c = SB6("w6c", [1, TC2], F32)
                w6d = SB6("w6d", [1, TC2], F32)
                sv_s2 = SB6("sv_s2", [1, TC2], F32)
                s2b6 = SB6("s2b6", [128, TC2], F32)
                x36 = SB6("x36", [128, 8 * TC2], F32)
                xtm6 = SB6("xtm6", [128, TC2], F32)
                ostg = SB6("ostg", [128, 2 * DM], F32)
                ps_u = PS6("ps_u", [128, 4 * TC2], F32)
                ps_d = PS6("ps_d", [128, 4 * TC2], F32)
                ps_h2 = PS6("ps_h2", [1, 2 * TC2], F32)
                ps_b6 = PS6("ps_b6", [128, 2 * TC2], F32)
                ps_to = PS6("ps_to", [128, 2 * 512], F32)

                def xn6v(fc, k):
                    o = ((fc % 2) * 8 + k) * TC2
                    return xn6[:, o:o + TC2]

                def x26v(fc, k):
                    o = ((fc % 2) * 8 + k) * TC2
                    return x26[:, o:o + TC2]

                def h6v(f):
                    return h6[:, f * TC2:(f + 1) * TC2]

                def x36v(f):
                    return x36[:, f * TC2:(f + 1) * TC2]

                def j2v(t_, i):
                    return t_[:, (i % 2) * TC2:(i % 2) * TC2 + TC2]

                def jpv(ps, i):
                    return ps[:, (i % 2) * 2 * TC2:(i % 2) * 2 * TC2 + TC2]

                def tov(g2):
                    return ps_to[:, g2 * 512:(g2 + 1) * 512]

                def ostv(i):
                    return ostg[:, i * DM:(i + 1) * DM]

                def sy_p6(sy):
                    sy.wait_ge(sem["x2st"][0], 128 * (NF5B // 2))
                    sy.wait_ge(sem["x2st"][1], 128 * (NF5B // 2))
                    sy.wait_ge(sem["xnst"][0], 16 * 4 * NF5B)
                    sy.wait_ge(sem["xnst"][1], 16 * 4 * NF5B)
                    for j in range(32):
                        sy.dma_start(wtd_r[:, j * DM:(j + 1) * DM],
                                     d_wtd[j * 128:(j + 1) * 128, :]
                                     ).then_inc(sem["wtl"], 16)

                    def ostores(fc):
                        for i in range(2):
                            sy.wait_ge(sem["tpe6"], 4 * fc + 2 * (i + 1))
                            sy.dma_start(
                                out[fc * TC2 + i * 128:
                                    fc * TC2 + (i + 1) * 128, :],
                                ostv(i)).then_inc(sem["ost"][i], 16)
                    for fc in range(NF6):
                        if fc >= 2:
                            sy.wait_ge(sem["upm"], 32 * (fc - 1))
                            sy.wait_ge(sem["x3e"], 8 * (fc - 1))
                        for k in range(8):
                            sy.wait_ge(sem["x2st"][fc % 2],
                                       128 * (fc // 2 + 1))
                            sy.wait_ge(sem["xnst"][k % 2],
                                       16 * (4 * fc + k // 2 + 1))
                            sy.dma_start(xn6v(fc, k),
                                         d_xn[k * 128:(k + 1) * 128,
                                              fc * TC2:(fc + 1) * TC2]
                                         ).then_inc(sem["xnl"][fc % 2], 16)
                            sy.dma_start(x26v(fc, k),
                                         d_x2[k * 128:(k + 1) * 128,
                                              fc * TC2:(fc + 1) * TC2]
                                         ).then_inc(sem["xnl"][fc % 2], 16)
                        if fc >= 1:
                            ostores(fc - 1)
                    ostores(NF6 - 1)
                    sy.wait_ge(sem["ost"][0], 16 * NF6)
                    sy.wait_ge(sem["ost"][1], 16 * NF6)

                def t_p6(t):
                    t.wait_ge(sem["wtl"], 16 * 40)
                    for fc in range(NF6):
                        t.wait_ge(sem["xnl"][fc % 2], 256 * (fc // 2 + 1))
                        if fc >= 1:
                            t.wait_ge(sem["dnm"], 8 * fc)
                        for f in range(32):
                            idx = 32 * fc + f
                            if idx >= 2:
                                t.wait_ge(sem["gel"], idx - 1)
                            for k in range(8):
                                mm = t.matmul(jpv(ps_u, f),
                                              wtu_r[:, k * DFF + f * 128:
                                                    k * DFF + (f + 1) * 128],
                                              xn6v(fc, k),
                                              start=(k == 0), stop=(k == 7))
                            mm.then_inc(sem["upm"], 1)
                            if f >= 2:
                                g2 = f - 2
                                if g2 == 0:
                                    t.wait_ge(sem["s2v"], fc)
                                t.wait_ge(sem["hsq"], 32 * fc + g2 + 1)
                                t.matmul(ps_h2[0:1, 0:TC2], ones_b[:, 0:1],
                                         j2v(sqh, g2),
                                         start=(g2 == 0),
                                         stop=False).then_inc(sem["h2m"], 1)
                        for g2 in (30, 31):
                            t.wait_ge(sem["hsq"], 32 * fc + g2 + 1)
                            t.matmul(ps_h2[0:1, 0:TC2], ones_b[:, 0:1],
                                     j2v(sqh, g2),
                                     start=False,
                                     stop=(g2 == 31)).then_inc(sem["h2m"], 1)
                        t.wait_ge(sem["s2v"], fc + 1)
                        if fc >= 1:
                            t.wait_ge(sem["sbe"], fc)
                        t.matmul(ps_b6[:, 0:TC2], ones_f[0:1, :],
                                 sv_s2[0:1, :], start=True,
                                 stop=True).then_inc(sem["opb"], 1)
                        t.wait_ge(sem["gel"], 32 * (fc + 1))
                        for fo in range(8):
                            idx = 8 * fc + fo
                            if idx >= 2:
                                t.wait_ge(sem["x3e"], idx - 1)
                            for k in range(32):
                                mm = t.matmul(jpv(ps_d, fo),
                                              wtd_r[:, k * DM + fo * 128:
                                                    k * DM + (fo + 1) * 128],
                                              h6v(k),
                                              start=(k == 0), stop=(k == 31))
                            mm.then_inc(sem["dnm"], 1)
                        for i in range(2):
                            for g2 in range(2):
                                tpidx = 4 * fc + 2 * i + g2
                                if tpidx >= 2:
                                    t.wait_ge(sem["tpe6"], tpidx - 1)
                                for q_ in range(4):
                                    dt = g2 * 4 + q_
                                    t.wait_ge(sem["x3e"], 8 * fc + dt + 1)
                                    tr = t.transpose(
                                        tov(g2)[:, q_ * 128:(q_ + 1) * 128],
                                        x36v(dt)[:, i * 128:(i + 1) * 128],
                                        identf[:, :])
                                tr.then_inc(sem["tp6"], 1)

                def s_p6(s):
                    for fc in range(NF6):
                        for f in range(32):
                            idx = 32 * fc + f
                            if f == 0 and fc >= 1:
                                s.wait_ge(sem["dnm"], 8 * fc)
                            s.wait_ge(sem["upm"], idx + 1)
                            s.activation(h6v(f), jpv(ps_u, f), AF.Gelu,
                                         bias=0.0,
                                         scale=1.0).then_inc(sem["gel"], 1)
                            if f >= 2:
                                s.wait_ge(sem["h2m"], 32 * fc + f - 1)
                            s.activation(j2v(sqh, f), h6v(f), AF.Square,
                                         bias=0.0,
                                         scale=1.0).then_inc(sem["hsq"], 1)
                        s.wait_ge(sem["lnv"], 4 * NCH + fc + 1)
                        s.activation(w6d[0:1, :], w6a[0:1, :], AF.Ln,
                                     bias=1e-35, scale=1.0)
                        s.activation(w6c[0:1, :], w6d[0:1, :], AF.Exp,
                                     bias=0.0, scale=0.5).then_inc(sem["lns"],
                                                                   1)
                        s.wait_ge(sem["opb"], NF5B + fc + 1)
                        if fc >= 1:
                            s.wait_ge(sem["x3e"], 8 * fc)
                        s.copy(s2b6[:, :],
                               ps_b6[:, 0:TC2]).then_inc(sem["sbe"], 1)
                        for i in range(2):
                            for g2 in range(2):
                                tpidx = 4 * fc + 2 * i + g2
                                s.wait_ge(sem["tp6"], tpidx + 1)
                                if fc >= 1:
                                    s.wait_ge(sem["ost"][i], 16 * fc)
                                s.copy(ostv(i)[:, g2 * 512:(g2 + 1) * 512],
                                       tov(g2)).then_inc(sem["tpe6"], 1)

                def v_p6(v):
                    for fc in range(NF6):
                        v.wait_ge(sem["h2m"], 32 * (fc + 1))
                        v.tensor_copy(w6a[0:1, :],
                                      ps_h2[0:1, 0:TC2]).then_inc(sem["lnv"],
                                                                  1)
                        v.wait_ge(sem["lns"], 4 * NCH + fc + 1)
                        v.tensor_single_scalar(w6b[0:1, :], w6c[0:1, :],
                                               1.0 / 64.0, ALU.mult)
                        v.tensor_single_scalar(w6b[0:1, :], w6b[0:1, :],
                                               1e-8, ALU.add)
                        v.reciprocal(sv_s2[0:1, :],
                                     w6b[0:1, :]).then_inc(sem["s2v"], 1)
                        v.wait_ge(sem["sbe"], fc + 1)
                        for fo in range(8):
                            idx = 8 * fc + fo
                            v.wait_ge(sem["dnm"], idx + 1)
                            if fc >= 1 and fo == 0:
                                v.wait_ge(sem["tp6"], 4 * fc)
                            v.tensor_tensor(xtm6[:, :], jpv(ps_d, fo),
                                            s2b6[:, :], ALU.mult)
                            v.tensor_tensor(x36v(fo), xtm6[:, :],
                                            x26v(fc, fo),
                                            ALU.add).then_inc(sem["x3e"], 1)

                blk.sync(sy_p6)
                blk.tensor(t_p6)
                blk.scalar(s_p6)
                blk.vector(v_p6)

    return nc


_CACHE = {}


def kernel(L0_attn_hidden_states, L0_attn_in_proj_w, L0_attn_conv_w,
           L0_attn_conv_b, L0_attn_x_proj_w, L0_attn_dt_proj_w,
           L0_attn_dt_proj_b, L0_attn_A_log, L0_attn_D_skip,
           L0_attn_out_proj_w, L0_ffn_w_up, L0_ffn_w_down):
    h = np.asarray(L0_attn_hidden_states, dtype=np.float32)
    ipw = np.asarray(L0_attn_in_proj_w, dtype=np.float32)
    cw = np.asarray(L0_attn_conv_w, dtype=np.float32)
    cb = np.asarray(L0_attn_conv_b, dtype=np.float32)
    xpw = np.asarray(L0_attn_x_proj_w, dtype=np.float32)
    dtw = np.asarray(L0_attn_dt_proj_w, dtype=np.float32)
    dtb = np.asarray(L0_attn_dt_proj_b, dtype=np.float32)
    alog = np.asarray(L0_attn_A_log, dtype=np.float32)
    dsk = np.asarray(L0_attn_D_skip, dtype=np.float32)
    opw = np.asarray(L0_attn_out_proj_w, dtype=np.float32)
    wup = np.asarray(L0_ffn_w_up, dtype=np.float32)
    wdn = np.asarray(L0_ffn_w_down, dtype=np.float32)

    B = h.shape[0]
    eyem = np.eye(128, dtype=np.float32)
    ones = np.ones((128, 128), dtype=np.float32)
    selm = np.zeros((32, 32 * 128), dtype=np.float32)
    for n in range(32):
        selm[n, n * 128:(n + 1) * 128] = 1.0
    wupt = np.ascontiguousarray(wup.T)
    wdnt = np.ascontiguousarray(wdn.T)

    if "nc" not in _CACHE:
        _CACHE["nc"] = build()
    ncp = _CACHE["nc"]

    def dslice(vec, r):
        return np.ascontiguousarray(
            vec[r * DH:(r + 1) * DH].reshape(8, 128).T)

    in_maps = []
    for c in range(8):
        b, rr = c >> 1, c & 1
        in_maps.append({
            "x_seq": np.ascontiguousarray(h[b]),
            "x_own": np.ascontiguousarray(h[b, rr * TOWN:(rr + 1) * TOWN]),
            "w_in": np.ascontiguousarray(np.concatenate(
                [ipw[:, rr * DH:(rr + 1) * DH],
                 ipw[:, DI + rr * DH:DI + (rr + 1) * DH]], axis=1)),
            "w_xp": np.ascontiguousarray(xpw[rr * DH:(rr + 1) * DH, :]),
            "w_dt": np.ascontiguousarray(dtw[:, rr * DH:(rr + 1) * DH]),
            "b_dt": dslice(dtb, rr),
            "a_log": np.ascontiguousarray(
                alog[rr * DH:(rr + 1) * DH].reshape(8, 128, NST)
                .transpose(1, 0, 2).reshape(128, 128)),
            "conv_w4": np.ascontiguousarray(
                cw[rr * DH:(rr + 1) * DH].reshape(8, 128, 4)
                .transpose(1, 0, 2).reshape(128, 32)),
            "conv_b8": dslice(cb, rr),
            "dsk8": dslice(dsk, rr),
            "w_out": opw,
            "w_up_t": wupt,
            "w_dn_t": wdnt,
            "eye": eyem,
            "onesm": ones,
            "selbc": selm,
            "roff": np.array([[rr, rr + 2]], dtype=np.int32),
            "cvals": np.tile(np.array(
                [[1.0, 1e-5, 1e-35, -1.0, 0.5, -0.5, 0.0, 0.0]],
                dtype=np.float32), (128, 1)),
        })

    res = run_bass_kernel_spmd(ncp, in_maps, core_ids=list(range(8)))
    _CACHE["res"] = res

    out_full = np.empty((B, L, DM), dtype=np.float32)
    for c in range(8):
        b, rr = c >> 1, c & 1
        out_full[b, rr * TOWN:(rr + 1) * TOWN, :] = res.results[c]["out"]
    return out_full


# version 12
# speedup vs baseline: 1.2153x; 1.2153x over previous
import os
import sys
sys.path.insert(0, "/opt/trn_rl_repo")
os.environ.setdefault("TRN_ALLOW_ICI_ALL_TO_ALL", "1")
from contextlib import ExitStack

import numpy as np
import concourse.bass as bass
import concourse.mybir as mybir
from concourse.bass_utils import run_bass_kernel_spmd

F32 = mybir.dt.float32
BF16 = mybir.dt.bfloat16
I32 = mybir.dt.int32
AF = mybir.ActivationFunctionType
ALU = mybir.AluOpType
AXX = mybir.AxisListType.X

L = 4096
DM = 1024
DI = 2048
DH = 1024
NST = 16
DTR = 64
DFF = 4096
TOWN = 2048
TC = 512
NCH = L // TC
NJ = NCH * 8
TC2 = 256
NF5 = TOWN // TC
NF6 = TOWN // TC2
NPC = 64
MAGIC = 0x5F3759DF
RG = [[0, 1], [2, 3], [4, 5], [6, 7]]

DEBUG_OUT = False


def nrsqrt(v, dst, src, ti, t1, t2, iters=3):
    v.tensor_single_scalar(ti, src.bitcast(I32), 1, ALU.logical_shift_right)
    v.tensor_single_scalar(ti, ti, MAGIC, ALU.subtract)
    v.tensor_single_scalar(dst.bitcast(I32), ti, -1, ALU.mult)
    for _ in range(iters):
        v.tensor_tensor(t1, src, dst, ALU.mult)
        v.tensor_tensor(t1, t1, dst, ALU.mult)
        v.tensor_scalar(t2, t1, -0.5, 1.5, ALU.mult, ALU.add)
        v.tensor_tensor(dst, dst, t2, ALU.mult)


def wpiece(w_up_t, w_dn_t, j):
    if j < 32:
        sj, q = divmod(j, 4)
        return w_up_t[sj * 128:(sj + 1) * 128, q * 1024:(q + 1) * 1024]
    js = j - 32
    return w_dn_t[js * 128:(js + 1) * 128, :]


def build():
    nc = bass.Bass("TRN2", target_bir_lowering=False,
                   detect_race_conditions=False)

    x_seq = nc.dram_tensor("x_seq", [L, DM], F32, kind="ExternalInput")
    x_own = nc.dram_tensor("x_own", [TOWN, DM], F32, kind="ExternalInput")
    w_in = nc.dram_tensor("w_in", [DM, 2 * DH], F32, kind="ExternalInput")
    w_xp = nc.dram_tensor("w_xp", [DH, 96], F32, kind="ExternalInput")
    w_dt = nc.dram_tensor("w_dt", [DTR, DH], F32, kind="ExternalInput")
    b_dt = nc.dram_tensor("b_dt", [128, 8], F32, kind="ExternalInput")
    a_log = nc.dram_tensor("a_log", [128, 128], F32, kind="ExternalInput")
    conv_w4 = nc.dram_tensor("conv_w4", [128, 32], F32, kind="ExternalInput")
    conv_b8 = nc.dram_tensor("conv_b8", [128, 8], F32, kind="ExternalInput")
    dsk8 = nc.dram_tensor("dsk8", [128, 8], F32, kind="ExternalInput")
    w_out = nc.dram_tensor("w_out", [DI, DM], F32, kind="ExternalInput")
    w_up_t = nc.dram_tensor("w_up_t", [DM, DFF], F32, kind="ExternalInput")
    w_dn_t = nc.dram_tensor("w_dn_t", [DFF, DM], F32, kind="ExternalInput")
    eye = nc.dram_tensor("eye", [128, 128], F32, kind="ExternalInput")
    onesm = nc.dram_tensor("onesm", [128, 128], F32, kind="ExternalInput")
    selbc = nc.dram_tensor("selbc", [32, 32 * 128], F32, kind="ExternalInput")
    roff = nc.dram_tensor("roff", [1, 2], I32, kind="ExternalInput")
    cvals = nc.dram_tensor("cvals", [128, 8], F32, kind="ExternalInput")

    out = nc.dram_tensor("out", [TOWN, DM], F32, kind="ExternalOutput")

    dk = "ExternalOutput" if DEBUG_OUT else "Internal"
    d_xbld = nc.dram_tensor("d_xbld", [DH, L], BF16, kind=dk)
    d_gate = nc.dram_tensor("d_gate", [DH, L], BF16, kind=dk)
    d_x2 = nc.dram_tensor("d_x2", [DM, TOWN], F32, kind=dk)
    d_xn = nc.dram_tensor("d_xn", [DM, TOWN], BF16, kind=dk)
    d_wtu = nc.dram_tensor("d_wtu", [DM, DFF], BF16, kind=dk)
    d_wtd = nc.dram_tensor("d_wtd", [DFF, DM], BF16, kind=dk)
    d_arin = nc.dram_tensor("d_arin", [96, L], F32)
    d_arout = nc.dram_tensor("d_arout", [96, L], F32)
    d_ystore = nc.dram_tensor("d_ystore", [2, DH, TOWN], BF16)
    d_agy = nc.dram_tensor("d_agy", [4, DH, TOWN], BF16)
    if DEBUG_OUT:
        d_dbgy = nc.dram_tensor("d_dbgy", [2, DH, TOWN], BF16,
                                kind="ExternalOutput")
        d_dbga = nc.dram_tensor("d_dbga", [4, DH, TOWN], BF16,
                                kind="ExternalOutput")

    with ExitStack() as eg:
        SBg = lambda n, s, d: eg.enter_context(nc.sbuf_tensor(n, s, d))
        blk = eg.enter_context(nc.Block())
        sem = {}
        for n in ["lc", "sq1", "vln", "tp", "xlf", "ipm", "ipe",
                  "cv", "sil", "xpm", "xpe",
                  "tnA", "tnV", "arv", "arl", "p2v", "p2s", "p2t", "lnv", "lns",
                  "bcm", "bce", "mmd", "spd", "dlv", "dAc", "scn", "cmv",
                  "gat", "a2a", "wol",
                  "opm", "tp5", "tpe5", "x2e", "sq5", "st5",
                  "vst5", "ast5", "nb5", "opb", "sse", "xne",
                  "wtl", "upm", "gel", "hsq", "h2m", "s2v", "sbe",
                  "dnm", "x3e", "tp6", "tpe6"]:
            sem[n] = eg.enter_context(nc.semaphore("c_" + n))
        for n in ["xt", "slab", "ug", "ytl", "x5", "xnl", "xbst", "gtst",
                  "xds", "ygst", "x2st", "xnst", "ost", "wtst"]:
            sem[n] = [eg.enter_context(nc.semaphore("c_" + n + str(i)))
                      for i in range(2)]

        identf = SBg("identf", [128, 128], F32)
        identb = SBg("identb", [128, 128], BF16)
        ones_f = SBg("ones_f", [128, 128], F32)
        ones_b = SBg("ones_b", [128, 128], BF16)
        bdt_s = SBg("bdt_s", [128, 8], F32)
        b2_s = SBg("b2_s", [128, 8], F32)
        alog_s = SBg("alog_s", [128, 128], F32)
        aneg_s = SBg("aneg_s", [128, 128], F32)
        cw_s = SBg("cw_s", [128, 32], F32)
        cb_s = SBg("cb_s", [128, 8], F32)
        dsk_s = SBg("dsk_s", [128, 8], F32)
        hcarry = SBg("hcarry", [128, 128], F32)
        asums = SBg("asums", [128, NPC], F32)
        tau_b = SBg("tau_b", [128, 2], F32)
        tauv = SBg("tauv", [128, 8], F32)
        cval_s = SBg("cval_s", [128, 8], F32)
        for _i, _v in enumerate([1.0, 1e-5, 1e-35, -1.0, 0.5, -0.5]):
            nc.const_aps.register(F32, _v, cval_s[:, _i:_i + 1])

        with ExitStack() as e13:
            SB13 = lambda n, s, d: e13.enter_context(nc.sbuf_tensor(n, s, d))
            slab = SB13("slab", [128, 2048], F32)
            tscr = SB13("tscr", [128, 1024], F32)
            sgn_s = SB13("sgn_s", [128, 1024], BF16)
            msk_s = SB13("msk_s", [128, 1024], BF16)
            wt_s = SB13("wt_s", [128, 2048], BF16)
            w_dt_b = SB13("w_dt_b", [64, 1024], BF16)
            sel_b = SB13("sel_b", [32, 32 * 128], BF16)
            ps_tau = e13.enter_context(nc.psum_tensor("ps_tau", [128, 512], F32))

            def slabv(j):
                return slab[:, (j % 2) * 1024:(j % 2) * 1024 + 1024]

            def wtv(j):
                return wt_s[:, (j % 2) * 1024:(j % 2) * 1024 + 1024]

            if True:
                e1 = ExitStack()
                SB1 = lambda n, s, d: e1.enter_context(nc.sbuf_tensor(n, s, d))
                PS1 = lambda n, s, d: e1.enter_context(nc.psum_tensor(n, s, d))
                w_in_b = SB1("w_in_b", [128, 8 * 2048], BF16)
                w_xp_b = SB1("w_xp_b", [128, 8 * 96], BF16)
                xt_s = SB1("xt_s", [128, 8 * DM], F32)
                sq_s = SB1("sq_s", [128, DM], F32)
                s1_s = SB1("s1_s", [128, 8], F32)
                ssq_s = SB1("ssq_s", [128, 8], F32)
                mva = SB1("mva", [128, 16], F32)
                xln_tm = SB1("xln_tm", [128, 4 * DM], BF16)
                xln_fm = SB1("xln_fm", [128, 8 * TC], BF16)
                xc_buf = SB1("xc_buf", [128, 8 * (TC + 6)], F32)
                cacc = SB1("cacc", [128, 2 * TC], F32)
                xbld_c = SB1("xbld_c", [128, 16 * TC], BF16)
                gate_c = SB1("gate_c", [128, 8 * TC], BF16)
                xdp = SB1("xdp", [96, 2 * TC], F32)

                ps_tr = PS1("ps_tr", [128, 4 * TC], BF16)
                ps_ip = PS1("ps_ip", [128, 2 * TC], F32)
                ps_xd = PS1("ps_xd", [128, TC], F32)

                def xtv(a, i):
                    o = ((a % 2) * 4 + i) * DM
                    return xt_s[:, o:o + DM]

                def xlntv(i):
                    return xln_tm[:, i * DM:(i + 1) * DM]

                def xlnfv(k):
                    return xln_fm[:, k * TC:(k + 1) * TC]

                def xcv(dt):
                    return xc_buf[:, dt * (TC + 6):(dt + 1) * (TC + 6)]

                def caccv(dt):
                    return cacc[:, (dt % 2) * TC:(dt % 2) * TC + TC]

                def xbldv(a, k):
                    o = ((a % 2) * 8 + k) * TC
                    return xbld_c[:, o:o + TC]

                def gatev(dt):
                    return gate_c[:, dt * TC:(dt + 1) * TC]

                def ipsv(ps, i):
                    return ps[:, (i % 2) * TC:(i % 2) * TC + TC]

                def trv(i):
                    return ps_tr[:, (i % 2) * 2 * TC:(i % 2) * 2 * TC + TC]


                def sy_p1(sy):
                    def stores(a):
                        for dt in range(8):
                            sy.wait_ge(sem["sil"], 8 * a + dt + 1)
                            sy.dma_start(
                                d_xbld[dt * 128:(dt + 1) * 128,
                                       a * TC:(a + 1) * TC],
                                xbldv(a, dt)).then_inc(sem["xbst"][a % 2], 16)
                            sy.wait_ge(sem["ipe"], 16 * a + 8 + dt + 1)
                            sy.dma_start(
                                d_gate[dt * 128:(dt + 1) * 128,
                                       a * TC:(a + 1) * TC],
                                gatev(dt)).then_inc(sem["gtst"][a % 2], 16)
                        sy.wait_ge(sem["xpe"], a + 1)
                        sy.dma_start(
                            d_arin[:, a * TC:(a + 1) * TC],
                            xdp[:, (a % 2) * TC:(a % 2) * TC + TC]
                        ).then_inc(sem["xds"][a % 2], 16)
                    for a in range(NCH):
                        if a >= 2:
                            sy.wait_ge(sem["vln"], 4 * (a - 2) + 4)
                            sy.wait_ge(sem["sq1"], 4 * (a - 2) + 4)
                        for i in range(4):
                            sy.dma_start(
                                xtv(a, i),
                                x_seq[a * TC + i * 128:a * TC + (i + 1) * 128,
                                      :]).then_inc(sem["xt"][a % 2], 16)
                        if a >= 1:
                            stores(a - 1)
                    stores(NCH - 1)

                def g_p1(g):
                    for dst, src in [(identf, eye), (identb, eye),
                                     (ones_f, onesm), (ones_b, onesm),
                                     (bdt_s, b_dt), (alog_s, a_log),
                                     (cw_s, conv_w4), (cb_s, conv_b8),
                                     (dsk_s, dsk8)]:
                        g.dma_start(dst[:, :], src[:, :]).then_inc(sem["lc"], 16)
                    for k in range(8):
                        g.dma_start(w_in_b[:, k * 2048:(k + 1) * 2048],
                                    w_in[k * 128:(k + 1) * 128, :]
                                    ).then_inc(sem["lc"], 16)
                    for k in range(8):
                        g.dma_start(w_xp_b[:, k * 96:(k + 1) * 96],
                                    w_xp[k * 128:(k + 1) * 128, :]
                                    ).then_inc(sem["lc"], 16)
                    g.dma_start(w_dt_b[:, :], w_dt[:, :]).then_inc(sem["lc"], 16)
                    g.dma_start(sel_b[:, :], selbc[:, :]).then_inc(sem["lc"], 16)
                    g.dma_start(cval_s[:, :], cvals[:, :]).then_inc(sem["lc"], 16)
                    for j in range(NPC):
                        if j >= 2:
                            g.wait_ge(sem["tnA"], j - 1)
                        g.dma_start(slabv(j),
                                    wpiece(w_up_t, w_dn_t, j)
                                    ).then_inc(sem["slab"][j % 2], 16)

                LC_N = (9 + 8 + 8 + 1 + 1 + 1) * 16

                def s_p1(s):
                    wsl = [0]

                    def do_wslab():
                        j = wsl[0]
                        if j < NPC:
                            s.wait_ge(sem["slab"][j % 2],
                                      16 * (j // 2 + 1))
                            s.activation(tscr[:, :], slabv(j), AF.Abs,
                                         bias=0.0, scale=1.0,
                                         accum_out=asums[:, j:j + 1]
                                         ).then_inc(sem["tnA"], 1)
                            wsl[0] += 1

                    for a in range(NCH):
                        for i in range(4):
                            s.wait_ge(sem["xt"][a % 2],
                                      64 * (a // 2 + 1))
                            s.activation(sq_s[:, :], xtv(a, i), AF.Square,
                                         bias=0.0, scale=1.0,
                                         accum_out=ssq_s[:, i:i + 1]
                                         ).then_inc(sem["sq1"], 1)
                        for i in range(4):
                            s.wait_ge(sem["lnv"], 4 * a + i + 1)
                            var = mva[:, 4 * i + 1:4 * i + 2]
                            rstd = mva[:, 4 * i + 2:4 * i + 3]
                            t1 = mva[:, 4 * i + 3:4 * i + 4]
                            s.activation(t1, var, AF.Ln, bias=0.0, scale=1.0)
                            s.activation(rstd, t1, AF.Exp, bias=0.0,
                                         scale=-0.5).then_inc(sem["lns"], 1)
                        for dt in range(8):
                            s.wait_ge(sem["tp"], 8 * a + dt + 1)
                            if a >= 1:
                                s.wait_ge(sem["ipm"], 16 * (a - 1) + 16)
                            s.copy(xlnfv(dt),
                                   trv(dt)).then_inc(sem["xlf"], 1)
                        for f in range(16):
                            s.wait_ge(sem["ipm"], 16 * a + f + 1)
                            if f < 8:
                                if a >= 1:
                                    s.wait_ge(sem["cv"],
                                              2 * (8 * (a - 1) + f) + 2)
                                s.copy(xcv(f)[:, 6:TC + 6],
                                       ipsv(ps_ip, f)).then_inc(sem["ipe"], 1)
                            else:
                                dt = f - 8
                                if a >= 1:
                                    s.wait_ge(sem["gtst"][(a - 1) % 2],
                                              128 * ((a - 1) // 2 + 1))
                                s.activation(gatev(dt), ipsv(ps_ip, f),
                                             AF.Silu, bias=0.0, scale=1.0
                                             ).then_inc(sem["ipe"], 1)
                        for dt in range(8):
                            s.wait_ge(sem["cv"], 2 * (8 * a + dt) + 1)
                            if a >= 2:
                                s.wait_ge(sem["xbst"][a % 2],
                                          128 * ((a - 2) // 2 + 1))
                                s.wait_ge(sem["xpm"], a - 1)
                            s.activation(xbldv(a, dt), caccv(dt), AF.Silu,
                                         bias=cb_s[:, dt:dt + 1], scale=1.0
                                         ).then_inc(sem["sil"], 1)
                        for _ in range(8):
                            do_wslab()
                    while wsl[0] < NPC:
                        do_wslab()

                def v_p1(v):
                    for dt in range(8):
                        v.memset(xcv(dt)[:, 0:6], 0.0)
                    for a in range(NCH):
                        for i in range(4):
                            v.wait_ge(sem["xt"][a % 2],
                                      64 * (a // 2 + 1))
                            v.reduce_sum(s1_s[:, i:i + 1], xtv(a, i), axis=AXX)
                        for i in range(4):
                            v.wait_ge(sem["sq1"], 4 * a + i + 1)
                            m = mva[:, 4 * i:4 * i + 1]
                            var = mva[:, 4 * i + 1:4 * i + 2]
                            t1 = mva[:, 4 * i + 3:4 * i + 4]
                            v.tensor_single_scalar(m, s1_s[:, i:i + 1],
                                                   1.0 / DM, ALU.mult)
                            v.tensor_single_scalar(var, ssq_s[:, i:i + 1],
                                                   1.0 / DM, ALU.mult)
                            v.tensor_scalar(t1, m, m[:, 0:1], None, ALU.mult)
                            v.tensor_tensor(var, var, t1, ALU.subtract)
                            v.tensor_single_scalar(var, var, 1e-5,
                                                   ALU.add).then_inc(sem["lnv"], 1)
                        for i in range(4):
                            v.wait_ge(sem["lns"], 4 * a + i + 1)
                            if a >= 1:
                                v.wait_ge(sem["tp"], 8 * (a - 1) + 8)
                            m = mva[:, 4 * i:4 * i + 1]
                            rstd = mva[:, 4 * i + 2:4 * i + 3]
                            v.tensor_scalar(xlntv(i), xtv(a, i), m[:, 0:1],
                                            rstd[:, 0:1], ALU.subtract,
                                            ALU.mult).then_inc(sem["vln"], 1)
                        for dt in range(8):
                            v.wait_ge(sem["ipe"], 16 * a + dt + 1)
                            if a >= 1 or dt >= 2:
                                v.wait_ge(sem["sil"], 8 * a + dt - 1)
                            acc = caccv(dt)
                            xc = xcv(dt)
                            v.tensor_scalar(acc, xc[:, 0:TC],
                                            cw_s[:, 4 * dt:4 * dt + 1], None,
                                            ALU.mult)
                            for k in range(1, 4):
                                ins = v.scalar_tensor_tensor(
                                    acc, xc[:, k:k + TC],
                                    cw_s[:, 4 * dt + k:4 * dt + k + 1],
                                    acc, ALU.mult, ALU.add)
                            ins.then_inc(sem["cv"], 1)
                            v.tensor_copy(xc[:, 0:6],
                                          xc[:, TC:TC + 6]).then_inc(sem["cv"], 1)
                        if a >= 1:
                            v.wait_ge(sem["xds"][a % 2],
                                      16 * ((a - 2) // 2 + 1))
                        v.wait_ge(sem["xpm"], a + 1)
                        v.tensor_copy(xdp[:, (a % 2) * TC:(a % 2) * TC + TC],
                                      ps_xd[0:96, :]).then_inc(sem["xpe"], 1)

                def t_p1(t):
                    t.wait_ge(sem["lc"], LC_N)
                    for a in range(NCH):
                        for dt in range(8):
                            idx = 8 * a + dt
                            t.wait_ge(sem["vln"], 4 * a + 4)
                            if idx >= 2:
                                t.wait_ge(sem["xlf"], idx - 1)
                            for i in range(4):
                                tr = t.transpose(
                                    trv(dt)[:, i * 128:(i + 1) * 128],
                                    xlntv(i)[:, dt * 128:(dt + 1) * 128],
                                    identb[:, :])
                            tr.then_inc(sem["tp"], 1)
                        for f in range(16):
                            idx = 16 * a + f
                            if f == 0:
                                t.wait_ge(sem["xlf"], 8 * a + 8)
                            if idx >= 2:
                                t.wait_ge(sem["ipe"], idx - 1)
                            for k in range(8):
                                mm = t.matmul(
                                    ipsv(ps_ip, f),
                                    w_in_b[:, k * 2048 + f * 128:
                                           k * 2048 + (f + 1) * 128],
                                    xlnfv(k),
                                    start=(k == 0), stop=(k == 7))
                            mm.then_inc(sem["ipm"], 1)
                        if a >= 1:
                            t.wait_ge(sem["xpe"], a)
                        for k in range(8):
                            t.wait_ge(sem["sil"], 8 * a + k + 1)
                            mm = t.matmul(ps_xd[0:96, :],
                                          w_xp_b[:, k * 96:(k + 1) * 96],
                                          xbldv(a, k),
                                          start=(k == 0), stop=(k == 7))
                        mm.then_inc(sem["xpm"], 1)

                blk.sync(sy_p1)
                blk.gpsimd(g_p1)
                blk.scalar(s_p1)
                blk.vector(v_p1)
                blk.tensor(t_p1)
                e1.close()

                with ExitStack() as e23:
                    SB23 = lambda n, s, d: e23.enter_context(
                        nc.sbuf_tensor(n, s, d))
                    xdbl = SB23("xdbl", [96, L], F32)
                    dtl_b = SB23("dtl_b", [64, L], BF16)
                    bc_b = SB23("bc_b", [32, L], BF16)

                    def g_p2(g):
                        g.wait_ge(sem["xds"][0], 16 * (NCH // 2))
                        g.wait_ge(sem["xds"][1], 16 * (NCH // 2))
                        g.collective_compute(
                            "AllReduce", ALU.add, replica_groups=RG,
                            ins=[d_arin.ap().opt()],
                            outs=[d_arout.ap().opt()],
                        ).then_inc(sem["arv"], 1)
                        g.wait_ge(sem["arv"], 1)
                        g.dma_start(xdbl[:, :],
                                    d_arout[:, :]).then_inc(sem["arl"], 16)

                    def s_p2(s):
                        s.wait_ge(sem["lc"], LC_N)
                        s.activation(aneg_s[:, :], alog_s[:, :], AF.Exp,
                                     bias=0.0, scale=1.0).then_inc(sem["p2s"], 1)

                    def v_p2(v):
                        v.wait_ge(sem["arl"], 16)
                        v.tensor_copy(dtl_b[:, :], xdbl[0:64, :])
                        v.tensor_copy(bc_b[:, :], xdbl[64:96, :])
                        v.tensor_tensor(b2_s[:, :], bdt_s[:, :], bdt_s[:, :],
                                        ALU.add)
                        v.memset(hcarry[:, :], 0.0).then_inc(sem["p2v"], 1)
                        v.wait_ge(sem["p2s"], 1)
                        v.tensor_single_scalar(aneg_s[:, :], aneg_s[:, :],
                                               -1.0,
                                               ALU.mult).then_inc(sem["p2v"], 1)
                        v.wait_ge(sem["tnA"], NPC)
                        v.reduce_sum(tauv[:, 0:1], asums[:, 0:32], axis=AXX)
                        v.reduce_sum(tauv[:, 1:2], asums[:, 32:NPC],
                                     axis=AXX).then_inc(sem["p2v"], 1)

                    def t_p2(t):
                        t.wait_ge(sem["p2v"], 3)
                        t.matmul(ps_tau[0:1, 0:1], ones_f[:, 0:1],
                                 tauv[:, 0:1], start=True, stop=True)
                        t.matmul(ps_tau[0:1, 1:2], ones_f[:, 0:1],
                                 tauv[:, 1:2], start=True,
                                 stop=True).then_inc(sem["p2t"], 1)

                    def v_p2b(v):
                        v.wait_ge(sem["p2t"], 1)
                        v.tensor_single_scalar(tauv[0:1, 2:3],
                                               ps_tau[0:1, 0:1],
                                               0.5 / (DM * DFF), ALU.mult)
                        v.tensor_single_scalar(tauv[0:1, 3:4],
                                               ps_tau[0:1, 1:2],
                                               0.5 / (DM * DFF),
                                               ALU.mult).then_inc(sem["p2v"], 1)

                    def t_p2b(t):
                        t.wait_ge(sem["p2v"], 4)
                        t.matmul(ps_tau[:, 2:3], ones_f[0:1, :],
                                 tauv[0:1, 2:3], start=True, stop=True)
                        t.matmul(ps_tau[:, 3:4], ones_f[0:1, :],
                                 tauv[0:1, 3:4], start=True,
                                 stop=True).then_inc(sem["p2t"], 1)

                    def s_p2b(s):
                        s.wait_ge(sem["p2t"], 2)
                        s.copy(tau_b[:, 0:2],
                               ps_tau[:, 2:4]).then_inc(sem["p2s"], 1)

                    blk.gpsimd(g_p2)
                    blk.scalar(s_p2)
                    blk.vector(v_p2)
                    blk.tensor(t_p2)
                    blk.vector(v_p2b)
                    blk.tensor(t_p2b)
                    blk.scalar(s_p2b)

                    with ExitStack() as e3:
                        SB3 = lambda n, s, d: e3.enter_context(
                            nc.sbuf_tensor(n, s, d))
                        PS3 = lambda n, s, d: e3.enter_context(
                            nc.psum_tensor(n, s, d))
                        Bb = SB3("Bb", [128, 2 * NST * TC], BF16)
                        Cb = SB3("Cb", [128, 2 * NST * TC], BF16)
                        spA = SB3("spA", [128, 2 * TC], F32)
                        spB = SB3("spB", [128, 2 * TC], F32)
                        dlt_s = SB3("dlt_s", [128, 2 * TC], BF16)
                        du_s = SB3("du_s", [128, 2 * TC], BF16)
                        u_s = SB3("u_s", [128, 2 * TC], BF16)
                        gt_s = SB3("gt_s", [128, 2 * TC], BF16)
                        dA_s = SB3("dA_s", [128, 4 * TC], F32)
                        bta_s = SB3("bta_s", [128, TC], BF16)
                        h_s = SB3("h_s", [128, NST * TC], BF16)
                        m_s = SB3("m_s", [128, NST * TC], BF16)
                        yg_s = SB3("yg_s", [128, 2 * TC], BF16)
                        y1_s = SB3("y1_s", [128, TC], F32)
                        ps_bc = PS3("ps_bc", [128, 2 * TC], F32)
                        ps_dl = PS3("ps_dl", [128, 2 * TC], F32)

                        def b2v(t_, j):
                            return t_[:, (j % 2) * TC:(j % 2) * TC + TC]

                        def bbv(t_, tc):
                            o = (tc % 2) * NST * TC
                            return t_[:, o:o + NST * TC]

                        def t_p3(t):
                            t.wait_ge(sem["p2v"], 1)
                            t.wait_ge(sem["p2s"], 2)
                            for tc in range(NCH):
                                for n in range(32):
                                    i = 32 * tc + n
                                    if i >= 2:
                                        t.wait_ge(sem["bce"], i - 1)
                                    row = (n // 2) + (16 if (n % 2) else 0)
                                    t.matmul(b2v(ps_bc, i),
                                             sel_b[:, row * 128:(row + 1) * 128],
                                             bc_b[:, tc * TC:(tc + 1) * TC],
                                             start=True,
                                             stop=True).then_inc(sem["bcm"], 1)
                                for dt in range(8):
                                    j = tc * 8 + dt
                                    if j >= 2:
                                        t.wait_ge(sem["spd"], j - 1)
                                    t.matmul(b2v(ps_dl, j),
                                             w_dt_b[:, dt * 128:(dt + 1) * 128],
                                             dtl_b[:, tc * TC:(tc + 1) * TC],
                                             start=True,
                                             stop=True).then_inc(sem["mmd"], 1)

                        def s_p3(s):
                            tern = [0]

                            def do_tern():
                                j = tern[0]
                                if j >= NPC:
                                    return
                                s.wait_ge(sem["slab"][j % 2],
                                          16 * (NPC // 2 + j // 2 + 1))
                                if j >= 1:
                                    s.wait_ge(sem["tnV"], j)
                                s.activation(tscr[:, :], slabv(j), AF.Abs,
                                             bias=0.0, scale=1.0)
                                s.activation(sgn_s[:, :], slabv(j), AF.Sign,
                                             bias=0.0,
                                             scale=1.0).then_inc(sem["tnA"], 1)
                                tern[0] += 1

                            s.wait_ge(sem["p2v"], 2)
                            for tc in range(NCH):
                                if tc >= 2:
                                    s.wait_ge(sem["cmv"], 8 * (tc - 1))
                                for n in range(32):
                                    i = 32 * tc + n
                                    s.wait_ge(sem["bcm"], i + 1)
                                    row = n // 2
                                    dstb = Bb if (n % 2) == 0 else Cb
                                    s.copy(bbv(dstb, tc)[:, row * TC:
                                                         (row + 1) * TC],
                                           b2v(ps_bc, i)).then_inc(sem["bce"], 1)
                                for dt in range(8):
                                    j = tc * 8 + dt
                                    s.wait_ge(sem["mmd"], j + 1)
                                    if j >= 2:
                                        s.wait_ge(sem["dlv"], j - 1)
                                    b2c = b2_s[:, dt:dt + 1]
                                    s.activation(b2v(spA, j), b2v(ps_dl, j),
                                                 AF.Abs, bias=b2c, scale=1.0)
                                    s.activation(b2v(spB, j), b2v(spA, j),
                                                 AF.Exp, bias=0.0, scale=-1.0)
                                    s.activation(b2v(spA, j), b2v(spB, j),
                                                 AF.Ln, bias=1.0, scale=1.0)
                                    s.activation(b2v(spB, j), b2v(ps_dl, j),
                                                 AF.Relu, bias=b2c,
                                                 scale=1.0).then_inc(sem["spd"], 1)
                                    for n in range(NST):
                                        idx = 16 * j + n
                                        s.wait_ge(sem["dlv"], j + 1)
                                        if idx >= 4:
                                            s.wait_ge(sem["scn"], idx - 3)
                                        col = dt * 16 + n
                                        s.activation(
                                            dA_s[:, (idx % 4) * TC:
                                                 (idx % 4) * TC + TC],
                                            b2v(dlt_s, j), AF.Exp, bias=0.0,
                                            scale=aneg_s[:, col:col + 1]
                                        ).then_inc(sem["dAc"], 1)
                                    do_tern()
                            while tern[0] < NPC:
                                do_tern()

                        def g_p3(g):
                            g.wait_ge(sem["p2s"], 2)
                            for j in range(NPC):
                                if j >= 2:
                                    g.wait_ge(sem["tnV"], j - 1)
                                g.dma_start(slabv(j),
                                            wpiece(w_up_t, w_dn_t, j)
                                            ).then_inc(sem["slab"][j % 2], 16)
                                if j >= 1:
                                    wj = j - 1
                                    g.wait_ge(sem["tnV"], wj + 1)
                                    if wj < 32:
                                        sj, q = divmod(wj, 4)
                                        wdst = d_wtu[sj * 128:(sj + 1) * 128,
                                                     q * 1024:(q + 1) * 1024]
                                    else:
                                        js = wj - 32
                                        wdst = d_wtd[js * 128:(js + 1) * 128, :]
                                    g.dma_start(wdst,
                                                wtv(wj)).then_inc(sem["wtst"][wj % 2], 16)
                            wj = NPC - 1
                            g.wait_ge(sem["tnV"], wj + 1)
                            g.dma_start(d_wtd[(wj - 32) * 128:
                                              (wj - 31) * 128, :],
                                        wtv(wj)).then_inc(sem["wtst"][wj % 2], 16)

                        def sy_p3(sy):
                            def ystore(pj):
                                ptc, pdt = divmod(pj, 8)
                                sy.wait_ge(sem["gat"], pj + 1)
                                chh = NCH // 2
                                dst = d_ystore[ptc // chh][
                                    pdt * 128:(pdt + 1) * 128,
                                    (ptc % chh) * TC:(ptc % chh + 1) * TC]
                                sy.dma_start(dst,
                                             b2v(yg_s, pj)).then_inc(
                                                 sem["ygst"][pj % 2], 16)
                            for j in range(NJ):
                                tc, dt = divmod(j, 8)
                                if j >= 2:
                                    sy.wait_ge(sem["gat"], j - 1)
                                sy.wait_ge(sem["xbst"][tc % 2],
                                           128 * (tc // 2 + 1))
                                sy.dma_start(b2v(u_s, j),
                                             d_xbld[dt * 128:(dt + 1) * 128,
                                                    tc * TC:(tc + 1) * TC]
                                             ).then_inc(sem["ug"][j % 2], 16)
                                sy.wait_ge(sem["gtst"][tc % 2],
                                           128 * (tc // 2 + 1))
                                sy.dma_start(b2v(gt_s, j),
                                             d_gate[dt * 128:(dt + 1) * 128,
                                                    tc * TC:(tc + 1) * TC]
                                             ).then_inc(sem["ug"][j % 2], 16)
                                if j >= 1:
                                    ystore(j - 1)
                            ystore(NJ - 1)

                        def v_p3(v):
                            tern = [0]

                            def do_tern():
                                j = tern[0]
                                if j >= NPC:
                                    return
                                v.wait_ge(sem["tnA"], NPC + j + 1)
                                if j >= 2:
                                    v.wait_ge(sem["wtst"][j % 2],
                                              16 * ((j - 2) // 2 + 1))
                                tcol = (tau_b[:, 0:1] if j < 32
                                        else tau_b[:, 1:2])
                                v.tensor_scalar(msk_s[:, :], tscr[:, :],
                                                tcol, None, ALU.is_gt)
                                v.tensor_tensor(wtv(j), msk_s[:, :],
                                                sgn_s[:, :],
                                                ALU.mult).then_inc(sem["tnV"], 1)
                                tern[0] += 1

                            for j in range(NJ):
                                tc, dt = divmod(j, 8)
                                v.wait_ge(sem["spd"], j + 1)
                                if j >= 2:
                                    v.wait_ge(sem["dAc"], 16 * (j - 2) + 16)
                                v.tensor_tensor(b2v(dlt_s, j), b2v(spB, j),
                                                b2v(spA, j),
                                                ALU.add).then_inc(sem["dlv"], 1)
                                v.wait_ge(sem["ug"][j % 2],
                                          32 * (j // 2 + 1))
                                v.tensor_tensor(b2v(du_s, j), b2v(dlt_s, j),
                                                b2v(u_s, j), ALU.mult)
                                v.wait_ge(sem["bce"], 32 * (tc + 1))
                                for n in range(NST):
                                    idx = 16 * j + n
                                    v.wait_ge(sem["dAc"], idx + 1)
                                    v.tensor_tensor(
                                        bta_s[:, :], b2v(du_s, j),
                                        bbv(Bb, tc)[:, n * TC:(n + 1) * TC],
                                        ALU.mult)
                                    v.tensor_tensor_scan(
                                        h_s[:, n * TC:(n + 1) * TC],
                                        dA_s[:, (idx % 4) * TC:
                                             (idx % 4) * TC + TC],
                                        bta_s[:, :],
                                        hcarry[:, dt * 16 + n:
                                               dt * 16 + n + 1],
                                        ALU.mult, ALU.add).then_inc(sem["scn"], 1)
                                v.wait_ge(sem["scn"], 16 * (j + 1))
                                v.tensor_copy(
                                    hcarry[:, dt * 16:(dt + 1) * 16],
                                    bass.AP(h_s, TC - 1,
                                            [[NST * TC, 128], [TC, NST]]))
                                v.tensor_tensor(m_s[:, :], h_s[:, :],
                                                bbv(Cb, tc), ALU.mult)
                                v.tensor_tensor(m_s[:, 0:8 * TC],
                                                m_s[:, 0:8 * TC],
                                                m_s[:, 8 * TC:16 * TC], ALU.add)
                                v.tensor_tensor(m_s[:, 0:4 * TC],
                                                m_s[:, 0:4 * TC],
                                                m_s[:, 4 * TC:8 * TC], ALU.add)
                                v.tensor_tensor(m_s[:, 0:2 * TC],
                                                m_s[:, 0:2 * TC],
                                                m_s[:, 2 * TC:4 * TC], ALU.add)
                                v.tensor_tensor(m_s[:, 0:TC], m_s[:, 0:TC],
                                                m_s[:, TC:2 * TC],
                                                ALU.add).then_inc(sem["cmv"], 1)
                                if j >= 2:
                                    v.wait_ge(sem["ygst"][j % 2],
                                              16 * ((j - 2) // 2 + 1))
                                v.tensor_scalar(y1_s[:, :], b2v(u_s, j),
                                                dsk_s[:, dt:dt + 1], None,
                                                ALU.mult)
                                v.tensor_tensor(y1_s[:, :], y1_s[:, :],
                                                m_s[:, 0:TC], ALU.add)
                                v.tensor_tensor(b2v(yg_s, j), y1_s[:, :],
                                                b2v(gt_s, j),
                                                ALU.mult).then_inc(sem["gat"], 1)
                                do_tern()

                            while tern[0] < NPC:
                                do_tern()

                        blk.tensor(t_p3)
                        blk.scalar(s_p3)
                        blk.gpsimd(g_p3)
                        blk.sync(sy_p3)
                        blk.vector(v_p3)

        with ExitStack() as e5:
            SB5 = lambda n, s, d: e5.enter_context(nc.sbuf_tensor(n, s, d))
            PS5 = lambda n, s, d: e5.enter_context(nc.psum_tensor(n, s, d))
            wob = SB5("wob", [128, 16 * DM], BF16)

            def g_p45(g):
                g.wait_ge(sem["ygst"][0], 16 * (NJ // 4))
                g.wait_ge(sem["ygst"][1], 16 * (NJ // 4))
                g.collective_compute(
                    "AllGather", ALU.bypass, replica_groups=RG,
                    ins=[d_ystore[0][:, :].opt()],
                    outs=[bass.AP(d_agy, 0,
                                  [[TOWN, 2 * DH], [1, TOWN]]).opt()],
                ).then_inc(sem["a2a"], 1)
                g.wait_ge(sem["ygst"][0], 16 * (NJ // 2))
                g.wait_ge(sem["ygst"][1], 16 * (NJ // 2))
                if DEBUG_OUT:
                    g.dma_start(d_dbgy.ap(), d_ystore.ap()).then_inc(
                        sem["lc"], 16)
                    g.wait_ge(sem["lc"], LC_N + 16)
                g.collective_compute(
                    "AllGather", ALU.bypass, replica_groups=RG,
                    ins=[d_ystore[1][:, :].opt()],
                    outs=[bass.AP(d_agy, 2 * DH * TOWN,
                                  [[TOWN, 2 * DH], [1, TOWN]]).opt()],
                ).then_inc(sem["a2a"], 1)
                g.wait_ge(sem["wtst"][0], 16 * (NPC // 2))
                g.wait_ge(sem["wtst"][1], 16 * (NPC // 2))
                for k in range(16):
                    g.dma_start(wob[:, k * DM:(k + 1) * DM],
                                w_out[k * 128:(k + 1) * 128, :]
                                ).then_inc(sem["wol"], 16)
                with g.register("rsec0") as r0, g.register("rsec1") as r1:
                    g.reg_load(r0, roff[0:1, 0:1])
                    g.reg_load(r1, roff[0:1, 1:2])
                    sec0 = g.snap(r0)
                    sec1 = g.snap(r1)
                    g.wait_ge(sem["a2a"], 2)
                    for fc in range(NF5):
                        if fc >= 2:
                            g.wait_ge(sem["opm"], 8 * (fc - 1))
                        for half, sec in ((0, sec0), (1, sec1)):
                            srcap = bass.AP(
                                d_agy,
                                sec * (DH * TOWN) + fc * TC,
                                [[TOWN, 128], [128 * TOWN, 8], [1, TC]])
                            dstap = bass.AP(
                                ytl,
                                ((fc % 2) * 16 + half * 8) * TC,
                                [[32 * TC, 128], [TC, 8], [1, TC]])
                            g.dma_start(dstap,
                                        srcap).then_inc(sem["ytl"][fc % 2], 16)

            ytl = SB5("ytl", [128, 32 * TC], BF16)
            xt5 = SB5("xt5", [128, 8 * DM], F32)
            xres = SB5("xres", [128, 2 * TC], F32)
            x2c = SB5("x2c", [128, 16 * TC], F32)
            sqt = SB5("sqt", [128, 2 * TC], F32)
            vm = SB5("vm", [1, TC], F32)
            vv = SB5("vv", [1, TC], F32)
            vq = SB5("vq", [1, TC], F32)
            vt1 = SB5("vt1", [1, TC], F32)
            vt2 = SB5("vt2", [1, TC], F32)
            vrstd = SB5("vrstd", [1, TC], F32)
            vrms2 = SB5("vrms2", [1, TC], F32)
            sv_s = SB5("sv_s", [1, TC], F32)
            sv_ms = SB5("sv_ms", [1, TC], F32)
            sb5 = SB5("sb5", [128, 2 * TC], F32)
            xnc = SB5("xnc", [128, 2 * TC], BF16)
            xtm5 = SB5("xtm5", [128, TC], F32)
            ps_o = PS5("ps_o", [128, 2 * TC], F32)
            ps_t5 = PS5("ps_t5", [128, 2 * TC], F32)
            ps_s15 = PS5("ps_s15", [1, TC], F32)
            ps_s25 = PS5("ps_s25", [1, TC], F32)
            ps_b5 = PS5("ps_b5", [128, 2 * TC], F32)

            def ytlv(fc, k):
                o = ((fc % 2) * 16 + k) * TC
                return ytl[:, o:o + TC]

            def xt5v(fc, i):
                o = ((fc % 2) * 4 + i) * DM
                return xt5[:, o:o + DM]

            def x2cv(fc, f):
                o = ((fc % 2) * 8 + f) * TC
                return x2c[:, o:o + TC]

            def i2v(t_, i):
                return t_[:, (i % 2) * TC:(i % 2) * TC + TC]

            blk.gpsimd(g_p45)

            def sy_p5(sy):
                sy.wait_ge(sem["wtst"][0], 16 * (NPC // 2))
                sy.wait_ge(sem["wtst"][1], 16 * (NPC // 2))
                sy.wait_ge(sem["ygst"][0], 16 * (NJ // 2))
                sy.wait_ge(sem["ygst"][1], 16 * (NJ // 2))

                def stores(fc):
                    for f in range(8):
                        sy.wait_ge(sem["x2e"], 8 * fc + f + 1)
                        sy.dma_start(
                            d_x2[f * 128:(f + 1) * 128,
                                 fc * TC:(fc + 1) * TC],
                            x2cv(fc, f)).then_inc(sem["x2st"][fc % 2], 16)
                        sy.wait_ge(sem["xne"], 8 * fc + f + 1)
                        sy.dma_start(
                            d_xn[f * 128:(f + 1) * 128,
                                 fc * TC:(fc + 1) * TC],
                            i2v(xnc, f)).then_inc(sem["xnst"][f % 2], 16)
                for fc in range(NF5):
                    if fc >= 2:
                        sy.wait_ge(sem["tp5"], 8 * (fc - 1))
                    for i in range(4):
                        row0 = fc * TC + i * 128
                        sy.dma_start(xt5v(fc, i),
                                     x_own[row0:row0 + 128, :]
                                     ).then_inc(sem["x5"][fc % 2], 16)
                    if fc >= 1:
                        stores(fc - 1)
                stores(NF5 - 1)

            def t_p5(t):
                t.wait_ge(sem["wol"], 16 * 16)
                for fc in range(NF5):
                    t.wait_ge(sem["ytl"][fc % 2], 32 * (fc // 2 + 1))
                    t.wait_ge(sem["x5"][fc % 2], 64 * (fc // 2 + 1))
                    for f in range(8):
                        idx = 8 * fc + f
                        if idx >= 2:
                            t.wait_ge(sem["tpe5"], idx - 1)
                        for i in range(4):
                            tr = t.transpose(
                                i2v(ps_t5, f)[:, i * 128:(i + 1) * 128],
                                xt5v(fc, i)[:, f * 128:(f + 1) * 128],
                                identf[:, :])
                        tr.then_inc(sem["tp5"], 1)
                        if idx >= 2:
                            t.wait_ge(sem["x2e"], idx - 1)
                        for k in range(16):
                            mm = t.matmul(i2v(ps_o, f),
                                          wob[:, k * DM + f * 128:
                                              k * DM + (f + 1) * 128],
                                          ytlv(fc, k),
                                          start=(k == 0), stop=(k == 15))
                        mm.then_inc(sem["opm"], 1)
                    if fc >= 1:
                        t.wait_ge(sem["nb5"], fc)
                    for k in range(8):
                        idx = 8 * fc + k
                        t.wait_ge(sem["x2e"], idx + 1)
                        t.wait_ge(sem["sq5"], idx + 1)
                        t.matmul(ps_s15[:, :], ones_f[:, 0:1], x2cv(fc, k),
                                 start=(k == 0), stop=(k == 7))
                        t.matmul(ps_s25[:, :], ones_f[:, 0:1], i2v(sqt, k),
                                 start=(k == 0),
                                 stop=(k == 7)).then_inc(sem["st5"], 1)
                    t.wait_ge(sem["nb5"], fc + 1)
                    if fc >= 1:
                        t.wait_ge(sem["sse"], 2 * fc)
                    t.matmul(ps_b5[:, 0:TC], ones_f[0:1, :], sv_s[0:1, :],
                             start=True, stop=True)
                    t.matmul(ps_b5[:, TC:2 * TC], ones_f[0:1, :],
                             sv_ms[0:1, :], start=True,
                             stop=True).then_inc(sem["opb"], 1)

            def s_p5(s):
                for fc in range(NF5):
                    for dt in range(8):
                        idx = 8 * fc + dt
                        s.wait_ge(sem["tp5"], idx + 1)
                        if idx >= 2:
                            s.wait_ge(sem["x2e"], idx - 1)
                        s.copy(i2v(xres, dt),
                               i2v(ps_t5, dt)).then_inc(sem["tpe5"], 1)
                    for k in range(8):
                        idx = 8 * fc + k
                        s.wait_ge(sem["x2e"], idx + 1)
                        if idx >= 2:
                            s.wait_ge(sem["st5"], idx - 1)
                        s.activation(i2v(sqt, k), x2cv(fc, k), AF.Square,
                                     bias=0.0, scale=1.0).then_inc(sem["sq5"], 1)
                    s.wait_ge(sem["vst5"], fc + 1)
                    s.activation(vt1[0:1, :], vv[0:1, :], AF.Ln, bias=1e-5,
                                 scale=1.0)
                    s.activation(vrstd[0:1, :], vt1[0:1, :], AF.Exp, bias=0.0,
                                 scale=-0.5)
                    s.activation(vt1[0:1, :], vq[0:1, :], AF.Ln, bias=1e-35,
                                 scale=1.0)
                    s.activation(vrms2[0:1, :], vt1[0:1, :], AF.Exp, bias=0.0,
                                 scale=0.5).then_inc(sem["ast5"], 1)
                    s.wait_ge(sem["opb"], fc + 1)
                    if fc >= 1:
                        s.wait_ge(sem["xne"], 8 * fc)
                    s.copy(sb5[:, 0:TC], ps_b5[:, 0:TC])
                    s.copy(sb5[:, TC:2 * TC],
                           ps_b5[:, TC:2 * TC]).then_inc(sem["sse"], 2)

            def v_p5(v):
                for fc in range(NF5):
                    for f in range(8):
                        idx = 8 * fc + f
                        v.wait_ge(sem["tpe5"], idx + 1)
                        v.wait_ge(sem["opm"], idx + 1)
                        if fc >= 2:
                            v.wait_ge(sem["x2st"][fc % 2],
                                      128 * ((fc - 2) // 2 + 1))
                            v.wait_ge(sem["st5"], 8 * (fc - 1))
                            v.wait_ge(sem["xne"], 8 * (fc - 1))
                        v.tensor_tensor(x2cv(fc, f), i2v(xres, f),
                                        i2v(ps_o, f),
                                        ALU.add).then_inc(sem["x2e"], 1)
                    v.wait_ge(sem["st5"], 8 * fc + 8)
                    v.tensor_copy(vm[0:1, :], ps_s15[:, :])
                    v.tensor_copy(vv[0:1, :], ps_s25[:, :])
                    v.tensor_single_scalar(vm[0:1, :], vm[0:1, :],
                                           1.0 / DM, ALU.mult)
                    v.tensor_single_scalar(vv[0:1, :], vv[0:1, :],
                                           1.0 / DM, ALU.mult)
                    v.tensor_tensor(vt1[0:1, :], vm[0:1, :], vm[0:1, :],
                                    ALU.mult)
                    v.tensor_tensor(vv[0:1, :], vv[0:1, :], vt1[0:1, :],
                                    ALU.subtract)
                    v.tensor_single_scalar(vt1[0:1, :], vv[0:1, :], 1e-5,
                                           ALU.add)
                    v.reciprocal(vt2[0:1, :], vt1[0:1, :])
                    v.tensor_tensor(vq[0:1, :], vv[0:1, :], vt2[0:1, :],
                                    ALU.mult).then_inc(sem["vst5"], 1)
                    v.wait_ge(sem["ast5"], fc + 1)
                    v.tensor_single_scalar(vt1[0:1, :], vrms2[0:1, :], 1e-8,
                                           ALU.add)
                    v.reciprocal(vt2[0:1, :], vt1[0:1, :])
                    v.tensor_tensor(sv_s[0:1, :], vrstd[0:1, :], vt2[0:1, :],
                                    ALU.mult)
                    v.tensor_tensor(sv_ms[0:1, :], vm[0:1, :], sv_s[0:1, :],
                                    ALU.mult).then_inc(sem["nb5"], 1)
                    v.wait_ge(sem["sse"], 2 * fc + 2)
                    for k in range(8):
                        idx = 8 * fc + k
                        if idx >= 2:
                            v.wait_ge(sem["xnst"][k % 2],
                                      16 * (4 * fc + k // 2))
                        v.tensor_tensor(xtm5[:, :], x2cv(fc, k),
                                        sb5[:, 0:TC], ALU.mult)
                        v.tensor_tensor(i2v(xnc, k), xtm5[:, :],
                                        sb5[:, TC:2 * TC],
                                        ALU.subtract).then_inc(sem["xne"], 1)

            blk.sync(sy_p5)
            blk.tensor(t_p5)
            blk.scalar(s_p5)
            blk.vector(v_p5)

        with ExitStack() as e6:
            SB6 = lambda n, s, d: e6.enter_context(nc.sbuf_tensor(n, s, d))
            PS6 = lambda n, s, d: e6.enter_context(nc.psum_tensor(n, s, d))
            wtu_r = SB6("wtu_r", [128, 8 * DFF], BF16)
            wtd_r = SB6("wtd_r", [128, 32 * DM], BF16)
            xn6 = SB6("xn6", [128, 16 * TC2], BF16)
            x26 = SB6("x26", [128, 16 * TC2], F32)
            h6 = SB6("h6", [128, 32 * TC2], BF16)
            sqh = SB6("sqh", [128, 2 * TC2], BF16)
            w6a = SB6("w6a", [1, TC2], F32)
            w6b = SB6("w6b", [1, TC2], F32)
            w6c = SB6("w6c", [1, TC2], F32)
            w6d = SB6("w6d", [1, TC2], F32)
            w6e = SB6("w6e", [1, TC2], F32)
            w6f = SB6("w6f", [1, TC2], F32)
            sv_s2 = SB6("sv_s2", [1, TC2], F32)
            s2b6 = SB6("s2b6", [128, TC2], F32)
            x36 = SB6("x36", [128, 8 * TC2], F32)
            xtm6 = SB6("xtm6", [128, TC2], F32)
            ostg = SB6("ostg", [128, 2 * DM], F32)
            ps_u = PS6("ps_u", [128, 4 * TC2], F32)
            ps_d = PS6("ps_d", [128, 4 * TC2], F32)
            ps_h2 = PS6("ps_h2", [1, 2 * TC2], F32)
            ps_b6 = PS6("ps_b6", [128, 2 * TC2], F32)
            ps_to = PS6("ps_to", [128, 2 * 512], F32)

            def xn6v(fc, k):
                o = ((fc % 2) * 8 + k) * TC2
                return xn6[:, o:o + TC2]

            def x26v(fc, k):
                o = ((fc % 2) * 8 + k) * TC2
                return x26[:, o:o + TC2]

            def h6v(f):
                return h6[:, f * TC2:(f + 1) * TC2]

            def x36v(f):
                return x36[:, f * TC2:(f + 1) * TC2]

            def j2v(t_, i):
                return t_[:, (i % 2) * TC2:(i % 2) * TC2 + TC2]

            def jpv(ps, i):
                return ps[:, (i % 2) * 2 * TC2:(i % 2) * 2 * TC2 + TC2]

            def tov(g2):
                return ps_to[:, g2 * 512:(g2 + 1) * 512]

            def ostv(i):
                return ostg[:, i * DM:(i + 1) * DM]

            def sy_p6(sy):
                sy.wait_ge(sem["x2st"][0], 128 * (NF5 // 2))
                sy.wait_ge(sem["x2st"][1], 128 * (NF5 // 2))
                sy.wait_ge(sem["xnst"][0], 16 * 4 * NF5)
                sy.wait_ge(sem["xnst"][1], 16 * 4 * NF5)
                sy.wait_ge(sem["wtst"][0], 16 * (NPC // 2))
                sy.wait_ge(sem["wtst"][1], 16 * (NPC // 2))
                for j in range(8):
                    sy.dma_start(wtu_r[:, j * DFF:(j + 1) * DFF],
                                 d_wtu[j * 128:(j + 1) * 128, :]
                                 ).then_inc(sem["wtl"], 16)
                for j in range(32):
                    sy.dma_start(wtd_r[:, j * DM:(j + 1) * DM],
                                 d_wtd[j * 128:(j + 1) * 128, :]
                                 ).then_inc(sem["wtl"], 16)

                def ostores(fc):
                    for i in range(2):
                        sy.wait_ge(sem["tpe6"], 4 * fc + 2 * (i + 1))
                        sy.dma_start(
                            out[fc * TC2 + i * 128:
                                fc * TC2 + (i + 1) * 128, :],
                            ostv(i)).then_inc(sem["ost"][i], 16)
                for fc in range(NF6):
                    if fc >= 2:
                        sy.wait_ge(sem["upm"], 32 * (fc - 1))
                        sy.wait_ge(sem["x3e"], 8 * (fc - 1))
                    for k in range(8):
                        sy.dma_start(xn6v(fc, k),
                                     d_xn[k * 128:(k + 1) * 128,
                                          fc * TC2:(fc + 1) * TC2]
                                     ).then_inc(sem["xnl"][fc % 2], 16)
                        sy.dma_start(x26v(fc, k),
                                     d_x2[k * 128:(k + 1) * 128,
                                          fc * TC2:(fc + 1) * TC2]
                                     ).then_inc(sem["xnl"][fc % 2], 16)
                    if fc >= 1:
                        ostores(fc - 1)
                ostores(NF6 - 1)
                sy.wait_ge(sem["ost"][0], 16 * NF6)
                sy.wait_ge(sem["ost"][1], 16 * NF6)

            def t_p6(t):
                t.wait_ge(sem["wtl"], 16 * 40)
                for fc in range(NF6):
                    t.wait_ge(sem["xnl"][fc % 2], 256 * (fc // 2 + 1))
                    if fc >= 1:
                        t.wait_ge(sem["dnm"], 8 * fc)
                    for f in range(32):
                        idx = 32 * fc + f
                        if idx >= 2:
                            t.wait_ge(sem["gel"], idx - 1)
                        for k in range(8):
                            mm = t.matmul(jpv(ps_u, f),
                                          wtu_r[:, k * DFF + f * 128:
                                                k * DFF + (f + 1) * 128],
                                          xn6v(fc, k),
                                          start=(k == 0), stop=(k == 7))
                        mm.then_inc(sem["upm"], 1)
                        if f >= 2:
                            g2 = f - 2
                            if g2 == 0:
                                t.wait_ge(sem["s2v"], fc)
                            t.wait_ge(sem["hsq"], 32 * fc + g2 + 1)
                            t.matmul(ps_h2[0:1, 0:TC2], ones_b[:, 0:1],
                                     j2v(sqh, g2),
                                     start=(g2 == 0),
                                     stop=False).then_inc(sem["h2m"], 1)
                    for g2 in (30, 31):
                        t.wait_ge(sem["hsq"], 32 * fc + g2 + 1)
                        t.matmul(ps_h2[0:1, 0:TC2], ones_b[:, 0:1], j2v(sqh, g2),
                                 start=False,
                                 stop=(g2 == 31)).then_inc(sem["h2m"], 1)
                    t.wait_ge(sem["s2v"], fc + 1)
                    if fc >= 1:
                        t.wait_ge(sem["sbe"], fc)
                    t.matmul(ps_b6[:, 0:TC2], ones_f[0:1, :], sv_s2[0:1, :],
                             start=True, stop=True).then_inc(sem["opb"], 1)
                    t.wait_ge(sem["gel"], 32 * (fc + 1))
                    for fo in range(8):
                        idx = 8 * fc + fo
                        if idx >= 2:
                            t.wait_ge(sem["x3e"], idx - 1)
                        for k in range(32):
                            mm = t.matmul(jpv(ps_d, fo),
                                          wtd_r[:, k * DM + fo * 128:
                                                k * DM + (fo + 1) * 128],
                                          h6v(k),
                                          start=(k == 0), stop=(k == 31))
                        mm.then_inc(sem["dnm"], 1)
                    for i in range(2):
                        for g2 in range(2):
                            tpidx = 4 * fc + 2 * i + g2
                            if tpidx >= 2:
                                t.wait_ge(sem["tpe6"], tpidx - 1)
                            for q in range(4):
                                dt = g2 * 4 + q
                                t.wait_ge(sem["x3e"], 8 * fc + dt + 1)
                                tr = t.transpose(
                                    tov(g2)[:, q * 128:(q + 1) * 128],
                                    x36v(dt)[:, i * 128:(i + 1) * 128],
                                    identf[:, :])
                            tr.then_inc(sem["tp6"], 1)

            def s_p6(s):
                for fc in range(NF6):
                    for f in range(32):
                        idx = 32 * fc + f
                        if f == 0 and fc >= 1:
                            s.wait_ge(sem["dnm"], 8 * fc)
                        s.wait_ge(sem["upm"], idx + 1)
                        s.activation(h6v(f), jpv(ps_u, f), AF.Gelu,
                                     bias=0.0, scale=1.0).then_inc(sem["gel"], 1)
                        if f >= 2:
                            s.wait_ge(sem["h2m"], 32 * fc + f - 1)
                        s.activation(j2v(sqh, f), h6v(f), AF.Square,
                                     bias=0.0, scale=1.0).then_inc(sem["hsq"], 1)
                    s.wait_ge(sem["lnv"], 4 * NCH + fc + 1)
                    s.activation(w6d[0:1, :], w6a[0:1, :], AF.Ln, bias=1e-35,
                                 scale=1.0)
                    s.activation(w6c[0:1, :], w6d[0:1, :], AF.Exp, bias=0.0,
                                 scale=0.5).then_inc(sem["lns"], 1)
                    s.wait_ge(sem["opb"], NF5 + fc + 1)
                    if fc >= 1:
                        s.wait_ge(sem["x3e"], 8 * fc)
                    s.copy(s2b6[:, :], ps_b6[:, 0:TC2]).then_inc(sem["sbe"], 1)
                    for i in range(2):
                        for g2 in range(2):
                            tpidx = 4 * fc + 2 * i + g2
                            s.wait_ge(sem["tp6"], tpidx + 1)
                            if fc >= 1:
                                s.wait_ge(sem["ost"][i], 16 * fc)
                            s.copy(ostv(i)[:, g2 * 512:(g2 + 1) * 512],
                                   tov(g2)).then_inc(sem["tpe6"], 1)

            def v_p6(v):
                for fc in range(NF6):
                    v.wait_ge(sem["h2m"], 32 * (fc + 1))
                    v.tensor_copy(w6a[0:1, :],
                                  ps_h2[0:1, 0:TC2]).then_inc(sem["lnv"], 1)
                    v.wait_ge(sem["lns"], 4 * NCH + fc + 1)
                    v.tensor_single_scalar(w6b[0:1, :], w6c[0:1, :],
                                           1.0 / 64.0, ALU.mult)
                    v.tensor_single_scalar(w6b[0:1, :], w6b[0:1, :], 1e-8,
                                           ALU.add)
                    v.reciprocal(sv_s2[0:1, :],
                                 w6b[0:1, :]).then_inc(sem["s2v"], 1)
                    v.wait_ge(sem["sbe"], fc + 1)
                    for fo in range(8):
                        idx = 8 * fc + fo
                        v.wait_ge(sem["dnm"], idx + 1)
                        if fc >= 1 and fo == 0:
                            v.wait_ge(sem["tp6"], 4 * fc)
                        v.tensor_tensor(xtm6[:, :], jpv(ps_d, fo),
                                        s2b6[:, :], ALU.mult)
                        v.tensor_tensor(x36v(fo), xtm6[:, :], x26v(fc, fo),
                                        ALU.add).then_inc(sem["x3e"], 1)

            blk.sync(sy_p6)
            blk.tensor(t_p6)
            blk.scalar(s_p6)
            blk.vector(v_p6)

    return nc


_CACHE = {}


def kernel(L0_attn_hidden_states, L0_attn_in_proj_w, L0_attn_conv_w,
           L0_attn_conv_b, L0_attn_x_proj_w, L0_attn_dt_proj_w,
           L0_attn_dt_proj_b, L0_attn_A_log, L0_attn_D_skip,
           L0_attn_out_proj_w, L0_ffn_w_up, L0_ffn_w_down):
    h = np.asarray(L0_attn_hidden_states, dtype=np.float32)
    ipw = np.asarray(L0_attn_in_proj_w, dtype=np.float32)
    cw = np.asarray(L0_attn_conv_w, dtype=np.float32)
    cb = np.asarray(L0_attn_conv_b, dtype=np.float32)
    xpw = np.asarray(L0_attn_x_proj_w, dtype=np.float32)
    dtw = np.asarray(L0_attn_dt_proj_w, dtype=np.float32)
    dtb = np.asarray(L0_attn_dt_proj_b, dtype=np.float32)
    alog = np.asarray(L0_attn_A_log, dtype=np.float32)
    dsk = np.asarray(L0_attn_D_skip, dtype=np.float32)
    opw = np.asarray(L0_attn_out_proj_w, dtype=np.float32)
    wup = np.asarray(L0_ffn_w_up, dtype=np.float32)
    wdn = np.asarray(L0_ffn_w_down, dtype=np.float32)

    B = h.shape[0]
    eyem = np.eye(128, dtype=np.float32)
    ones = np.ones((128, 128), dtype=np.float32)
    selm = np.zeros((32, 32 * 128), dtype=np.float32)
    for n in range(32):
        selm[n, n * 128:(n + 1) * 128] = 1.0
    wupt = np.ascontiguousarray(wup.T)
    wdnt = np.ascontiguousarray(wdn.T)

    if "nc" not in _CACHE:
        _CACHE["nc"] = build()
    ncp = _CACHE["nc"]

    def dslice(vec, r):
        return np.ascontiguousarray(
            vec[r * DH:(r + 1) * DH].reshape(8, 128).T)

    in_maps = []
    for c in range(8):
        b, rr = c >> 1, c & 1
        in_maps.append({
            "x_seq": np.ascontiguousarray(h[b]),
            "x_own": np.ascontiguousarray(h[b, rr * TOWN:(rr + 1) * TOWN]),
            "w_in": np.ascontiguousarray(np.concatenate(
                [ipw[:, rr * DH:(rr + 1) * DH],
                 ipw[:, DI + rr * DH:DI + (rr + 1) * DH]], axis=1)),
            "w_xp": np.ascontiguousarray(xpw[rr * DH:(rr + 1) * DH, :]),
            "w_dt": np.ascontiguousarray(dtw[:, rr * DH:(rr + 1) * DH]),
            "b_dt": dslice(dtb, rr),
            "a_log": np.ascontiguousarray(
                alog[rr * DH:(rr + 1) * DH].reshape(8, 128, NST)
                .transpose(1, 0, 2).reshape(128, 128)),
            "conv_w4": np.ascontiguousarray(
                cw[rr * DH:(rr + 1) * DH].reshape(8, 128, 4)
                .transpose(1, 0, 2).reshape(128, 32)),
            "conv_b8": dslice(cb, rr),
            "dsk8": dslice(dsk, rr),
            "w_out": opw,
            "w_up_t": wupt,
            "w_dn_t": wdnt,
            "eye": eyem,
            "onesm": ones,
            "selbc": selm,
            "roff": np.array([[2 * rr, 2 * rr + 1]], dtype=np.int32),
            "cvals": np.tile(np.array(
                [[1.0, 1e-5, 1e-35, -1.0, 0.5, -0.5, 0.0, 0.0]],
                dtype=np.float32), (128, 1)),
        })

    res = run_bass_kernel_spmd(ncp, in_maps, core_ids=list(range(8)))
    _CACHE["res"] = res

    out_full = np.empty((B, L, DM), dtype=np.float32)
    for c in range(8):
        b, rr = c >> 1, c & 1
        out_full[b, rr * TOWN:(rr + 1) * TOWN, :] = res.results[c]["out"]
    return out_full


# version 13 (final)
# speedup vs baseline: 1.2187x; 1.0028x over previous
import os
import sys
sys.path.insert(0, "/opt/trn_rl_repo")
os.environ.setdefault("TRN_ALLOW_ICI_ALL_TO_ALL", "1")
from contextlib import ExitStack

import numpy as np
import concourse.bass as bass
import concourse.mybir as mybir
from concourse.bass_utils import run_bass_kernel_spmd

F32 = mybir.dt.float32
BF16 = mybir.dt.bfloat16
I32 = mybir.dt.int32
AF = mybir.ActivationFunctionType
ALU = mybir.AluOpType
AXX = mybir.AxisListType.X

L = 4096
DM = 1024
DI = 2048
DH = 1024
NST = 16
DTR = 64
DFF = 4096
TOWN = 2048
TC = 512
NCH = L // TC
NJ = NCH * 8
TC2 = 256
NF5 = TOWN // TC
NF6 = TOWN // TC2
NPC = 64
MAGIC = 0x5F3759DF
RG = [[0, 1], [2, 3], [4, 5], [6, 7]]

DEBUG_OUT = False


def nrsqrt(v, dst, src, ti, t1, t2, iters=3):
    v.tensor_single_scalar(ti, src.bitcast(I32), 1, ALU.logical_shift_right)
    v.tensor_single_scalar(ti, ti, MAGIC, ALU.subtract)
    v.tensor_single_scalar(dst.bitcast(I32), ti, -1, ALU.mult)
    for _ in range(iters):
        v.tensor_tensor(t1, src, dst, ALU.mult)
        v.tensor_tensor(t1, t1, dst, ALU.mult)
        v.tensor_scalar(t2, t1, -0.5, 1.5, ALU.mult, ALU.add)
        v.tensor_tensor(dst, dst, t2, ALU.mult)


def wpiece(w_up_t, w_dn_t, j):
    if j < 32:
        sj, q = divmod(j, 4)
        return w_up_t[sj * 128:(sj + 1) * 128, q * 1024:(q + 1) * 1024]
    js = j - 32
    return w_dn_t[js * 128:(js + 1) * 128, :]


def build():
    nc = bass.Bass("TRN2", target_bir_lowering=False,
                   detect_race_conditions=False)

    x_seq = nc.dram_tensor("x_seq", [L, DM], F32, kind="ExternalInput")
    x_own = nc.dram_tensor("x_own", [TOWN, DM], F32, kind="ExternalInput")
    w_in = nc.dram_tensor("w_in", [DM, 2 * DH], F32, kind="ExternalInput")
    w_xp = nc.dram_tensor("w_xp", [DH, 96], F32, kind="ExternalInput")
    w_dt = nc.dram_tensor("w_dt", [DTR, DH], F32, kind="ExternalInput")
    b_dt = nc.dram_tensor("b_dt", [128, 8], F32, kind="ExternalInput")
    a_log = nc.dram_tensor("a_log", [128, 128], F32, kind="ExternalInput")
    conv_w4 = nc.dram_tensor("conv_w4", [128, 32], F32, kind="ExternalInput")
    conv_b8 = nc.dram_tensor("conv_b8", [128, 8], F32, kind="ExternalInput")
    dsk8 = nc.dram_tensor("dsk8", [128, 8], F32, kind="ExternalInput")
    w_out = nc.dram_tensor("w_out", [DI, DM], F32, kind="ExternalInput")
    w_up_t = nc.dram_tensor("w_up_t", [DM, DFF], F32, kind="ExternalInput")
    w_dn_t = nc.dram_tensor("w_dn_t", [DFF, DM], F32, kind="ExternalInput")
    eye = nc.dram_tensor("eye", [128, 128], F32, kind="ExternalInput")
    onesm = nc.dram_tensor("onesm", [128, 128], F32, kind="ExternalInput")
    selbc = nc.dram_tensor("selbc", [32, 32 * 128], F32, kind="ExternalInput")
    roff = nc.dram_tensor("roff", [1, 2], I32, kind="ExternalInput")
    cvals = nc.dram_tensor("cvals", [128, 8], F32, kind="ExternalInput")

    out = nc.dram_tensor("out", [TOWN, DM], F32, kind="ExternalOutput")

    dk = "ExternalOutput" if DEBUG_OUT else "Internal"
    d_xbld = nc.dram_tensor("d_xbld", [DH, L], BF16, kind=dk)
    d_gate = nc.dram_tensor("d_gate", [DH, L], BF16, kind=dk)
    d_x2 = nc.dram_tensor("d_x2", [DM, TOWN], F32, kind=dk)
    d_xn = nc.dram_tensor("d_xn", [DM, TOWN], BF16, kind=dk)
    d_wtu = nc.dram_tensor("d_wtu", [DM, DFF], BF16, kind=dk)
    d_wtd = nc.dram_tensor("d_wtd", [DFF, DM], BF16, kind=dk)
    d_arin = nc.dram_tensor("d_arin", [96, L], F32)
    d_arout = nc.dram_tensor("d_arout", [96, L], F32)
    d_ystore = nc.dram_tensor("d_ystore", [2, DH, TOWN], BF16)
    d_agy = nc.dram_tensor("d_agy", [4, DH, TOWN], BF16)
    if DEBUG_OUT:
        d_dbgy = nc.dram_tensor("d_dbgy", [2, DH, TOWN], BF16,
                                kind="ExternalOutput")
        d_dbga = nc.dram_tensor("d_dbga", [4, DH, TOWN], BF16,
                                kind="ExternalOutput")

    with ExitStack() as eg:
        SBg = lambda n, s, d: eg.enter_context(nc.sbuf_tensor(n, s, d))
        blk = eg.enter_context(nc.Block())
        sem = {}
        for n in ["lc", "sq1", "vln", "tp", "xlf", "ipm", "ipe",
                  "cv", "sil", "xpm", "xpe",
                  "tnA", "tnV", "arv", "arl", "p2v", "p2s", "p2t", "lnv", "lns",
                  "bcm", "bce", "mmd", "spd", "dlv", "dAc", "scn", "cmv",
                  "gat", "a2a", "wol",
                  "opm", "tp5", "tpe5", "x2e", "sq5", "st5",
                  "vst5", "ast5", "nb5", "opb", "sse", "xne",
                  "wtl", "upm", "gel", "hsq", "h2m", "s2v", "sbe",
                  "dnm", "x3e", "tp6", "tpe6"]:
            sem[n] = eg.enter_context(nc.semaphore("c_" + n))
        for n in ["xt", "slab", "ug", "ytl", "x5", "xnl", "xbst", "gtst",
                  "xds", "ygst", "x2st", "xnst", "ost", "wtst"]:
            sem[n] = [eg.enter_context(nc.semaphore("c_" + n + str(i)))
                      for i in range(2)]

        identf = SBg("identf", [128, 128], F32)
        identb = SBg("identb", [128, 128], BF16)
        ones_f = SBg("ones_f", [128, 128], F32)
        ones_b = SBg("ones_b", [128, 128], BF16)
        bdt_s = SBg("bdt_s", [128, 8], F32)
        b2_s = SBg("b2_s", [128, 8], F32)
        alog_s = SBg("alog_s", [128, 128], F32)
        aneg_s = SBg("aneg_s", [128, 128], F32)
        cw_s = SBg("cw_s", [128, 32], F32)
        cb_s = SBg("cb_s", [128, 8], F32)
        dsk_s = SBg("dsk_s", [128, 8], F32)
        hcarry = SBg("hcarry", [128, 128], F32)
        asums = SBg("asums", [128, NPC], F32)
        tau_b = SBg("tau_b", [128, 2], F32)
        tauv = SBg("tauv", [128, 8], F32)
        cval_s = SBg("cval_s", [128, 8], F32)
        for _i, _v in enumerate([1.0, 1e-5, 1e-35, -1.0, 0.5, -0.5]):
            nc.const_aps.register(F32, _v, cval_s[:, _i:_i + 1])

        with ExitStack() as e13:
            SB13 = lambda n, s, d: e13.enter_context(nc.sbuf_tensor(n, s, d))
            slab = SB13("slab", [128, 2048], F32)
            tscr = SB13("tscr", [128, 1024], F32)
            sgn_s = SB13("sgn_s", [128, 1024], BF16)
            msk_s = SB13("msk_s", [128, 1024], BF16)
            wt_s = SB13("wt_s", [128, 2048], BF16)
            w_dt_b = SB13("w_dt_b", [64, 1024], BF16)
            sel_b = SB13("sel_b", [32, 32 * 128], BF16)
            ps_tau = e13.enter_context(nc.psum_tensor("ps_tau", [128, 512], F32))

            def slabv(j):
                return slab[:, (j % 2) * 1024:(j % 2) * 1024 + 1024]

            def wtv(j):
                return wt_s[:, (j % 2) * 1024:(j % 2) * 1024 + 1024]

            if True:
                e1 = ExitStack()
                SB1 = lambda n, s, d: e1.enter_context(nc.sbuf_tensor(n, s, d))
                PS1 = lambda n, s, d: e1.enter_context(nc.psum_tensor(n, s, d))
                w_in_b = SB1("w_in_b", [128, 8 * 2048], BF16)
                w_xp_b = SB1("w_xp_b", [128, 8 * 96], BF16)
                xt_s = SB1("xt_s", [128, 8 * DM], F32)
                sq_s = SB1("sq_s", [128, DM], F32)
                s1_s = SB1("s1_s", [128, 8], F32)
                ssq_s = SB1("ssq_s", [128, 8], F32)
                mva = SB1("mva", [128, 16], F32)
                xln_tm = SB1("xln_tm", [128, 4 * DM], BF16)
                xln_fm = SB1("xln_fm", [128, 8 * TC], BF16)
                xc_buf = SB1("xc_buf", [128, 8 * (TC + 6)], F32)
                cacc = SB1("cacc", [128, 2 * TC], F32)
                xbld_c = SB1("xbld_c", [128, 16 * TC], BF16)
                gate_c = SB1("gate_c", [128, 8 * TC], BF16)
                xdp = SB1("xdp", [96, 2 * TC], F32)

                ps_tr = PS1("ps_tr", [128, 4 * TC], BF16)
                ps_ip = PS1("ps_ip", [128, 2 * TC], F32)
                ps_xd = PS1("ps_xd", [128, TC], F32)

                def xtv(a, i):
                    o = ((a % 2) * 4 + i) * DM
                    return xt_s[:, o:o + DM]

                def xlntv(i):
                    return xln_tm[:, i * DM:(i + 1) * DM]

                def xlnfv(k):
                    return xln_fm[:, k * TC:(k + 1) * TC]

                def xcv(dt):
                    return xc_buf[:, dt * (TC + 6):(dt + 1) * (TC + 6)]

                def caccv(dt):
                    return cacc[:, (dt % 2) * TC:(dt % 2) * TC + TC]

                def xbldv(a, k):
                    o = ((a % 2) * 8 + k) * TC
                    return xbld_c[:, o:o + TC]

                def gatev(dt):
                    return gate_c[:, dt * TC:(dt + 1) * TC]

                def ipsv(ps, i):
                    return ps[:, (i % 2) * TC:(i % 2) * TC + TC]

                def trv(i):
                    return ps_tr[:, (i % 2) * 2 * TC:(i % 2) * 2 * TC + TC]


                def sy_p1(sy):
                    def stores(a):
                        for dt in range(8):
                            sy.wait_ge(sem["sil"], 8 * a + dt + 1)
                            sy.dma_start(
                                d_xbld[dt * 128:(dt + 1) * 128,
                                       a * TC:(a + 1) * TC],
                                xbldv(a, dt)).then_inc(sem["xbst"][a % 2], 16)
                            sy.wait_ge(sem["ipe"], 16 * a + 8 + dt + 1)
                            sy.dma_start(
                                d_gate[dt * 128:(dt + 1) * 128,
                                       a * TC:(a + 1) * TC],
                                gatev(dt)).then_inc(sem["gtst"][a % 2], 16)
                        sy.wait_ge(sem["xpe"], a + 1)
                        sy.dma_start(
                            d_arin[:, a * TC:(a + 1) * TC],
                            xdp[:, (a % 2) * TC:(a % 2) * TC + TC]
                        ).then_inc(sem["xds"][a % 2], 16)
                    for a in range(NCH):
                        if a >= 2:
                            sy.wait_ge(sem["vln"], 4 * (a - 2) + 4)
                            sy.wait_ge(sem["sq1"], 4 * (a - 2) + 4)
                        for i in range(4):
                            sy.dma_start(
                                xtv(a, i),
                                x_seq[a * TC + i * 128:a * TC + (i + 1) * 128,
                                      :]).then_inc(sem["xt"][a % 2], 16)
                        if a >= 1:
                            stores(a - 1)
                    stores(NCH - 1)

                def g_p1(g):
                    for dst, src in [(identf, eye), (identb, eye),
                                     (ones_f, onesm), (ones_b, onesm),
                                     (bdt_s, b_dt), (alog_s, a_log),
                                     (cw_s, conv_w4), (cb_s, conv_b8),
                                     (dsk_s, dsk8)]:
                        g.dma_start(dst[:, :], src[:, :]).then_inc(sem["lc"], 16)
                    for k in range(8):
                        g.dma_start(w_in_b[:, k * 2048:(k + 1) * 2048],
                                    w_in[k * 128:(k + 1) * 128, :]
                                    ).then_inc(sem["lc"], 16)
                    for k in range(8):
                        g.dma_start(w_xp_b[:, k * 96:(k + 1) * 96],
                                    w_xp[k * 128:(k + 1) * 128, :]
                                    ).then_inc(sem["lc"], 16)
                    g.dma_start(w_dt_b[:, :], w_dt[:, :]).then_inc(sem["lc"], 16)
                    g.dma_start(sel_b[:, :], selbc[:, :]).then_inc(sem["lc"], 16)
                    g.dma_start(cval_s[:, :], cvals[:, :]).then_inc(sem["lc"], 16)
                    for j in range(NPC):
                        if j >= 2:
                            g.wait_ge(sem["tnA"], j - 1)
                        g.dma_start(slabv(j),
                                    wpiece(w_up_t, w_dn_t, j)
                                    ).then_inc(sem["slab"][j % 2], 16)

                LC_N = (9 + 8 + 8 + 1 + 1 + 1) * 16

                def s_p1(s):
                    wsl = [0]

                    def do_wslab():
                        j = wsl[0]
                        if j < NPC:
                            s.wait_ge(sem["slab"][j % 2],
                                      16 * (j // 2 + 1))
                            s.activation(tscr[:, :], slabv(j), AF.Abs,
                                         bias=0.0, scale=1.0,
                                         accum_out=asums[:, j:j + 1]
                                         ).then_inc(sem["tnA"], 1)
                            wsl[0] += 1

                    for a in range(NCH):
                        for i in range(4):
                            s.wait_ge(sem["xt"][a % 2],
                                      64 * (a // 2 + 1))
                            s.activation(sq_s[:, :], xtv(a, i), AF.Square,
                                         bias=0.0, scale=1.0,
                                         accum_out=ssq_s[:, i:i + 1]
                                         ).then_inc(sem["sq1"], 1)
                        for i in range(4):
                            s.wait_ge(sem["lnv"], 4 * a + i + 1)
                            var = mva[:, 4 * i + 1:4 * i + 2]
                            rstd = mva[:, 4 * i + 2:4 * i + 3]
                            t1 = mva[:, 4 * i + 3:4 * i + 4]
                            s.activation(t1, var, AF.Ln, bias=0.0, scale=1.0)
                            s.activation(rstd, t1, AF.Exp, bias=0.0,
                                         scale=-0.5).then_inc(sem["lns"], 1)
                        for dt in range(8):
                            s.wait_ge(sem["tp"], 8 * a + dt + 1)
                            if a >= 1:
                                s.wait_ge(sem["ipm"], 16 * (a - 1) + 16)
                            s.copy(xlnfv(dt),
                                   trv(dt)).then_inc(sem["xlf"], 1)
                        for f in range(16):
                            s.wait_ge(sem["ipm"], 16 * a + f + 1)
                            if f < 8:
                                if a >= 1:
                                    s.wait_ge(sem["cv"],
                                              2 * (8 * (a - 1) + f) + 2)
                                s.copy(xcv(f)[:, 6:TC + 6],
                                       ipsv(ps_ip, f)).then_inc(sem["ipe"], 1)
                            else:
                                dt = f - 8
                                if a >= 1:
                                    s.wait_ge(sem["gtst"][(a - 1) % 2],
                                              128 * ((a - 1) // 2 + 1))
                                s.activation(gatev(dt), ipsv(ps_ip, f),
                                             AF.Silu, bias=0.0, scale=1.0
                                             ).then_inc(sem["ipe"], 1)
                        for dt in range(8):
                            s.wait_ge(sem["cv"], 2 * (8 * a + dt) + 1)
                            if a >= 2:
                                s.wait_ge(sem["xbst"][a % 2],
                                          128 * ((a - 2) // 2 + 1))
                                s.wait_ge(sem["xpm"], a - 1)
                            s.activation(xbldv(a, dt), caccv(dt), AF.Silu,
                                         bias=cb_s[:, dt:dt + 1], scale=1.0
                                         ).then_inc(sem["sil"], 1)
                        for _ in range(8):
                            do_wslab()
                    while wsl[0] < NPC:
                        do_wslab()

                def v_p1(v):
                    for dt in range(8):
                        v.memset(xcv(dt)[:, 0:6], 0.0)
                    for a in range(NCH):
                        for i in range(4):
                            v.wait_ge(sem["xt"][a % 2],
                                      64 * (a // 2 + 1))
                            v.reduce_sum(s1_s[:, i:i + 1], xtv(a, i), axis=AXX)
                        for i in range(4):
                            v.wait_ge(sem["sq1"], 4 * a + i + 1)
                            m = mva[:, 4 * i:4 * i + 1]
                            var = mva[:, 4 * i + 1:4 * i + 2]
                            t1 = mva[:, 4 * i + 3:4 * i + 4]
                            v.tensor_single_scalar(m, s1_s[:, i:i + 1],
                                                   1.0 / DM, ALU.mult)
                            v.tensor_single_scalar(var, ssq_s[:, i:i + 1],
                                                   1.0 / DM, ALU.mult)
                            v.tensor_scalar(t1, m, m[:, 0:1], None, ALU.mult)
                            v.tensor_tensor(var, var, t1, ALU.subtract)
                            v.tensor_single_scalar(var, var, 1e-5,
                                                   ALU.add).then_inc(sem["lnv"], 1)
                        for i in range(4):
                            v.wait_ge(sem["lns"], 4 * a + i + 1)
                            if a >= 1:
                                v.wait_ge(sem["tp"], 8 * (a - 1) + 8)
                            m = mva[:, 4 * i:4 * i + 1]
                            rstd = mva[:, 4 * i + 2:4 * i + 3]
                            v.tensor_scalar(xlntv(i), xtv(a, i), m[:, 0:1],
                                            rstd[:, 0:1], ALU.subtract,
                                            ALU.mult).then_inc(sem["vln"], 1)
                        for dt in range(8):
                            v.wait_ge(sem["ipe"], 16 * a + dt + 1)
                            if a >= 1 or dt >= 2:
                                v.wait_ge(sem["sil"], 8 * a + dt - 1)
                            acc = caccv(dt)
                            xc = xcv(dt)
                            v.tensor_scalar(acc, xc[:, 0:TC],
                                            cw_s[:, 4 * dt:4 * dt + 1], None,
                                            ALU.mult)
                            for k in range(1, 4):
                                ins = v.scalar_tensor_tensor(
                                    acc, xc[:, k:k + TC],
                                    cw_s[:, 4 * dt + k:4 * dt + k + 1],
                                    acc, ALU.mult, ALU.add)
                            ins.then_inc(sem["cv"], 1)
                            v.tensor_copy(xc[:, 0:6],
                                          xc[:, TC:TC + 6]).then_inc(sem["cv"], 1)
                        if a >= 1:
                            v.wait_ge(sem["xds"][a % 2],
                                      16 * ((a - 2) // 2 + 1))
                        v.wait_ge(sem["xpm"], a + 1)
                        v.tensor_copy(xdp[:, (a % 2) * TC:(a % 2) * TC + TC],
                                      ps_xd[0:96, :]).then_inc(sem["xpe"], 1)

                def t_p1(t):
                    t.wait_ge(sem["lc"], LC_N)
                    for a in range(NCH):
                        for dt in range(8):
                            idx = 8 * a + dt
                            t.wait_ge(sem["vln"], 4 * a + 4)
                            if idx >= 2:
                                t.wait_ge(sem["xlf"], idx - 1)
                            for i in range(4):
                                tr = t.transpose(
                                    trv(dt)[:, i * 128:(i + 1) * 128],
                                    xlntv(i)[:, dt * 128:(dt + 1) * 128],
                                    identb[:, :])
                            tr.then_inc(sem["tp"], 1)
                        for f in range(16):
                            idx = 16 * a + f
                            if f == 0:
                                t.wait_ge(sem["xlf"], 8 * a + 8)
                            if idx >= 2:
                                t.wait_ge(sem["ipe"], idx - 1)
                            for k in range(8):
                                mm = t.matmul(
                                    ipsv(ps_ip, f),
                                    w_in_b[:, k * 2048 + f * 128:
                                           k * 2048 + (f + 1) * 128],
                                    xlnfv(k),
                                    start=(k == 0), stop=(k == 7))
                            mm.then_inc(sem["ipm"], 1)
                        if a >= 1:
                            t.wait_ge(sem["xpe"], a)
                        for k in range(8):
                            t.wait_ge(sem["sil"], 8 * a + k + 1)
                            mm = t.matmul(ps_xd[0:96, :],
                                          w_xp_b[:, k * 96:(k + 1) * 96],
                                          xbldv(a, k),
                                          start=(k == 0), stop=(k == 7))
                        mm.then_inc(sem["xpm"], 1)

                blk.sync(sy_p1)
                blk.gpsimd(g_p1)
                blk.scalar(s_p1)
                blk.vector(v_p1)
                blk.tensor(t_p1)
                e1.close()

                with ExitStack() as e23:
                    SB23 = lambda n, s, d: e23.enter_context(
                        nc.sbuf_tensor(n, s, d))
                    xdbl = SB23("xdbl", [96, L], F32)
                    dtl_b = SB23("dtl_b", [64, L], BF16)
                    bc_b = SB23("bc_b", [32, L], BF16)

                    def g_p2(g):
                        g.wait_ge(sem["xds"][0], 16 * (NCH // 2))
                        g.wait_ge(sem["xds"][1], 16 * (NCH // 2))
                        g.collective_compute(
                            "AllReduce", ALU.add, replica_groups=RG,
                            ins=[d_arin.ap().opt()],
                            outs=[d_arout.ap().opt()],
                        ).then_inc(sem["arv"], 1)
                        g.wait_ge(sem["arv"], 1)
                        g.dma_start(xdbl[:, :],
                                    d_arout[:, :]).then_inc(sem["arl"], 16)

                    def s_p2(s):
                        s.wait_ge(sem["lc"], LC_N)
                        s.activation(aneg_s[:, :], alog_s[:, :], AF.Exp,
                                     bias=0.0, scale=1.0).then_inc(sem["p2s"], 1)

                    def v_p2(v):
                        v.wait_ge(sem["arl"], 16)
                        v.tensor_copy(dtl_b[:, :], xdbl[0:64, :])
                        v.tensor_copy(bc_b[:, :], xdbl[64:96, :])
                        v.tensor_tensor(b2_s[:, :], bdt_s[:, :], bdt_s[:, :],
                                        ALU.add)
                        v.memset(hcarry[:, :], 0.0).then_inc(sem["p2v"], 1)
                        v.wait_ge(sem["p2s"], 1)
                        v.tensor_single_scalar(aneg_s[:, :], aneg_s[:, :],
                                               -1.0,
                                               ALU.mult).then_inc(sem["p2v"], 1)
                        v.wait_ge(sem["tnA"], NPC)
                        v.reduce_sum(tauv[:, 0:1], asums[:, 0:32], axis=AXX)
                        v.reduce_sum(tauv[:, 1:2], asums[:, 32:NPC],
                                     axis=AXX).then_inc(sem["p2v"], 1)

                    def t_p2(t):
                        t.wait_ge(sem["p2v"], 3)
                        t.matmul(ps_tau[0:1, 0:1], ones_f[:, 0:1],
                                 tauv[:, 0:1], start=True, stop=True)
                        t.matmul(ps_tau[0:1, 1:2], ones_f[:, 0:1],
                                 tauv[:, 1:2], start=True,
                                 stop=True).then_inc(sem["p2t"], 1)

                    def v_p2b(v):
                        v.wait_ge(sem["p2t"], 1)
                        v.tensor_single_scalar(tauv[0:1, 2:3],
                                               ps_tau[0:1, 0:1],
                                               0.5 / (DM * DFF), ALU.mult)
                        v.tensor_single_scalar(tauv[0:1, 3:4],
                                               ps_tau[0:1, 1:2],
                                               0.5 / (DM * DFF),
                                               ALU.mult).then_inc(sem["p2v"], 1)

                    def t_p2b(t):
                        t.wait_ge(sem["p2v"], 4)
                        t.matmul(ps_tau[:, 2:3], ones_f[0:1, :],
                                 tauv[0:1, 2:3], start=True, stop=True)
                        t.matmul(ps_tau[:, 3:4], ones_f[0:1, :],
                                 tauv[0:1, 3:4], start=True,
                                 stop=True).then_inc(sem["p2t"], 1)

                    def s_p2b(s):
                        s.wait_ge(sem["p2t"], 2)
                        s.copy(tau_b[:, 0:2],
                               ps_tau[:, 2:4]).then_inc(sem["p2s"], 1)

                    blk.gpsimd(g_p2)
                    blk.scalar(s_p2)
                    blk.vector(v_p2)
                    blk.tensor(t_p2)
                    blk.vector(v_p2b)
                    blk.tensor(t_p2b)
                    blk.scalar(s_p2b)

                    with ExitStack() as e3:
                        SB3 = lambda n, s, d: e3.enter_context(
                            nc.sbuf_tensor(n, s, d))
                        PS3 = lambda n, s, d: e3.enter_context(
                            nc.psum_tensor(n, s, d))
                        Bb = SB3("Bb", [128, 2 * NST * TC], BF16)
                        Cb = SB3("Cb", [128, 2 * NST * TC], BF16)
                        spA = SB3("spA", [128, 2 * TC], F32)
                        spB = SB3("spB", [128, 2 * TC], F32)
                        dlt_s = SB3("dlt_s", [128, 2 * TC], BF16)
                        du_s = SB3("du_s", [128, 2 * TC], BF16)
                        u_s = SB3("u_s", [128, 2 * TC], BF16)
                        gt_s = SB3("gt_s", [128, 2 * TC], BF16)
                        dA_s = SB3("dA_s", [128, 4 * TC], F32)
                        bta_s = SB3("bta_s", [128, TC], BF16)
                        h_s = SB3("h_s", [128, NST * TC], BF16)
                        m_s = SB3("m_s", [128, NST * TC], BF16)
                        yg_s = SB3("yg_s", [128, 2 * TC], BF16)
                        y1_s = SB3("y1_s", [128, TC], F32)
                        ps_bc = PS3("ps_bc", [128, 2 * TC], F32)
                        ps_dl = PS3("ps_dl", [128, 2 * TC], F32)

                        def b2v(t_, j):
                            return t_[:, (j % 2) * TC:(j % 2) * TC + TC]

                        def bbv(t_, tc):
                            o = (tc % 2) * NST * TC
                            return t_[:, o:o + NST * TC]

                        def t_p3(t):
                            t.wait_ge(sem["p2v"], 1)
                            t.wait_ge(sem["p2s"], 2)
                            for tc in range(NCH):
                                for n in range(32):
                                    i = 32 * tc + n
                                    if i >= 2:
                                        t.wait_ge(sem["bce"], i - 1)
                                    row = (n // 2) + (16 if (n % 2) else 0)
                                    t.matmul(b2v(ps_bc, i),
                                             sel_b[:, row * 128:(row + 1) * 128],
                                             bc_b[:, tc * TC:(tc + 1) * TC],
                                             start=True,
                                             stop=True).then_inc(sem["bcm"], 1)
                                for dt in range(8):
                                    j = tc * 8 + dt
                                    if j >= 2:
                                        t.wait_ge(sem["spd"], j - 1)
                                    t.matmul(b2v(ps_dl, j),
                                             w_dt_b[:, dt * 128:(dt + 1) * 128],
                                             dtl_b[:, tc * TC:(tc + 1) * TC],
                                             start=True,
                                             stop=True).then_inc(sem["mmd"], 1)

                        def s_p3(s):
                            tern = [0]

                            def do_tern():
                                j = tern[0]
                                if j >= NPC:
                                    return
                                s.wait_ge(sem["slab"][j % 2],
                                          16 * (NPC // 2 + j // 2 + 1))
                                if j >= 1:
                                    s.wait_ge(sem["tnV"], j)
                                s.activation(tscr[:, :], slabv(j), AF.Abs,
                                             bias=0.0, scale=1.0)
                                s.activation(sgn_s[:, :], slabv(j), AF.Sign,
                                             bias=0.0,
                                             scale=1.0).then_inc(sem["tnA"], 1)
                                tern[0] += 1

                            s.wait_ge(sem["p2v"], 2)
                            for tc in range(NCH):
                                if tc >= 2:
                                    s.wait_ge(sem["cmv"], 8 * (tc - 1))
                                for n in range(32):
                                    i = 32 * tc + n
                                    s.wait_ge(sem["bcm"], i + 1)
                                    row = n // 2
                                    dstb = Bb if (n % 2) == 0 else Cb
                                    s.copy(bbv(dstb, tc)[:, row * TC:
                                                         (row + 1) * TC],
                                           b2v(ps_bc, i)).then_inc(sem["bce"], 1)
                                for dt in range(8):
                                    j = tc * 8 + dt
                                    s.wait_ge(sem["mmd"], j + 1)
                                    if j >= 2:
                                        s.wait_ge(sem["dlv"], j - 1)
                                    b2c = b2_s[:, dt:dt + 1]
                                    s.activation(b2v(spA, j), b2v(ps_dl, j),
                                                 AF.Abs, bias=b2c, scale=1.0)
                                    s.activation(b2v(spB, j), b2v(spA, j),
                                                 AF.Exp, bias=0.0, scale=-1.0)
                                    s.activation(b2v(spA, j), b2v(spB, j),
                                                 AF.Ln, bias=1.0, scale=1.0)
                                    s.activation(b2v(spB, j), b2v(ps_dl, j),
                                                 AF.Relu, bias=b2c,
                                                 scale=1.0).then_inc(sem["spd"], 1)
                                    for n in range(NST):
                                        idx = 16 * j + n
                                        s.wait_ge(sem["dlv"], j + 1)
                                        if idx >= 4:
                                            s.wait_ge(sem["scn"], idx - 3)
                                        col = dt * 16 + n
                                        s.activation(
                                            dA_s[:, (idx % 4) * TC:
                                                 (idx % 4) * TC + TC],
                                            b2v(dlt_s, j), AF.Exp, bias=0.0,
                                            scale=aneg_s[:, col:col + 1]
                                        ).then_inc(sem["dAc"], 1)
                                    do_tern()
                            while tern[0] < NPC:
                                do_tern()

                        def g_p3(g):
                            g.wait_ge(sem["p2s"], 2)
                            for j in range(NPC):
                                if j >= 2:
                                    g.wait_ge(sem["tnV"], j - 1)
                                g.dma_start(slabv(j),
                                            wpiece(w_up_t, w_dn_t, j)
                                            ).then_inc(sem["slab"][j % 2], 16)
                                if j >= 1:
                                    wj = j - 1
                                    g.wait_ge(sem["tnV"], wj + 1)
                                    if wj < 32:
                                        sj, q = divmod(wj, 4)
                                        wdst = d_wtu[sj * 128:(sj + 1) * 128,
                                                     q * 1024:(q + 1) * 1024]
                                    else:
                                        js = wj - 32
                                        wdst = d_wtd[js * 128:(js + 1) * 128, :]
                                    g.dma_start(wdst,
                                                wtv(wj)).then_inc(sem["wtst"][wj % 2], 16)
                            wj = NPC - 1
                            g.wait_ge(sem["tnV"], wj + 1)
                            g.dma_start(d_wtd[(wj - 32) * 128:
                                              (wj - 31) * 128, :],
                                        wtv(wj)).then_inc(sem["wtst"][wj % 2], 16)

                        def sy_p3(sy):
                            def ystore(pj):
                                ptc, pdt = divmod(pj, 8)
                                sy.wait_ge(sem["gat"], pj + 1)
                                chh = NCH // 2
                                dst = d_ystore[ptc // chh][
                                    pdt * 128:(pdt + 1) * 128,
                                    (ptc % chh) * TC:(ptc % chh + 1) * TC]
                                sy.dma_start(dst,
                                             b2v(yg_s, pj)).then_inc(
                                                 sem["ygst"][pj % 2], 16)
                            for j in range(NJ):
                                tc, dt = divmod(j, 8)
                                if j >= 2:
                                    sy.wait_ge(sem["gat"], j - 1)
                                sy.wait_ge(sem["xbst"][tc % 2],
                                           128 * (tc // 2 + 1))
                                sy.dma_start(b2v(u_s, j),
                                             d_xbld[dt * 128:(dt + 1) * 128,
                                                    tc * TC:(tc + 1) * TC]
                                             ).then_inc(sem["ug"][j % 2], 16)
                                sy.wait_ge(sem["gtst"][tc % 2],
                                           128 * (tc // 2 + 1))
                                sy.dma_start(b2v(gt_s, j),
                                             d_gate[dt * 128:(dt + 1) * 128,
                                                    tc * TC:(tc + 1) * TC]
                                             ).then_inc(sem["ug"][j % 2], 16)
                                if j >= 1:
                                    ystore(j - 1)
                            ystore(NJ - 1)

                        def v_p3(v):
                            tern = [0]

                            def do_tern():
                                j = tern[0]
                                if j >= NPC:
                                    return
                                v.wait_ge(sem["tnA"], NPC + j + 1)
                                if j >= 2:
                                    v.wait_ge(sem["wtst"][j % 2],
                                              16 * ((j - 2) // 2 + 1))
                                tcol = (tau_b[:, 0:1] if j < 32
                                        else tau_b[:, 1:2])
                                v.tensor_scalar(msk_s[:, :], tscr[:, :],
                                                tcol, None, ALU.is_gt)
                                v.tensor_tensor(wtv(j), msk_s[:, :],
                                                sgn_s[:, :],
                                                ALU.mult).then_inc(sem["tnV"], 1)
                                tern[0] += 1

                            for j in range(NJ):
                                tc, dt = divmod(j, 8)
                                v.wait_ge(sem["spd"], j + 1)
                                if j >= 2:
                                    v.wait_ge(sem["dAc"], 16 * (j - 2) + 16)
                                v.tensor_tensor(b2v(dlt_s, j), b2v(spB, j),
                                                b2v(spA, j),
                                                ALU.add).then_inc(sem["dlv"], 1)
                                v.wait_ge(sem["ug"][j % 2],
                                          32 * (j // 2 + 1))
                                v.tensor_tensor(b2v(du_s, j), b2v(dlt_s, j),
                                                b2v(u_s, j), ALU.mult)
                                v.wait_ge(sem["bce"], 32 * (tc + 1))
                                for n in range(NST):
                                    idx = 16 * j + n
                                    v.wait_ge(sem["dAc"], idx + 1)
                                    v.tensor_tensor(
                                        bta_s[:, :], b2v(du_s, j),
                                        bbv(Bb, tc)[:, n * TC:(n + 1) * TC],
                                        ALU.mult)
                                    v.tensor_tensor_scan(
                                        h_s[:, n * TC:(n + 1) * TC],
                                        dA_s[:, (idx % 4) * TC:
                                             (idx % 4) * TC + TC],
                                        bta_s[:, :],
                                        hcarry[:, dt * 16 + n:
                                               dt * 16 + n + 1],
                                        ALU.mult, ALU.add).then_inc(sem["scn"], 1)
                                v.wait_ge(sem["scn"], 16 * (j + 1))
                                v.tensor_copy(
                                    hcarry[:, dt * 16:(dt + 1) * 16],
                                    bass.AP(h_s, TC - 1,
                                            [[NST * TC, 128], [TC, NST]]))
                                v.tensor_tensor(m_s[:, :], h_s[:, :],
                                                bbv(Cb, tc), ALU.mult)
                                v.tensor_tensor(m_s[:, 0:8 * TC],
                                                m_s[:, 0:8 * TC],
                                                m_s[:, 8 * TC:16 * TC], ALU.add)
                                v.tensor_tensor(m_s[:, 0:4 * TC],
                                                m_s[:, 0:4 * TC],
                                                m_s[:, 4 * TC:8 * TC], ALU.add)
                                v.tensor_tensor(m_s[:, 0:2 * TC],
                                                m_s[:, 0:2 * TC],
                                                m_s[:, 2 * TC:4 * TC], ALU.add)
                                v.tensor_tensor(m_s[:, 0:TC], m_s[:, 0:TC],
                                                m_s[:, TC:2 * TC],
                                                ALU.add).then_inc(sem["cmv"], 1)
                                if j >= 2:
                                    v.wait_ge(sem["ygst"][j % 2],
                                              16 * ((j - 2) // 2 + 1))
                                v.tensor_scalar(y1_s[:, :], b2v(u_s, j),
                                                dsk_s[:, dt:dt + 1], None,
                                                ALU.mult)
                                v.tensor_tensor(y1_s[:, :], y1_s[:, :],
                                                m_s[:, 0:TC], ALU.add)
                                v.tensor_tensor(b2v(yg_s, j), y1_s[:, :],
                                                b2v(gt_s, j),
                                                ALU.mult).then_inc(sem["gat"], 1)
                                do_tern()

                            while tern[0] < NPC:
                                do_tern()

                        blk.tensor(t_p3)
                        blk.scalar(s_p3)
                        blk.gpsimd(g_p3)
                        blk.sync(sy_p3)
                        blk.vector(v_p3)

        with ExitStack() as e5:
            SB5 = lambda n, s, d: e5.enter_context(nc.sbuf_tensor(n, s, d))
            PS5 = lambda n, s, d: e5.enter_context(nc.psum_tensor(n, s, d))
            wob = SB5("wob", [128, 16 * DM], BF16)

            def g_p45(g):
                g.wait_ge(sem["ygst"][0], 16 * (NJ // 4))
                g.wait_ge(sem["ygst"][1], 16 * (NJ // 4))
                g.collective_compute(
                    "AllGather", ALU.bypass, replica_groups=RG,
                    ins=[d_ystore[0][:, :].opt()],
                    outs=[bass.AP(d_agy, 0,
                                  [[TOWN, 2 * DH], [1, TOWN]]).opt()],
                ).then_inc(sem["a2a"], 1)
                g.wait_ge(sem["ygst"][0], 16 * (NJ // 2))
                g.wait_ge(sem["ygst"][1], 16 * (NJ // 2))
                if DEBUG_OUT:
                    g.dma_start(d_dbgy.ap(), d_ystore.ap()).then_inc(
                        sem["lc"], 16)
                    g.wait_ge(sem["lc"], LC_N + 16)
                g.collective_compute(
                    "AllGather", ALU.bypass, replica_groups=RG,
                    ins=[d_ystore[1][:, :].opt()],
                    outs=[bass.AP(d_agy, 2 * DH * TOWN,
                                  [[TOWN, 2 * DH], [1, TOWN]]).opt()],
                ).then_inc(sem["a2a"], 1)
                g.wait_ge(sem["wtst"][0], 16 * (NPC // 2))
                g.wait_ge(sem["wtst"][1], 16 * (NPC // 2))
                for k in range(16):
                    g.dma_start(wob[:, k * DM:(k + 1) * DM],
                                w_out[k * 128:(k + 1) * 128, :]
                                ).then_inc(sem["wol"], 16)
                with g.register("rsec0") as r0, g.register("rsec1") as r1:
                    g.reg_load(r0, roff[0:1, 0:1])
                    g.reg_load(r1, roff[0:1, 1:2])
                    sec0 = g.snap(r0)
                    sec1 = g.snap(r1)
                    g.wait_ge(sem["a2a"], 2)
                    for fc in range(NF5):
                        if fc >= 2:
                            g.wait_ge(sem["opm"], 8 * (fc - 1))
                        for half, sec in ((0, sec0), (1, sec1)):
                            srcap = bass.AP(
                                d_agy,
                                sec * (DH * TOWN) + fc * TC,
                                [[TOWN, 128], [128 * TOWN, 8], [1, TC]])
                            dstap = bass.AP(
                                ytl,
                                ((fc % 2) * 16 + half * 8) * TC,
                                [[32 * TC, 128], [TC, 8], [1, TC]])
                            g.dma_start(dstap,
                                        srcap).then_inc(sem["ytl"][fc % 2], 16)

            ytl = SB5("ytl", [128, 32 * TC], BF16)
            xt5 = SB5("xt5", [128, 8 * DM], F32)
            xres = SB5("xres", [128, 2 * TC], F32)
            x2c = SB5("x2c", [128, 16 * TC], F32)
            sqt = SB5("sqt", [128, 2 * TC], F32)
            vm = SB5("vm", [1, TC], F32)
            vv = SB5("vv", [1, TC], F32)
            vq = SB5("vq", [1, TC], F32)
            vt1 = SB5("vt1", [1, TC], F32)
            vt2 = SB5("vt2", [1, TC], F32)
            vrstd = SB5("vrstd", [1, TC], F32)
            vrms2 = SB5("vrms2", [1, TC], F32)
            sv_s = SB5("sv_s", [1, TC], F32)
            sv_ms = SB5("sv_ms", [1, TC], F32)
            sb5 = SB5("sb5", [128, 2 * TC], F32)
            xnc = SB5("xnc", [128, 2 * TC], BF16)
            xtm5 = SB5("xtm5", [128, TC], F32)
            ps_o = PS5("ps_o", [128, 2 * TC], F32)
            ps_t5 = PS5("ps_t5", [128, 2 * TC], F32)
            ps_s15 = PS5("ps_s15", [1, TC], F32)
            ps_s25 = PS5("ps_s25", [1, TC], F32)
            ps_b5 = PS5("ps_b5", [128, 2 * TC], F32)

            def ytlv(fc, k):
                o = ((fc % 2) * 16 + k) * TC
                return ytl[:, o:o + TC]

            def xt5v(fc, i):
                o = ((fc % 2) * 4 + i) * DM
                return xt5[:, o:o + DM]

            def x2cv(fc, f):
                o = ((fc % 2) * 8 + f) * TC
                return x2c[:, o:o + TC]

            def i2v(t_, i):
                return t_[:, (i % 2) * TC:(i % 2) * TC + TC]

            blk.gpsimd(g_p45)

            def sy_p5(sy):
                sy.wait_ge(sem["wtst"][0], 16 * (NPC // 2))
                sy.wait_ge(sem["wtst"][1], 16 * (NPC // 2))
                sy.wait_ge(sem["ygst"][0], 16 * (NJ // 2))
                sy.wait_ge(sem["ygst"][1], 16 * (NJ // 2))

                def stores(fc):
                    for f in range(8):
                        sy.wait_ge(sem["x2e"], 8 * fc + f + 1)
                        sy.dma_start(
                            d_x2[f * 128:(f + 1) * 128,
                                 fc * TC:(fc + 1) * TC],
                            x2cv(fc, f)).then_inc(sem["x2st"][fc % 2], 16)
                        sy.wait_ge(sem["xne"], 8 * fc + f + 1)
                        sy.dma_start(
                            d_xn[f * 128:(f + 1) * 128,
                                 fc * TC:(fc + 1) * TC],
                            i2v(xnc, f)).then_inc(sem["xnst"][f % 2], 16)
                for fc in range(NF5):
                    if fc >= 2:
                        sy.wait_ge(sem["tp5"], 8 * (fc - 1))
                    for i in range(4):
                        row0 = fc * TC + i * 128
                        sy.dma_start(xt5v(fc, i),
                                     x_own[row0:row0 + 128, :]
                                     ).then_inc(sem["x5"][fc % 2], 16)
                    if fc >= 1:
                        stores(fc - 1)
                stores(NF5 - 1)

            def t_p5(t):
                t.wait_ge(sem["wol"], 16 * 16)
                for fc in range(NF5):
                    t.wait_ge(sem["ytl"][fc % 2], 32 * (fc // 2 + 1))
                    t.wait_ge(sem["x5"][fc % 2], 64 * (fc // 2 + 1))
                    for f in range(8):
                        idx = 8 * fc + f
                        if idx >= 2:
                            t.wait_ge(sem["tpe5"], idx - 1)
                        for i in range(4):
                            tr = t.transpose(
                                i2v(ps_t5, f)[:, i * 128:(i + 1) * 128],
                                xt5v(fc, i)[:, f * 128:(f + 1) * 128],
                                identf[:, :])
                        tr.then_inc(sem["tp5"], 1)
                        if idx >= 2:
                            t.wait_ge(sem["x2e"], idx - 1)
                        for k in range(16):
                            mm = t.matmul(i2v(ps_o, f),
                                          wob[:, k * DM + f * 128:
                                              k * DM + (f + 1) * 128],
                                          ytlv(fc, k),
                                          start=(k == 0), stop=(k == 15))
                        mm.then_inc(sem["opm"], 1)
                    if fc >= 1:
                        t.wait_ge(sem["nb5"], fc)
                    for k in range(8):
                        idx = 8 * fc + k
                        t.wait_ge(sem["x2e"], idx + 1)
                        t.wait_ge(sem["sq5"], idx + 1)
                        t.matmul(ps_s15[:, :], ones_f[:, 0:1], x2cv(fc, k),
                                 start=(k == 0), stop=(k == 7))
                        t.matmul(ps_s25[:, :], ones_f[:, 0:1], i2v(sqt, k),
                                 start=(k == 0),
                                 stop=(k == 7)).then_inc(sem["st5"], 1)
                    t.wait_ge(sem["nb5"], fc + 1)
                    if fc >= 1:
                        t.wait_ge(sem["sse"], 2 * fc)
                    t.matmul(ps_b5[:, 0:TC], ones_f[0:1, :], sv_s[0:1, :],
                             start=True, stop=True)
                    t.matmul(ps_b5[:, TC:2 * TC], ones_f[0:1, :],
                             sv_ms[0:1, :], start=True,
                             stop=True).then_inc(sem["opb"], 1)

            def s_p5(s):
                for fc in range(NF5):
                    for dt in range(8):
                        idx = 8 * fc + dt
                        s.wait_ge(sem["tp5"], idx + 1)
                        if idx >= 2:
                            s.wait_ge(sem["x2e"], idx - 1)
                        s.copy(i2v(xres, dt),
                               i2v(ps_t5, dt)).then_inc(sem["tpe5"], 1)
                    for k in range(8):
                        idx = 8 * fc + k
                        s.wait_ge(sem["x2e"], idx + 1)
                        if idx >= 2:
                            s.wait_ge(sem["st5"], idx - 1)
                        s.activation(i2v(sqt, k), x2cv(fc, k), AF.Square,
                                     bias=0.0, scale=1.0).then_inc(sem["sq5"], 1)
                    s.wait_ge(sem["vst5"], fc + 1)
                    s.activation(vt1[0:1, :], vv[0:1, :], AF.Ln, bias=1e-5,
                                 scale=1.0)
                    s.activation(vrstd[0:1, :], vt1[0:1, :], AF.Exp, bias=0.0,
                                 scale=-0.5)
                    s.activation(vt1[0:1, :], vq[0:1, :], AF.Ln, bias=1e-35,
                                 scale=1.0)
                    s.activation(vrms2[0:1, :], vt1[0:1, :], AF.Exp, bias=0.0,
                                 scale=0.5).then_inc(sem["ast5"], 1)
                    s.wait_ge(sem["opb"], fc + 1)
                    if fc >= 1:
                        s.wait_ge(sem["xne"], 8 * fc)
                    s.copy(sb5[:, 0:TC], ps_b5[:, 0:TC])
                    s.copy(sb5[:, TC:2 * TC],
                           ps_b5[:, TC:2 * TC]).then_inc(sem["sse"], 2)

            def v_p5(v):
                for fc in range(NF5):
                    for f in range(8):
                        idx = 8 * fc + f
                        v.wait_ge(sem["tpe5"], idx + 1)
                        v.wait_ge(sem["opm"], idx + 1)
                        if fc >= 2:
                            v.wait_ge(sem["x2st"][fc % 2],
                                      128 * ((fc - 2) // 2 + 1))
                            v.wait_ge(sem["st5"], 8 * (fc - 1))
                            v.wait_ge(sem["xne"], 8 * (fc - 1))
                        v.tensor_tensor(x2cv(fc, f), i2v(xres, f),
                                        i2v(ps_o, f),
                                        ALU.add).then_inc(sem["x2e"], 1)
                    v.wait_ge(sem["st5"], 8 * fc + 8)
                    v.tensor_copy(vm[0:1, :], ps_s15[:, :])
                    v.tensor_copy(vv[0:1, :], ps_s25[:, :])
                    v.tensor_single_scalar(vm[0:1, :], vm[0:1, :],
                                           1.0 / DM, ALU.mult)
                    v.tensor_single_scalar(vv[0:1, :], vv[0:1, :],
                                           1.0 / DM, ALU.mult)
                    v.tensor_tensor(vt1[0:1, :], vm[0:1, :], vm[0:1, :],
                                    ALU.mult)
                    v.tensor_tensor(vv[0:1, :], vv[0:1, :], vt1[0:1, :],
                                    ALU.subtract)
                    v.tensor_single_scalar(vt1[0:1, :], vv[0:1, :], 1e-5,
                                           ALU.add)
                    v.reciprocal(vt2[0:1, :], vt1[0:1, :])
                    v.tensor_tensor(vq[0:1, :], vv[0:1, :], vt2[0:1, :],
                                    ALU.mult).then_inc(sem["vst5"], 1)
                    v.wait_ge(sem["ast5"], fc + 1)
                    v.tensor_single_scalar(vt1[0:1, :], vrms2[0:1, :], 1e-8,
                                           ALU.add)
                    v.reciprocal(vt2[0:1, :], vt1[0:1, :])
                    v.tensor_tensor(sv_s[0:1, :], vrstd[0:1, :], vt2[0:1, :],
                                    ALU.mult)
                    v.tensor_tensor(sv_ms[0:1, :], vm[0:1, :], sv_s[0:1, :],
                                    ALU.mult).then_inc(sem["nb5"], 1)
                    v.wait_ge(sem["sse"], 2 * fc + 2)
                    for k in range(8):
                        idx = 8 * fc + k
                        if idx >= 2:
                            v.wait_ge(sem["xnst"][k % 2],
                                      16 * (4 * fc + k // 2))
                        v.tensor_tensor(xtm5[:, :], x2cv(fc, k),
                                        sb5[:, 0:TC], ALU.mult)
                        v.tensor_tensor(i2v(xnc, k), xtm5[:, :],
                                        sb5[:, TC:2 * TC],
                                        ALU.subtract).then_inc(sem["xne"], 1)

            blk.sync(sy_p5)
            blk.tensor(t_p5)
            blk.scalar(s_p5)
            blk.vector(v_p5)

        with ExitStack() as e6:
            SB6 = lambda n, s, d: e6.enter_context(nc.sbuf_tensor(n, s, d))
            PS6 = lambda n, s, d: e6.enter_context(nc.psum_tensor(n, s, d))
            wtu_r = SB6("wtu_r", [128, 8 * DFF], BF16)
            wtd_r = SB6("wtd_r", [128, 32 * DM], BF16)
            xn6 = SB6("xn6", [128, 16 * TC2], BF16)
            x26 = SB6("x26", [128, 16 * TC2], F32)
            h6 = SB6("h6", [128, 32 * TC2], BF16)
            sqh = SB6("sqh", [128, 2 * TC2], BF16)
            w6a = SB6("w6a", [1, TC2], F32)
            w6b = SB6("w6b", [1, TC2], F32)
            w6c = SB6("w6c", [1, TC2], F32)
            w6d = SB6("w6d", [1, TC2], F32)
            w6e = SB6("w6e", [1, TC2], F32)
            w6f = SB6("w6f", [1, TC2], F32)
            sv_s2 = SB6("sv_s2", [1, TC2], F32)
            s2b6 = SB6("s2b6", [128, TC2], F32)
            x36 = SB6("x36", [128, 8 * TC2], F32)
            xtm6 = SB6("xtm6", [128, TC2], F32)
            ostg = SB6("ostg", [128, 2 * DM], F32)
            ps_u = PS6("ps_u", [128, 4 * TC2], F32)
            ps_d = PS6("ps_d", [128, 4 * TC2], F32)
            ps_h2 = PS6("ps_h2", [1, 2 * TC2], F32)
            ps_b6 = PS6("ps_b6", [128, 2 * TC2], F32)
            ps_to = PS6("ps_to", [128, 2 * 512], F32)

            def xn6v(fc, k):
                o = ((fc % 2) * 8 + k) * TC2
                return xn6[:, o:o + TC2]

            def x26v(fc, k):
                o = ((fc % 2) * 8 + k) * TC2
                return x26[:, o:o + TC2]

            def h6v(f):
                return h6[:, f * TC2:(f + 1) * TC2]

            def x36v(f):
                return x36[:, f * TC2:(f + 1) * TC2]

            def j2v(t_, i):
                return t_[:, (i % 2) * TC2:(i % 2) * TC2 + TC2]

            def jpv(ps, i):
                return ps[:, (i % 2) * 2 * TC2:(i % 2) * 2 * TC2 + TC2]

            def tov(g2):
                return ps_to[:, g2 * 512:(g2 + 1) * 512]

            def ostv(i):
                return ostg[:, i * DM:(i + 1) * DM]

            def sy_p6(sy):
                sy.wait_ge(sem["x2st"][0], 128 * (NF5 // 2))
                sy.wait_ge(sem["x2st"][1], 128 * (NF5 // 2))
                sy.wait_ge(sem["xnst"][0], 16 * 4 * NF5)
                sy.wait_ge(sem["xnst"][1], 16 * 4 * NF5)
                sy.wait_ge(sem["wtst"][0], 16 * (NPC // 2))
                sy.wait_ge(sem["wtst"][1], 16 * (NPC // 2))
                for k in range(8):
                    sy.dma_start(xn6v(0, k),
                                 d_xn[k * 128:(k + 1) * 128, 0:TC2]
                                 ).then_inc(sem["xnl"][0], 16)
                    sy.dma_start(x26v(0, k),
                                 d_x2[k * 128:(k + 1) * 128, 0:TC2]
                                 ).then_inc(sem["xnl"][0], 16)
                for j in range(8):
                    sy.dma_start(wtu_r[:, j * DFF:(j + 1) * DFF],
                                 d_wtu[j * 128:(j + 1) * 128, :]
                                 ).then_inc(sem["wtl"], 16)
                for j in range(32):
                    sy.dma_start(wtd_r[:, j * DM:(j + 1) * DM],
                                 d_wtd[j * 128:(j + 1) * 128, :]
                                 ).then_inc(sem["wtl"], 16)

                def ostores(fc):
                    for i in range(2):
                        sy.wait_ge(sem["tpe6"], 4 * fc + 2 * (i + 1))
                        sy.dma_start(
                            out[fc * TC2 + i * 128:
                                fc * TC2 + (i + 1) * 128, :],
                            ostv(i)).then_inc(sem["ost"][i], 16)
                for fc in range(1, NF6 + 1):
                    if fc >= 2:
                        sy.wait_ge(sem["upm"], 32 * (fc - 1))
                        sy.wait_ge(sem["x3e"], 8 * (fc - 1))
                    if fc < NF6:
                        for k in range(8):
                            sy.dma_start(xn6v(fc, k),
                                         d_xn[k * 128:(k + 1) * 128,
                                              fc * TC2:(fc + 1) * TC2]
                                         ).then_inc(sem["xnl"][fc % 2], 16)
                            sy.dma_start(x26v(fc, k),
                                         d_x2[k * 128:(k + 1) * 128,
                                              fc * TC2:(fc + 1) * TC2]
                                         ).then_inc(sem["xnl"][fc % 2], 16)
                    ostores(fc - 1)
                sy.wait_ge(sem["ost"][0], 16 * NF6)
                sy.wait_ge(sem["ost"][1], 16 * NF6)

            def t_p6(t):
                t.wait_ge(sem["wtl"], 16 * 8)
                for fc in range(NF6):
                    t.wait_ge(sem["xnl"][fc % 2], 256 * (fc // 2 + 1))
                    if fc >= 1:
                        t.wait_ge(sem["dnm"], 8 * fc)
                    for f in range(32):
                        idx = 32 * fc + f
                        if idx >= 2:
                            t.wait_ge(sem["gel"], idx - 1)
                        for k in range(8):
                            mm = t.matmul(jpv(ps_u, f),
                                          wtu_r[:, k * DFF + f * 128:
                                                k * DFF + (f + 1) * 128],
                                          xn6v(fc, k),
                                          start=(k == 0), stop=(k == 7))
                        mm.then_inc(sem["upm"], 1)
                        if f >= 2:
                            g2 = f - 2
                            if g2 == 0:
                                t.wait_ge(sem["s2v"], fc)
                            t.wait_ge(sem["hsq"], 32 * fc + g2 + 1)
                            t.matmul(ps_h2[0:1, 0:TC2], ones_b[:, 0:1],
                                     j2v(sqh, g2),
                                     start=(g2 == 0),
                                     stop=False).then_inc(sem["h2m"], 1)
                    for g2 in (30, 31):
                        t.wait_ge(sem["hsq"], 32 * fc + g2 + 1)
                        t.matmul(ps_h2[0:1, 0:TC2], ones_b[:, 0:1], j2v(sqh, g2),
                                 start=False,
                                 stop=(g2 == 31)).then_inc(sem["h2m"], 1)
                    t.wait_ge(sem["s2v"], fc + 1)
                    if fc >= 1:
                        t.wait_ge(sem["sbe"], fc)
                    t.matmul(ps_b6[:, 0:TC2], ones_f[0:1, :], sv_s2[0:1, :],
                             start=True, stop=True).then_inc(sem["opb"], 1)
                    if fc == 0:
                        t.wait_ge(sem["wtl"], 16 * 40)
                    t.wait_ge(sem["gel"], 32 * (fc + 1))
                    for fo in range(8):
                        idx = 8 * fc + fo
                        if idx >= 2:
                            t.wait_ge(sem["x3e"], idx - 1)
                        for k in range(32):
                            mm = t.matmul(jpv(ps_d, fo),
                                          wtd_r[:, k * DM + fo * 128:
                                                k * DM + (fo + 1) * 128],
                                          h6v(k),
                                          start=(k == 0), stop=(k == 31))
                        mm.then_inc(sem["dnm"], 1)
                    for i in range(2):
                        for g2 in range(2):
                            tpidx = 4 * fc + 2 * i + g2
                            if tpidx >= 2:
                                t.wait_ge(sem["tpe6"], tpidx - 1)
                            for q in range(4):
                                dt = g2 * 4 + q
                                t.wait_ge(sem["x3e"], 8 * fc + dt + 1)
                                tr = t.transpose(
                                    tov(g2)[:, q * 128:(q + 1) * 128],
                                    x36v(dt)[:, i * 128:(i + 1) * 128],
                                    identf[:, :])
                            tr.then_inc(sem["tp6"], 1)

            def s_p6(s):
                for fc in range(NF6):
                    for f in range(32):
                        idx = 32 * fc + f
                        if f == 0 and fc >= 1:
                            s.wait_ge(sem["dnm"], 8 * fc)
                        s.wait_ge(sem["upm"], idx + 1)
                        s.activation(h6v(f), jpv(ps_u, f), AF.Gelu,
                                     bias=0.0, scale=1.0).then_inc(sem["gel"], 1)
                        if f >= 2:
                            s.wait_ge(sem["h2m"], 32 * fc + f - 1)
                        s.activation(j2v(sqh, f), h6v(f), AF.Square,
                                     bias=0.0, scale=1.0).then_inc(sem["hsq"], 1)
                    s.wait_ge(sem["lnv"], 4 * NCH + fc + 1)
                    s.activation(w6d[0:1, :], w6a[0:1, :], AF.Ln, bias=1e-35,
                                 scale=1.0)
                    s.activation(w6c[0:1, :], w6d[0:1, :], AF.Exp, bias=0.0,
                                 scale=0.5).then_inc(sem["lns"], 1)
                    s.wait_ge(sem["opb"], NF5 + fc + 1)
                    if fc >= 1:
                        s.wait_ge(sem["x3e"], 8 * fc)
                    s.copy(s2b6[:, :], ps_b6[:, 0:TC2]).then_inc(sem["sbe"], 1)
                    for i in range(2):
                        for g2 in range(2):
                            tpidx = 4 * fc + 2 * i + g2
                            s.wait_ge(sem["tp6"], tpidx + 1)
                            if fc >= 1:
                                s.wait_ge(sem["ost"][i], 16 * fc)
                            s.copy(ostv(i)[:, g2 * 512:(g2 + 1) * 512],
                                   tov(g2)).then_inc(sem["tpe6"], 1)

            def v_p6(v):
                for fc in range(NF6):
                    v.wait_ge(sem["h2m"], 32 * (fc + 1))
                    v.tensor_copy(w6a[0:1, :],
                                  ps_h2[0:1, 0:TC2]).then_inc(sem["lnv"], 1)
                    v.wait_ge(sem["lns"], 4 * NCH + fc + 1)
                    v.tensor_single_scalar(w6b[0:1, :], w6c[0:1, :],
                                           1.0 / 64.0, ALU.mult)
                    v.tensor_single_scalar(w6b[0:1, :], w6b[0:1, :], 1e-8,
                                           ALU.add)
                    v.reciprocal(sv_s2[0:1, :],
                                 w6b[0:1, :]).then_inc(sem["s2v"], 1)
                    v.wait_ge(sem["sbe"], fc + 1)
                    for fo in range(8):
                        idx = 8 * fc + fo
                        v.wait_ge(sem["dnm"], idx + 1)
                        if fc >= 1 and fo == 0:
                            v.wait_ge(sem["tp6"], 4 * fc)
                        v.tensor_tensor(xtm6[:, :], jpv(ps_d, fo),
                                        s2b6[:, :], ALU.mult)
                        v.tensor_tensor(x36v(fo), xtm6[:, :], x26v(fc, fo),
                                        ALU.add).then_inc(sem["x3e"], 1)

            blk.sync(sy_p6)
            blk.tensor(t_p6)
            blk.scalar(s_p6)
            blk.vector(v_p6)

    return nc


_CACHE = {}


def kernel(L0_attn_hidden_states, L0_attn_in_proj_w, L0_attn_conv_w,
           L0_attn_conv_b, L0_attn_x_proj_w, L0_attn_dt_proj_w,
           L0_attn_dt_proj_b, L0_attn_A_log, L0_attn_D_skip,
           L0_attn_out_proj_w, L0_ffn_w_up, L0_ffn_w_down):
    h = np.asarray(L0_attn_hidden_states, dtype=np.float32)
    ipw = np.asarray(L0_attn_in_proj_w, dtype=np.float32)
    cw = np.asarray(L0_attn_conv_w, dtype=np.float32)
    cb = np.asarray(L0_attn_conv_b, dtype=np.float32)
    xpw = np.asarray(L0_attn_x_proj_w, dtype=np.float32)
    dtw = np.asarray(L0_attn_dt_proj_w, dtype=np.float32)
    dtb = np.asarray(L0_attn_dt_proj_b, dtype=np.float32)
    alog = np.asarray(L0_attn_A_log, dtype=np.float32)
    dsk = np.asarray(L0_attn_D_skip, dtype=np.float32)
    opw = np.asarray(L0_attn_out_proj_w, dtype=np.float32)
    wup = np.asarray(L0_ffn_w_up, dtype=np.float32)
    wdn = np.asarray(L0_ffn_w_down, dtype=np.float32)

    B = h.shape[0]
    eyem = np.eye(128, dtype=np.float32)
    ones = np.ones((128, 128), dtype=np.float32)
    selm = np.zeros((32, 32 * 128), dtype=np.float32)
    for n in range(32):
        selm[n, n * 128:(n + 1) * 128] = 1.0
    wupt = np.ascontiguousarray(wup.T)
    wdnt = np.ascontiguousarray(wdn.T)

    if "nc" not in _CACHE:
        _CACHE["nc"] = build()
    ncp = _CACHE["nc"]

    def dslice(vec, r):
        return np.ascontiguousarray(
            vec[r * DH:(r + 1) * DH].reshape(8, 128).T)

    in_maps = []
    for c in range(8):
        b, rr = c >> 1, c & 1
        in_maps.append({
            "x_seq": np.ascontiguousarray(h[b]),
            "x_own": np.ascontiguousarray(h[b, rr * TOWN:(rr + 1) * TOWN]),
            "w_in": np.ascontiguousarray(np.concatenate(
                [ipw[:, rr * DH:(rr + 1) * DH],
                 ipw[:, DI + rr * DH:DI + (rr + 1) * DH]], axis=1)),
            "w_xp": np.ascontiguousarray(xpw[rr * DH:(rr + 1) * DH, :]),
            "w_dt": np.ascontiguousarray(dtw[:, rr * DH:(rr + 1) * DH]),
            "b_dt": dslice(dtb, rr),
            "a_log": np.ascontiguousarray(
                alog[rr * DH:(rr + 1) * DH].reshape(8, 128, NST)
                .transpose(1, 0, 2).reshape(128, 128)),
            "conv_w4": np.ascontiguousarray(
                cw[rr * DH:(rr + 1) * DH].reshape(8, 128, 4)
                .transpose(1, 0, 2).reshape(128, 32)),
            "conv_b8": dslice(cb, rr),
            "dsk8": dslice(dsk, rr),
            "w_out": opw,
            "w_up_t": wupt,
            "w_dn_t": wdnt,
            "eye": eyem,
            "onesm": ones,
            "selbc": selm,
            "roff": np.array([[2 * rr, 2 * rr + 1]], dtype=np.int32),
            "cvals": np.tile(np.array(
                [[1.0, 1e-5, 1e-35, -1.0, 0.5, -0.5, 0.0, 0.0]],
                dtype=np.float32), (128, 1)),
        })

    res = run_bass_kernel_spmd(ncp, in_maps, core_ids=list(range(8)))
    _CACHE["res"] = res

    out_full = np.empty((B, L, DM), dtype=np.float32)
    for c in range(8):
        b, rr = c >> 1, c & 1
        out_full[b, rr * TOWN:(rr + 1) * TOWN, :] = res.results[c]["out"]
    return out_full


# version 15
# speedup vs baseline: 1.2213x; 1.0021x over previous
import os
import sys
sys.path.insert(0, "/opt/trn_rl_repo")
os.environ.setdefault("TRN_ALLOW_ICI_ALL_TO_ALL", "1")
from contextlib import ExitStack

import numpy as np
import concourse.bass as bass
import concourse.mybir as mybir
from concourse.bass_utils import run_bass_kernel_spmd

F32 = mybir.dt.float32
BF16 = mybir.dt.bfloat16
I32 = mybir.dt.int32
AF = mybir.ActivationFunctionType
ALU = mybir.AluOpType
AXX = mybir.AxisListType.X

L = 4096
DM = 1024
DI = 2048
DH = 1024
NST = 16
DTR = 64
DFF = 4096
TOWN = 2048
TC = 512
NCH = L // TC
NJ = NCH * 8
TC2 = 256
NF5 = TOWN // TC
NF6 = TOWN // TC2
NPC = 64
MAGIC = 0x5F3759DF
RG = [[0, 1], [2, 3], [4, 5], [6, 7]]

DEBUG_OUT = False


def nrsqrt(v, dst, src, ti, t1, t2, iters=3):
    v.tensor_single_scalar(ti, src.bitcast(I32), 1, ALU.logical_shift_right)
    v.tensor_single_scalar(ti, ti, MAGIC, ALU.subtract)
    v.tensor_single_scalar(dst.bitcast(I32), ti, -1, ALU.mult)
    for _ in range(iters):
        v.tensor_tensor(t1, src, dst, ALU.mult)
        v.tensor_tensor(t1, t1, dst, ALU.mult)
        v.tensor_scalar(t2, t1, -0.5, 1.5, ALU.mult, ALU.add)
        v.tensor_tensor(dst, dst, t2, ALU.mult)


def wpiece(w_up_t, w_dn_t, j):
    if j < 32:
        sj, q = divmod(j, 4)
        return w_up_t[sj * 128:(sj + 1) * 128, q * 1024:(q + 1) * 1024]
    js = j - 32
    return w_dn_t[js * 128:(js + 1) * 128, :]


def build():
    nc = bass.Bass("TRN2", target_bir_lowering=False,
                   detect_race_conditions=False)

    x_seq = nc.dram_tensor("x_seq", [L, DM], F32, kind="ExternalInput")
    x_own = nc.dram_tensor("x_own", [TOWN, DM], F32, kind="ExternalInput")
    w_in = nc.dram_tensor("w_in", [DM, 2 * DH], F32, kind="ExternalInput")
    w_xp = nc.dram_tensor("w_xp", [DH, 96], F32, kind="ExternalInput")
    w_dt = nc.dram_tensor("w_dt", [DTR, DH], F32, kind="ExternalInput")
    b_dt = nc.dram_tensor("b_dt", [128, 8], F32, kind="ExternalInput")
    a_log = nc.dram_tensor("a_log", [128, 128], F32, kind="ExternalInput")
    conv_w4 = nc.dram_tensor("conv_w4", [128, 32], F32, kind="ExternalInput")
    conv_b8 = nc.dram_tensor("conv_b8", [128, 8], F32, kind="ExternalInput")
    dsk8 = nc.dram_tensor("dsk8", [128, 8], F32, kind="ExternalInput")
    w_out = nc.dram_tensor("w_out", [DI, DM], F32, kind="ExternalInput")
    w_up_t = nc.dram_tensor("w_up_t", [DM, DFF], F32, kind="ExternalInput")
    w_dn_t = nc.dram_tensor("w_dn_t", [DFF, DM], F32, kind="ExternalInput")
    eye = nc.dram_tensor("eye", [128, 128], F32, kind="ExternalInput")
    onesm = nc.dram_tensor("onesm", [128, 128], F32, kind="ExternalInput")
    selbc = nc.dram_tensor("selbc", [32, 32 * 128], F32, kind="ExternalInput")
    roff = nc.dram_tensor("roff", [1, 2], I32, kind="ExternalInput")
    cvals = nc.dram_tensor("cvals", [128, 8], F32, kind="ExternalInput")

    out = nc.dram_tensor("out", [TOWN, DM], F32, kind="ExternalOutput")

    dk = "ExternalOutput" if DEBUG_OUT else "Internal"
    d_xbld = nc.dram_tensor("d_xbld", [DH, L], BF16, kind=dk)
    d_gate = nc.dram_tensor("d_gate", [DH, L], BF16, kind=dk)
    d_x2 = nc.dram_tensor("d_x2", [DM, TOWN], F32, kind=dk)
    d_xn = nc.dram_tensor("d_xn", [DM, TOWN], BF16, kind=dk)
    d_wtu = nc.dram_tensor("d_wtu", [DM, DFF], BF16, kind=dk)
    d_wtd = nc.dram_tensor("d_wtd", [DFF, DM], BF16, kind=dk)
    d_arin = nc.dram_tensor("d_arin", [96, L], F32)
    d_arout = nc.dram_tensor("d_arout", [96, L], F32)
    d_ystore = nc.dram_tensor("d_ystore", [2, DH, TOWN], BF16)
    d_wob = nc.dram_tensor("d_wob", [DI, DM], BF16)
    d_agy = nc.dram_tensor("d_agy", [4, DH, TOWN], BF16)
    if DEBUG_OUT:
        d_dbgy = nc.dram_tensor("d_dbgy", [2, DH, TOWN], BF16,
                                kind="ExternalOutput")
        d_dbga = nc.dram_tensor("d_dbga", [4, DH, TOWN], BF16,
                                kind="ExternalOutput")

    with ExitStack() as eg:
        SBg = lambda n, s, d: eg.enter_context(nc.sbuf_tensor(n, s, d))
        blk = eg.enter_context(nc.Block())
        sem = {}
        for n in ["lc", "sq1", "vln", "tp", "xlf", "ipm", "ipe",
                  "cv", "sil", "xpm", "xpe",
                  "tnA", "tnV", "arv", "arl", "p2v", "p2s", "p2t", "lnv", "lns",
                  "bcm", "bce", "mmd", "spd", "dlv", "dAc", "scn", "cmv",
                  "gat", "a2a", "wol",
                  "opm", "tp5", "tpe5", "x2e", "sq5", "st5",
                  "vst5", "ast5", "nb5", "opb", "sse", "xne",
                  "wtl", "upm", "gel", "hsq", "h2m", "s2v", "sbe",
                  "dnm", "x3e", "tp6", "tpe6"]:
            sem[n] = eg.enter_context(nc.semaphore("c_" + n))
        for n in ["xt", "slab", "ug", "ytl", "x5", "xnl", "xbst", "gtst",
                  "xds", "ygst", "x2st", "xnst", "ost", "wtst"]:
            sem[n] = [eg.enter_context(nc.semaphore("c_" + n + str(i)))
                      for i in range(2)]

        identf = SBg("identf", [128, 128], F32)
        identb = SBg("identb", [128, 128], BF16)
        ones_f = SBg("ones_f", [128, 128], F32)
        ones_b = SBg("ones_b", [128, 128], BF16)
        bdt_s = SBg("bdt_s", [128, 8], F32)
        b2_s = SBg("b2_s", [128, 8], F32)
        alog_s = SBg("alog_s", [128, 128], F32)
        aneg_s = SBg("aneg_s", [128, 128], F32)
        cw_s = SBg("cw_s", [128, 32], F32)
        cb_s = SBg("cb_s", [128, 8], F32)
        dsk_s = SBg("dsk_s", [128, 8], F32)
        hcarry = SBg("hcarry", [128, 128], F32)
        asums = SBg("asums", [128, NPC], F32)
        tau_b = SBg("tau_b", [128, 2], F32)
        tauv = SBg("tauv", [128, 8], F32)
        cval_s = SBg("cval_s", [128, 8], F32)
        for _i, _v in enumerate([1.0, 1e-5, 1e-35, -1.0, 0.5, -0.5]):
            nc.const_aps.register(F32, _v, cval_s[:, _i:_i + 1])

        with ExitStack() as e13:
            SB13 = lambda n, s, d: e13.enter_context(nc.sbuf_tensor(n, s, d))
            slab = SB13("slab", [128, 2048], F32)
            tscr = SB13("tscr", [128, 1024], F32)
            sgn_s = SB13("sgn_s", [128, 1024], BF16)
            msk_s = SB13("msk_s", [128, 1024], BF16)
            wt_s = SB13("wt_s", [128, 2048], BF16)
            w_dt_b = SB13("w_dt_b", [64, 1024], BF16)
            sel_b = SB13("sel_b", [32, 32 * 128], BF16)
            ps_tau = e13.enter_context(nc.psum_tensor("ps_tau", [128, 512], F32))

            def slabv(j):
                return slab[:, (j % 2) * 1024:(j % 2) * 1024 + 1024]

            def wtv(j):
                return wt_s[:, (j % 2) * 1024:(j % 2) * 1024 + 1024]

            if True:
                e1 = ExitStack()
                SB1 = lambda n, s, d: e1.enter_context(nc.sbuf_tensor(n, s, d))
                PS1 = lambda n, s, d: e1.enter_context(nc.psum_tensor(n, s, d))
                w_in_b = SB1("w_in_b", [128, 8 * 2048], BF16)
                w_xp_b = SB1("w_xp_b", [128, 8 * 96], BF16)
                xt_s = SB1("xt_s", [128, 8 * DM], F32)
                sq_s = SB1("sq_s", [128, DM], F32)
                s1_s = SB1("s1_s", [128, 8], F32)
                ssq_s = SB1("ssq_s", [128, 8], F32)
                mva = SB1("mva", [128, 16], F32)
                xln_tm = SB1("xln_tm", [128, 4 * DM], BF16)
                xln_fm = SB1("xln_fm", [128, 8 * TC], BF16)
                xc_buf = SB1("xc_buf", [128, 8 * (TC + 6)], F32)
                cacc = SB1("cacc", [128, 2 * TC], F32)
                xbld_c = SB1("xbld_c", [128, 16 * TC], BF16)
                gate_c = SB1("gate_c", [128, 8 * TC], BF16)
                xdp = SB1("xdp", [96, 2 * TC], F32)

                ps_tr = PS1("ps_tr", [128, 4 * TC], BF16)
                ps_ip = PS1("ps_ip", [128, 2 * TC], F32)
                ps_xd = PS1("ps_xd", [128, TC], F32)

                def xtv(a, i):
                    o = ((a % 2) * 4 + i) * DM
                    return xt_s[:, o:o + DM]

                def xlntv(i):
                    return xln_tm[:, i * DM:(i + 1) * DM]

                def xlnfv(k):
                    return xln_fm[:, k * TC:(k + 1) * TC]

                def xcv(dt):
                    return xc_buf[:, dt * (TC + 6):(dt + 1) * (TC + 6)]

                def caccv(dt):
                    return cacc[:, (dt % 2) * TC:(dt % 2) * TC + TC]

                def xbldv(a, k):
                    o = ((a % 2) * 8 + k) * TC
                    return xbld_c[:, o:o + TC]

                def gatev(dt):
                    return gate_c[:, dt * TC:(dt + 1) * TC]

                def ipsv(ps, i):
                    return ps[:, (i % 2) * TC:(i % 2) * TC + TC]

                def trv(i):
                    return ps_tr[:, (i % 2) * 2 * TC:(i % 2) * 2 * TC + TC]


                def sy_p1(sy):
                    def stores(a):
                        for dt in range(8):
                            sy.wait_ge(sem["sil"], 8 * a + dt + 1)
                            sy.dma_start(
                                d_xbld[dt * 128:(dt + 1) * 128,
                                       a * TC:(a + 1) * TC],
                                xbldv(a, dt)).then_inc(sem["xbst"][a % 2], 16)
                            sy.wait_ge(sem["ipe"], 16 * a + 8 + dt + 1)
                            sy.dma_start(
                                d_gate[dt * 128:(dt + 1) * 128,
                                       a * TC:(a + 1) * TC],
                                gatev(dt)).then_inc(sem["gtst"][a % 2], 16)
                        sy.wait_ge(sem["xpe"], a + 1)
                        sy.dma_start(
                            d_arin[:, a * TC:(a + 1) * TC],
                            xdp[:, (a % 2) * TC:(a % 2) * TC + TC]
                        ).then_inc(sem["xds"][a % 2], 16)
                    for a in range(NCH):
                        if a >= 2:
                            sy.wait_ge(sem["vln"], 4 * (a - 2) + 4)
                            sy.wait_ge(sem["sq1"], 4 * (a - 2) + 4)
                        for i in range(4):
                            sy.dma_start(
                                xtv(a, i),
                                x_seq[a * TC + i * 128:a * TC + (i + 1) * 128,
                                      :]).then_inc(sem["xt"][a % 2], 16)
                        if a >= 1:
                            stores(a - 1)
                    stores(NCH - 1)

                def g_p1(g):
                    for dst, src in [(identf, eye), (identb, eye),
                                     (ones_f, onesm), (ones_b, onesm),
                                     (bdt_s, b_dt), (alog_s, a_log),
                                     (cw_s, conv_w4), (cb_s, conv_b8),
                                     (dsk_s, dsk8)]:
                        g.dma_start(dst[:, :], src[:, :]).then_inc(sem["lc"], 16)
                    for k in range(8):
                        g.dma_start(w_in_b[:, k * 2048:(k + 1) * 2048],
                                    w_in[k * 128:(k + 1) * 128, :]
                                    ).then_inc(sem["lc"], 16)
                    for k in range(8):
                        g.dma_start(w_xp_b[:, k * 96:(k + 1) * 96],
                                    w_xp[k * 128:(k + 1) * 128, :]
                                    ).then_inc(sem["lc"], 16)
                    g.dma_start(w_dt_b[:, :], w_dt[:, :]).then_inc(sem["lc"], 16)
                    g.dma_start(sel_b[:, :], selbc[:, :]).then_inc(sem["lc"], 16)
                    g.dma_start(cval_s[:, :], cvals[:, :]).then_inc(sem["lc"], 16)
                    g.dma_start(d_wob.ap(),
                                w_out.ap()).then_inc(sem["wol"], 16)
                    for j in range(NPC):
                        if j >= 2:
                            g.wait_ge(sem["tnA"], j - 1)
                        g.dma_start(slabv(j),
                                    wpiece(w_up_t, w_dn_t, j)
                                    ).then_inc(sem["slab"][j % 2], 16)

                LC_N = (9 + 8 + 8 + 1 + 1 + 1) * 16

                def s_p1(s):
                    wsl = [0]

                    def do_wslab():
                        j = wsl[0]
                        if j < NPC:
                            s.wait_ge(sem["slab"][j % 2],
                                      16 * (j // 2 + 1))
                            s.activation(tscr[:, :], slabv(j), AF.Abs,
                                         bias=0.0, scale=1.0,
                                         accum_out=asums[:, j:j + 1]
                                         ).then_inc(sem["tnA"], 1)
                            wsl[0] += 1

                    for a in range(NCH):
                        for i in range(4):
                            s.wait_ge(sem["xt"][a % 2],
                                      64 * (a // 2 + 1))
                            s.activation(sq_s[:, :], xtv(a, i), AF.Square,
                                         bias=0.0, scale=1.0,
                                         accum_out=ssq_s[:, i:i + 1]
                                         ).then_inc(sem["sq1"], 1)
                        for i in range(4):
                            s.wait_ge(sem["lnv"], 4 * a + i + 1)
                            var = mva[:, 4 * i + 1:4 * i + 2]
                            rstd = mva[:, 4 * i + 2:4 * i + 3]
                            t1 = mva[:, 4 * i + 3:4 * i + 4]
                            s.activation(t1, var, AF.Ln, bias=0.0, scale=1.0)
                            s.activation(rstd, t1, AF.Exp, bias=0.0,
                                         scale=-0.5).then_inc(sem["lns"], 1)
                        for dt in range(8):
                            s.wait_ge(sem["tp"], 8 * a + dt + 1)
                            if a >= 1:
                                s.wait_ge(sem["ipm"], 16 * (a - 1) + 16)
                            s.copy(xlnfv(dt),
                                   trv(dt)).then_inc(sem["xlf"], 1)
                        for f in range(16):
                            s.wait_ge(sem["ipm"], 16 * a + f + 1)
                            if f < 8:
                                if a >= 1:
                                    s.wait_ge(sem["cv"],
                                              2 * (8 * (a - 1) + f) + 2)
                                s.copy(xcv(f)[:, 6:TC + 6],
                                       ipsv(ps_ip, f)).then_inc(sem["ipe"], 1)
                            else:
                                dt = f - 8
                                if a >= 1:
                                    s.wait_ge(sem["gtst"][(a - 1) % 2],
                                              128 * ((a - 1) // 2 + 1))
                                s.activation(gatev(dt), ipsv(ps_ip, f),
                                             AF.Silu, bias=0.0, scale=1.0
                                             ).then_inc(sem["ipe"], 1)
                        for dt in range(8):
                            s.wait_ge(sem["cv"], 2 * (8 * a + dt) + 1)
                            if a >= 2:
                                s.wait_ge(sem["xbst"][a % 2],
                                          128 * ((a - 2) // 2 + 1))
                                s.wait_ge(sem["xpm"], a - 1)
                            s.activation(xbldv(a, dt), caccv(dt), AF.Silu,
                                         bias=cb_s[:, dt:dt + 1], scale=1.0
                                         ).then_inc(sem["sil"], 1)
                        for _ in range(8):
                            do_wslab()
                    while wsl[0] < NPC:
                        do_wslab()

                def v_p1(v):
                    for dt in range(8):
                        v.memset(xcv(dt)[:, 0:6], 0.0)
                    for a in range(NCH):
                        for i in range(4):
                            v.wait_ge(sem["xt"][a % 2],
                                      64 * (a // 2 + 1))
                            v.reduce_sum(s1_s[:, i:i + 1], xtv(a, i), axis=AXX)
                        for i in range(4):
                            v.wait_ge(sem["sq1"], 4 * a + i + 1)
                            m = mva[:, 4 * i:4 * i + 1]
                            var = mva[:, 4 * i + 1:4 * i + 2]
                            t1 = mva[:, 4 * i + 3:4 * i + 4]
                            v.tensor_single_scalar(m, s1_s[:, i:i + 1],
                                                   1.0 / DM, ALU.mult)
                            v.tensor_single_scalar(var, ssq_s[:, i:i + 1],
                                                   1.0 / DM, ALU.mult)
                            v.tensor_scalar(t1, m, m[:, 0:1], None, ALU.mult)
                            v.tensor_tensor(var, var, t1, ALU.subtract)
                            v.tensor_single_scalar(var, var, 1e-5,
                                                   ALU.add).then_inc(sem["lnv"], 1)
                        for i in range(4):
                            v.wait_ge(sem["lns"], 4 * a + i + 1)
                            if a >= 1:
                                v.wait_ge(sem["tp"], 8 * (a - 1) + 8)
                            m = mva[:, 4 * i:4 * i + 1]
                            rstd = mva[:, 4 * i + 2:4 * i + 3]
                            v.tensor_scalar(xlntv(i), xtv(a, i), m[:, 0:1],
                                            rstd[:, 0:1], ALU.subtract,
                                            ALU.mult).then_inc(sem["vln"], 1)
                        for dt in range(8):
                            v.wait_ge(sem["ipe"], 16 * a + dt + 1)
                            if a >= 1 or dt >= 2:
                                v.wait_ge(sem["sil"], 8 * a + dt - 1)
                            acc = caccv(dt)
                            xc = xcv(dt)
                            v.tensor_scalar(acc, xc[:, 0:TC],
                                            cw_s[:, 4 * dt:4 * dt + 1], None,
                                            ALU.mult)
                            for k in range(1, 4):
                                ins = v.scalar_tensor_tensor(
                                    acc, xc[:, k:k + TC],
                                    cw_s[:, 4 * dt + k:4 * dt + k + 1],
                                    acc, ALU.mult, ALU.add)
                            ins.then_inc(sem["cv"], 1)
                            v.tensor_copy(xc[:, 0:6],
                                          xc[:, TC:TC + 6]).then_inc(sem["cv"], 1)
                        if a >= 1:
                            v.wait_ge(sem["xds"][a % 2],
                                      16 * ((a - 2) // 2 + 1))
                        v.wait_ge(sem["xpm"], a + 1)
                        v.tensor_copy(xdp[:, (a % 2) * TC:(a % 2) * TC + TC],
                                      ps_xd[0:96, :]).then_inc(sem["xpe"], 1)

                def t_p1(t):
                    t.wait_ge(sem["lc"], LC_N)
                    for a in range(NCH):
                        for dt in range(8):
                            idx = 8 * a + dt
                            t.wait_ge(sem["vln"], 4 * a + 4)
                            if idx >= 2:
                                t.wait_ge(sem["xlf"], idx - 1)
                            for i in range(4):
                                tr = t.transpose(
                                    trv(dt)[:, i * 128:(i + 1) * 128],
                                    xlntv(i)[:, dt * 128:(dt + 1) * 128],
                                    identb[:, :])
                            tr.then_inc(sem["tp"], 1)
                        for f in range(16):
                            idx = 16 * a + f
                            if f == 0:
                                t.wait_ge(sem["xlf"], 8 * a + 8)
                            if idx >= 2:
                                t.wait_ge(sem["ipe"], idx - 1)
                            for k in range(8):
                                mm = t.matmul(
                                    ipsv(ps_ip, f),
                                    w_in_b[:, k * 2048 + f * 128:
                                           k * 2048 + (f + 1) * 128],
                                    xlnfv(k),
                                    start=(k == 0), stop=(k == 7))
                            mm.then_inc(sem["ipm"], 1)
                        if a >= 1:
                            t.wait_ge(sem["xpe"], a)
                        for k in range(8):
                            t.wait_ge(sem["sil"], 8 * a + k + 1)
                            mm = t.matmul(ps_xd[0:96, :],
                                          w_xp_b[:, k * 96:(k + 1) * 96],
                                          xbldv(a, k),
                                          start=(k == 0), stop=(k == 7))
                        mm.then_inc(sem["xpm"], 1)

                blk.sync(sy_p1)
                blk.gpsimd(g_p1)
                blk.scalar(s_p1)
                blk.vector(v_p1)
                blk.tensor(t_p1)
                e1.close()

                with ExitStack() as e23:
                    SB23 = lambda n, s, d: e23.enter_context(
                        nc.sbuf_tensor(n, s, d))
                    xdbl = SB23("xdbl", [96, L], F32)
                    dtl_b = SB23("dtl_b", [64, L], BF16)
                    bc_b = SB23("bc_b", [32, L], BF16)

                    def g_p2(g):
                        g.wait_ge(sem["xds"][0], 16 * (NCH // 2))
                        g.wait_ge(sem["xds"][1], 16 * (NCH // 2))
                        g.collective_compute(
                            "AllReduce", ALU.add, replica_groups=RG,
                            ins=[d_arin.ap().opt()],
                            outs=[d_arout.ap().opt()],
                        ).then_inc(sem["arv"], 1)
                        g.wait_ge(sem["arv"], 1)
                        g.dma_start(xdbl[:, :],
                                    d_arout[:, :]).then_inc(sem["arl"], 16)

                    def s_p2(s):
                        s.wait_ge(sem["lc"], LC_N)
                        s.activation(aneg_s[:, :], alog_s[:, :], AF.Exp,
                                     bias=0.0, scale=1.0).then_inc(sem["p2s"], 1)

                    def v_p2(v):
                        v.wait_ge(sem["arl"], 16)
                        v.tensor_copy(dtl_b[:, :], xdbl[0:64, :])
                        v.tensor_copy(bc_b[:, :], xdbl[64:96, :])
                        v.tensor_tensor(b2_s[:, :], bdt_s[:, :], bdt_s[:, :],
                                        ALU.add)
                        v.memset(hcarry[:, :], 0.0).then_inc(sem["p2v"], 1)
                        v.wait_ge(sem["p2s"], 1)
                        v.tensor_single_scalar(aneg_s[:, :], aneg_s[:, :],
                                               -1.0,
                                               ALU.mult).then_inc(sem["p2v"], 1)
                        v.wait_ge(sem["tnA"], NPC)
                        v.reduce_sum(tauv[:, 0:1], asums[:, 0:32], axis=AXX)
                        v.reduce_sum(tauv[:, 1:2], asums[:, 32:NPC],
                                     axis=AXX).then_inc(sem["p2v"], 1)

                    def t_p2(t):
                        t.wait_ge(sem["p2v"], 3)
                        t.matmul(ps_tau[0:1, 0:1], ones_f[:, 0:1],
                                 tauv[:, 0:1], start=True, stop=True)
                        t.matmul(ps_tau[0:1, 1:2], ones_f[:, 0:1],
                                 tauv[:, 1:2], start=True,
                                 stop=True).then_inc(sem["p2t"], 1)

                    def v_p2b(v):
                        v.wait_ge(sem["p2t"], 1)
                        v.tensor_single_scalar(tauv[0:1, 2:3],
                                               ps_tau[0:1, 0:1],
                                               0.5 / (DM * DFF), ALU.mult)
                        v.tensor_single_scalar(tauv[0:1, 3:4],
                                               ps_tau[0:1, 1:2],
                                               0.5 / (DM * DFF),
                                               ALU.mult).then_inc(sem["p2v"], 1)

                    def t_p2b(t):
                        t.wait_ge(sem["p2v"], 4)
                        t.matmul(ps_tau[:, 2:3], ones_f[0:1, :],
                                 tauv[0:1, 2:3], start=True, stop=True)
                        t.matmul(ps_tau[:, 3:4], ones_f[0:1, :],
                                 tauv[0:1, 3:4], start=True,
                                 stop=True).then_inc(sem["p2t"], 1)

                    def s_p2b(s):
                        s.wait_ge(sem["p2t"], 2)
                        s.copy(tau_b[:, 0:2],
                               ps_tau[:, 2:4]).then_inc(sem["p2s"], 1)

                    blk.gpsimd(g_p2)
                    blk.scalar(s_p2)
                    blk.vector(v_p2)
                    blk.tensor(t_p2)
                    blk.vector(v_p2b)
                    blk.tensor(t_p2b)
                    blk.scalar(s_p2b)

                    with ExitStack() as e3:
                        SB3 = lambda n, s, d: e3.enter_context(
                            nc.sbuf_tensor(n, s, d))
                        PS3 = lambda n, s, d: e3.enter_context(
                            nc.psum_tensor(n, s, d))
                        Bb = SB3("Bb", [128, 2 * NST * TC], BF16)
                        Cb = SB3("Cb", [128, 2 * NST * TC], BF16)
                        spA = SB3("spA", [128, 2 * TC], F32)
                        spB = SB3("spB", [128, 2 * TC], F32)
                        dlt_s = SB3("dlt_s", [128, 2 * TC], BF16)
                        du_s = SB3("du_s", [128, 2 * TC], BF16)
                        u_s = SB3("u_s", [128, 2 * TC], BF16)
                        gt_s = SB3("gt_s", [128, 2 * TC], BF16)
                        dA_s = SB3("dA_s", [128, 4 * TC], F32)
                        bta_s = SB3("bta_s", [128, TC], BF16)
                        h_s = SB3("h_s", [128, NST * TC], BF16)
                        m_s = SB3("m_s", [128, NST * TC], BF16)
                        yg_s = SB3("yg_s", [128, 2 * TC], BF16)
                        y1_s = SB3("y1_s", [128, TC], F32)
                        ps_bc = PS3("ps_bc", [128, 2 * TC], F32)
                        ps_dl = PS3("ps_dl", [128, 2 * TC], F32)

                        def b2v(t_, j):
                            return t_[:, (j % 2) * TC:(j % 2) * TC + TC]

                        def bbv(t_, tc):
                            o = (tc % 2) * NST * TC
                            return t_[:, o:o + NST * TC]

                        def t_p3(t):
                            t.wait_ge(sem["p2v"], 1)
                            t.wait_ge(sem["p2s"], 2)
                            for tc in range(NCH):
                                for n in range(32):
                                    i = 32 * tc + n
                                    if i >= 2:
                                        t.wait_ge(sem["bce"], i - 1)
                                    row = (n // 2) + (16 if (n % 2) else 0)
                                    t.matmul(b2v(ps_bc, i),
                                             sel_b[:, row * 128:(row + 1) * 128],
                                             bc_b[:, tc * TC:(tc + 1) * TC],
                                             start=True,
                                             stop=True).then_inc(sem["bcm"], 1)
                                for dt in range(8):
                                    j = tc * 8 + dt
                                    if j >= 2:
                                        t.wait_ge(sem["spd"], j - 1)
                                    t.matmul(b2v(ps_dl, j),
                                             w_dt_b[:, dt * 128:(dt + 1) * 128],
                                             dtl_b[:, tc * TC:(tc + 1) * TC],
                                             start=True,
                                             stop=True).then_inc(sem["mmd"], 1)

                        def s_p3(s):
                            tern = [0]

                            def do_tern():
                                j = tern[0]
                                if j >= NPC:
                                    return
                                s.wait_ge(sem["slab"][j % 2],
                                          16 * (NPC // 2 + j // 2 + 1))
                                if j >= 1:
                                    s.wait_ge(sem["tnV"], j)
                                s.activation(tscr[:, :], slabv(j), AF.Abs,
                                             bias=0.0, scale=1.0)
                                s.activation(sgn_s[:, :], slabv(j), AF.Sign,
                                             bias=0.0,
                                             scale=1.0).then_inc(sem["tnA"], 1)
                                tern[0] += 1

                            s.wait_ge(sem["p2v"], 2)
                            for tc in range(NCH):
                                if tc >= 2:
                                    s.wait_ge(sem["cmv"], 8 * (tc - 1))
                                for n in range(32):
                                    i = 32 * tc + n
                                    s.wait_ge(sem["bcm"], i + 1)
                                    row = n // 2
                                    dstb = Bb if (n % 2) == 0 else Cb
                                    s.copy(bbv(dstb, tc)[:, row * TC:
                                                         (row + 1) * TC],
                                           b2v(ps_bc, i)).then_inc(sem["bce"], 1)
                                for dt in range(8):
                                    j = tc * 8 + dt
                                    s.wait_ge(sem["mmd"], j + 1)
                                    if j >= 2:
                                        s.wait_ge(sem["dlv"], j - 1)
                                    b2c = b2_s[:, dt:dt + 1]
                                    s.activation(b2v(spA, j), b2v(ps_dl, j),
                                                 AF.Abs, bias=b2c, scale=1.0)
                                    s.activation(b2v(spB, j), b2v(spA, j),
                                                 AF.Exp, bias=0.0, scale=-1.0)
                                    s.activation(b2v(spA, j), b2v(spB, j),
                                                 AF.Ln, bias=1.0, scale=1.0)
                                    s.activation(b2v(spB, j), b2v(ps_dl, j),
                                                 AF.Relu, bias=b2c,
                                                 scale=1.0).then_inc(sem["spd"], 1)
                                    for n in range(NST):
                                        idx = 16 * j + n
                                        s.wait_ge(sem["dlv"], j + 1)
                                        if idx >= 4:
                                            s.wait_ge(sem["scn"], idx - 3)
                                        col = dt * 16 + n
                                        s.activation(
                                            dA_s[:, (idx % 4) * TC:
                                                 (idx % 4) * TC + TC],
                                            b2v(dlt_s, j), AF.Exp, bias=0.0,
                                            scale=aneg_s[:, col:col + 1]
                                        ).then_inc(sem["dAc"], 1)
                                    do_tern()
                            while tern[0] < NPC:
                                do_tern()

                        def g_p3(g):
                            g.wait_ge(sem["p2s"], 2)
                            for j in range(NPC):
                                if j >= 2:
                                    g.wait_ge(sem["tnV"], j - 1)
                                g.dma_start(slabv(j),
                                            wpiece(w_up_t, w_dn_t, j)
                                            ).then_inc(sem["slab"][j % 2], 16)
                                if j >= 1:
                                    wj = j - 1
                                    g.wait_ge(sem["tnV"], wj + 1)
                                    if wj < 32:
                                        sj, q = divmod(wj, 4)
                                        wdst = d_wtu[sj * 128:(sj + 1) * 128,
                                                     q * 1024:(q + 1) * 1024]
                                    else:
                                        js = wj - 32
                                        wdst = d_wtd[js * 128:(js + 1) * 128, :]
                                    g.dma_start(wdst,
                                                wtv(wj)).then_inc(sem["wtst"][wj % 2], 16)
                            wj = NPC - 1
                            g.wait_ge(sem["tnV"], wj + 1)
                            g.dma_start(d_wtd[(wj - 32) * 128:
                                              (wj - 31) * 128, :],
                                        wtv(wj)).then_inc(sem["wtst"][wj % 2], 16)

                        def sy_p3(sy):
                            def ystore(pj):
                                ptc, pdt = divmod(pj, 8)
                                sy.wait_ge(sem["gat"], pj + 1)
                                chh = NCH // 2
                                dst = d_ystore[ptc // chh][
                                    pdt * 128:(pdt + 1) * 128,
                                    (ptc % chh) * TC:(ptc % chh + 1) * TC]
                                sy.dma_start(dst,
                                             b2v(yg_s, pj)).then_inc(
                                                 sem["ygst"][pj % 2], 16)
                            for j in range(NJ):
                                tc, dt = divmod(j, 8)
                                if j >= 2:
                                    sy.wait_ge(sem["gat"], j - 1)
                                sy.wait_ge(sem["xbst"][tc % 2],
                                           128 * (tc // 2 + 1))
                                sy.dma_start(b2v(u_s, j),
                                             d_xbld[dt * 128:(dt + 1) * 128,
                                                    tc * TC:(tc + 1) * TC]
                                             ).then_inc(sem["ug"][j % 2], 16)
                                sy.wait_ge(sem["gtst"][tc % 2],
                                           128 * (tc // 2 + 1))
                                sy.dma_start(b2v(gt_s, j),
                                             d_gate[dt * 128:(dt + 1) * 128,
                                                    tc * TC:(tc + 1) * TC]
                                             ).then_inc(sem["ug"][j % 2], 16)
                                if j >= 1:
                                    ystore(j - 1)
                            ystore(NJ - 1)

                        def v_p3(v):
                            tern = [0]

                            def do_tern():
                                j = tern[0]
                                if j >= NPC:
                                    return
                                v.wait_ge(sem["tnA"], NPC + j + 1)
                                if j >= 2:
                                    v.wait_ge(sem["wtst"][j % 2],
                                              16 * ((j - 2) // 2 + 1))
                                tcol = (tau_b[:, 0:1] if j < 32
                                        else tau_b[:, 1:2])
                                v.tensor_scalar(msk_s[:, :], tscr[:, :],
                                                tcol, None, ALU.is_gt)
                                v.tensor_tensor(wtv(j), msk_s[:, :],
                                                sgn_s[:, :],
                                                ALU.mult).then_inc(sem["tnV"], 1)
                                tern[0] += 1

                            for j in range(NJ):
                                tc, dt = divmod(j, 8)
                                v.wait_ge(sem["spd"], j + 1)
                                if j >= 2:
                                    v.wait_ge(sem["dAc"], 16 * (j - 2) + 16)
                                v.tensor_tensor(b2v(dlt_s, j), b2v(spB, j),
                                                b2v(spA, j),
                                                ALU.add).then_inc(sem["dlv"], 1)
                                v.wait_ge(sem["ug"][j % 2],
                                          32 * (j // 2 + 1))
                                v.tensor_tensor(b2v(du_s, j), b2v(dlt_s, j),
                                                b2v(u_s, j), ALU.mult)
                                v.wait_ge(sem["bce"], 32 * (tc + 1))
                                for n in range(NST):
                                    idx = 16 * j + n
                                    v.wait_ge(sem["dAc"], idx + 1)
                                    v.tensor_tensor(
                                        bta_s[:, :], b2v(du_s, j),
                                        bbv(Bb, tc)[:, n * TC:(n + 1) * TC],
                                        ALU.mult)
                                    v.tensor_tensor_scan(
                                        h_s[:, n * TC:(n + 1) * TC],
                                        dA_s[:, (idx % 4) * TC:
                                             (idx % 4) * TC + TC],
                                        bta_s[:, :],
                                        hcarry[:, dt * 16 + n:
                                               dt * 16 + n + 1],
                                        ALU.mult, ALU.add).then_inc(sem["scn"], 1)
                                v.wait_ge(sem["scn"], 16 * (j + 1))
                                v.tensor_copy(
                                    hcarry[:, dt * 16:(dt + 1) * 16],
                                    bass.AP(h_s, TC - 1,
                                            [[NST * TC, 128], [TC, NST]]))
                                v.tensor_tensor(m_s[:, :], h_s[:, :],
                                                bbv(Cb, tc), ALU.mult)
                                v.tensor_tensor(m_s[:, 0:8 * TC],
                                                m_s[:, 0:8 * TC],
                                                m_s[:, 8 * TC:16 * TC], ALU.add)
                                v.tensor_tensor(m_s[:, 0:4 * TC],
                                                m_s[:, 0:4 * TC],
                                                m_s[:, 4 * TC:8 * TC], ALU.add)
                                v.tensor_tensor(m_s[:, 0:2 * TC],
                                                m_s[:, 0:2 * TC],
                                                m_s[:, 2 * TC:4 * TC], ALU.add)
                                v.tensor_tensor(m_s[:, 0:TC], m_s[:, 0:TC],
                                                m_s[:, TC:2 * TC],
                                                ALU.add).then_inc(sem["cmv"], 1)
                                if j >= 2:
                                    v.wait_ge(sem["ygst"][j % 2],
                                              16 * ((j - 2) // 2 + 1))
                                v.tensor_scalar(y1_s[:, :], b2v(u_s, j),
                                                dsk_s[:, dt:dt + 1], None,
                                                ALU.mult)
                                v.tensor_tensor(y1_s[:, :], y1_s[:, :],
                                                m_s[:, 0:TC], ALU.add)
                                v.tensor_tensor(b2v(yg_s, j), y1_s[:, :],
                                                b2v(gt_s, j),
                                                ALU.mult).then_inc(sem["gat"], 1)
                                do_tern()

                            while tern[0] < NPC:
                                do_tern()

                        blk.tensor(t_p3)
                        blk.scalar(s_p3)
                        blk.gpsimd(g_p3)
                        blk.sync(sy_p3)
                        blk.vector(v_p3)

        with ExitStack() as e5:
            SB5 = lambda n, s, d: e5.enter_context(nc.sbuf_tensor(n, s, d))
            PS5 = lambda n, s, d: e5.enter_context(nc.psum_tensor(n, s, d))
            wob = SB5("wob", [128, 16 * DM], BF16)

            def g_p45(g):
                g.wait_ge(sem["ygst"][0], 16 * (NJ // 4))
                g.wait_ge(sem["ygst"][1], 16 * (NJ // 4))
                g.collective_compute(
                    "AllGather", ALU.bypass, replica_groups=RG,
                    ins=[d_ystore[0][:, :].opt()],
                    outs=[bass.AP(d_agy, 0,
                                  [[TOWN, 2 * DH], [1, TOWN]]).opt()],
                ).then_inc(sem["a2a"], 1)
                g.wait_ge(sem["ygst"][0], 16 * (NJ // 2))
                g.wait_ge(sem["ygst"][1], 16 * (NJ // 2))
                if DEBUG_OUT:
                    g.dma_start(d_dbgy.ap(), d_ystore.ap()).then_inc(
                        sem["lc"], 16)
                    g.wait_ge(sem["lc"], LC_N + 16)
                g.collective_compute(
                    "AllGather", ALU.bypass, replica_groups=RG,
                    ins=[d_ystore[1][:, :].opt()],
                    outs=[bass.AP(d_agy, 2 * DH * TOWN,
                                  [[TOWN, 2 * DH], [1, TOWN]]).opt()],
                ).then_inc(sem["a2a"], 1)
                g.wait_ge(sem["wtst"][0], 16 * (NPC // 2))
                g.wait_ge(sem["wtst"][1], 16 * (NPC // 2))
                with g.register("rsec0") as r0, g.register("rsec1") as r1:
                    g.reg_load(r0, roff[0:1, 0:1])
                    g.reg_load(r1, roff[0:1, 1:2])
                    sec0 = g.snap(r0)
                    sec1 = g.snap(r1)
                    g.wait_ge(sem["a2a"], 2)
                    for fc in range(NF5):
                        if fc >= 2:
                            g.wait_ge(sem["opm"], 8 * (fc - 1))
                        for half, sec in ((0, sec0), (1, sec1)):
                            srcap = bass.AP(
                                d_agy,
                                sec * (DH * TOWN) + fc * TC,
                                [[TOWN, 128], [128 * TOWN, 8], [1, TC]])
                            dstap = bass.AP(
                                ytl,
                                ((fc % 2) * 16 + half * 8) * TC,
                                [[32 * TC, 128], [TC, 8], [1, TC]])
                            g.dma_start(dstap,
                                        srcap).then_inc(sem["ytl"][fc % 2], 16)

            ytl = SB5("ytl", [128, 32 * TC], BF16)
            xt5 = SB5("xt5", [128, 8 * DM], F32)
            xres = SB5("xres", [128, 2 * TC], F32)
            x2c = SB5("x2c", [128, 16 * TC], F32)
            sqt = SB5("sqt", [128, 2 * TC], F32)
            vm = SB5("vm", [1, TC], F32)
            vv = SB5("vv", [1, TC], F32)
            vq = SB5("vq", [1, TC], F32)
            vt1 = SB5("vt1", [1, TC], F32)
            vt2 = SB5("vt2", [1, TC], F32)
            vrstd = SB5("vrstd", [1, TC], F32)
            vrms2 = SB5("vrms2", [1, TC], F32)
            sv_s = SB5("sv_s", [1, TC], F32)
            sv_ms = SB5("sv_ms", [1, TC], F32)
            sb5 = SB5("sb5", [128, 2 * TC], F32)
            xnc = SB5("xnc", [128, 2 * TC], BF16)
            xtm5 = SB5("xtm5", [128, TC], F32)
            ps_o = PS5("ps_o", [128, 2 * TC], F32)
            ps_t5 = PS5("ps_t5", [128, 2 * TC], F32)
            ps_s15 = PS5("ps_s15", [1, TC], F32)
            ps_s25 = PS5("ps_s25", [1, TC], F32)
            ps_b5 = PS5("ps_b5", [128, 2 * TC], F32)

            def ytlv(fc, k):
                o = ((fc % 2) * 16 + k) * TC
                return ytl[:, o:o + TC]

            def xt5v(fc, i):
                o = ((fc % 2) * 4 + i) * DM
                return xt5[:, o:o + DM]

            def x2cv(fc, f):
                o = ((fc % 2) * 8 + f) * TC
                return x2c[:, o:o + TC]

            def i2v(t_, i):
                return t_[:, (i % 2) * TC:(i % 2) * TC + TC]

            blk.gpsimd(g_p45)

            def sy_p5(sy):
                sy.wait_ge(sem["wtst"][0], 16 * (NPC // 2))
                sy.wait_ge(sem["wtst"][1], 16 * (NPC // 2))
                sy.wait_ge(sem["ygst"][0], 16 * (NJ // 2))
                sy.wait_ge(sem["ygst"][1], 16 * (NJ // 2))

                def stores(fc):
                    for f in range(8):
                        sy.wait_ge(sem["x2e"], 8 * fc + f + 1)
                        sy.dma_start(
                            d_x2[f * 128:(f + 1) * 128,
                                 fc * TC:(fc + 1) * TC],
                            x2cv(fc, f)).then_inc(sem["x2st"][fc % 2], 16)
                        sy.wait_ge(sem["xne"], 8 * fc + f + 1)
                        sy.dma_start(
                            d_xn[f * 128:(f + 1) * 128,
                                 fc * TC:(fc + 1) * TC],
                            i2v(xnc, f)).then_inc(sem["xnst"][f % 2], 16)
                for fc in range(NF5):
                    if fc >= 2:
                        sy.wait_ge(sem["tp5"], 8 * (fc - 1))
                    for i in range(4):
                        row0 = fc * TC + i * 128
                        sy.dma_start(xt5v(fc, i),
                                     x_own[row0:row0 + 128, :]
                                     ).then_inc(sem["x5"][fc % 2], 16)
                    if fc == 1:
                        sy.wait_ge(sem["wol"], 16)
                        for k in range(16):
                            sy.dma_start(wob[:, k * DM:(k + 1) * DM],
                                         d_wob[k * 128:(k + 1) * 128, :]
                                         ).then_inc(sem["wol"], 16)
                    if fc >= 1:
                        stores(fc - 1)
                stores(NF5 - 1)

            def t_p5(t):
                t.wait_ge(sem["wol"], 16 * 17)
                for fc in range(NF5):
                    t.wait_ge(sem["ytl"][fc % 2], 32 * (fc // 2 + 1))
                    t.wait_ge(sem["x5"][fc % 2], 64 * (fc // 2 + 1))
                    for f in range(8):
                        idx = 8 * fc + f
                        if idx >= 2:
                            t.wait_ge(sem["tpe5"], idx - 1)
                        for i in range(4):
                            tr = t.transpose(
                                i2v(ps_t5, f)[:, i * 128:(i + 1) * 128],
                                xt5v(fc, i)[:, f * 128:(f + 1) * 128],
                                identf[:, :])
                        tr.then_inc(sem["tp5"], 1)
                        if idx >= 2:
                            t.wait_ge(sem["x2e"], idx - 1)
                        for k in range(16):
                            mm = t.matmul(i2v(ps_o, f),
                                          wob[:, k * DM + f * 128:
                                              k * DM + (f + 1) * 128],
                                          ytlv(fc, k),
                                          start=(k == 0), stop=(k == 15))
                        mm.then_inc(sem["opm"], 1)
                    if fc >= 1:
                        t.wait_ge(sem["nb5"], fc)
                    for k in range(8):
                        idx = 8 * fc + k
                        t.wait_ge(sem["x2e"], idx + 1)
                        t.wait_ge(sem["sq5"], idx + 1)
                        t.matmul(ps_s15[:, :], ones_f[:, 0:1], x2cv(fc, k),
                                 start=(k == 0), stop=(k == 7))
                        t.matmul(ps_s25[:, :], ones_f[:, 0:1], i2v(sqt, k),
                                 start=(k == 0),
                                 stop=(k == 7)).then_inc(sem["st5"], 1)
                    t.wait_ge(sem["nb5"], fc + 1)
                    if fc >= 1:
                        t.wait_ge(sem["sse"], 2 * fc)
                    t.matmul(ps_b5[:, 0:TC], ones_f[0:1, :], sv_s[0:1, :],
                             start=True, stop=True)
                    t.matmul(ps_b5[:, TC:2 * TC], ones_f[0:1, :],
                             sv_ms[0:1, :], start=True,
                             stop=True).then_inc(sem["opb"], 1)

            def s_p5(s):
                for fc in range(NF5):
                    for dt in range(8):
                        idx = 8 * fc + dt
                        s.wait_ge(sem["tp5"], idx + 1)
                        if idx >= 2:
                            s.wait_ge(sem["x2e"], idx - 1)
                        s.copy(i2v(xres, dt),
                               i2v(ps_t5, dt)).then_inc(sem["tpe5"], 1)
                    for k in range(8):
                        idx = 8 * fc + k
                        s.wait_ge(sem["x2e"], idx + 1)
                        if idx >= 2:
                            s.wait_ge(sem["st5"], idx - 1)
                        s.activation(i2v(sqt, k), x2cv(fc, k), AF.Square,
                                     bias=0.0, scale=1.0).then_inc(sem["sq5"], 1)
                    s.wait_ge(sem["vst5"], fc + 1)
                    s.activation(vt1[0:1, :], vv[0:1, :], AF.Ln, bias=1e-5,
                                 scale=1.0)
                    s.activation(vrstd[0:1, :], vt1[0:1, :], AF.Exp, bias=0.0,
                                 scale=-0.5)
                    s.activation(vt1[0:1, :], vq[0:1, :], AF.Ln, bias=1e-35,
                                 scale=1.0)
                    s.activation(vrms2[0:1, :], vt1[0:1, :], AF.Exp, bias=0.0,
                                 scale=0.5).then_inc(sem["ast5"], 1)
                    s.wait_ge(sem["opb"], fc + 1)
                    if fc >= 1:
                        s.wait_ge(sem["xne"], 8 * fc)
                    s.copy(sb5[:, 0:TC], ps_b5[:, 0:TC])
                    s.copy(sb5[:, TC:2 * TC],
                           ps_b5[:, TC:2 * TC]).then_inc(sem["sse"], 2)

            def v_p5(v):
                for fc in range(NF5):
                    for f in range(8):
                        idx = 8 * fc + f
                        v.wait_ge(sem["tpe5"], idx + 1)
                        v.wait_ge(sem["opm"], idx + 1)
                        if fc >= 2:
                            v.wait_ge(sem["x2st"][fc % 2],
                                      128 * ((fc - 2) // 2 + 1))
                            v.wait_ge(sem["st5"], 8 * (fc - 1))
                            v.wait_ge(sem["xne"], 8 * (fc - 1))
                        v.tensor_tensor(x2cv(fc, f), i2v(xres, f),
                                        i2v(ps_o, f),
                                        ALU.add).then_inc(sem["x2e"], 1)
                    v.wait_ge(sem["st5"], 8 * fc + 8)
                    v.tensor_copy(vm[0:1, :], ps_s15[:, :])
                    v.tensor_copy(vv[0:1, :], ps_s25[:, :])
                    v.tensor_single_scalar(vm[0:1, :], vm[0:1, :],
                                           1.0 / DM, ALU.mult)
                    v.tensor_single_scalar(vv[0:1, :], vv[0:1, :],
                                           1.0 / DM, ALU.mult)
                    v.tensor_tensor(vt1[0:1, :], vm[0:1, :], vm[0:1, :],
                                    ALU.mult)
                    v.tensor_tensor(vv[0:1, :], vv[0:1, :], vt1[0:1, :],
                                    ALU.subtract)
                    v.tensor_single_scalar(vt1[0:1, :], vv[0:1, :], 1e-5,
                                           ALU.add)
                    v.reciprocal(vt2[0:1, :], vt1[0:1, :])
                    v.tensor_tensor(vq[0:1, :], vv[0:1, :], vt2[0:1, :],
                                    ALU.mult).then_inc(sem["vst5"], 1)
                    v.wait_ge(sem["ast5"], fc + 1)
                    v.tensor_single_scalar(vt1[0:1, :], vrms2[0:1, :], 1e-8,
                                           ALU.add)
                    v.reciprocal(vt2[0:1, :], vt1[0:1, :])
                    v.tensor_tensor(sv_s[0:1, :], vrstd[0:1, :], vt2[0:1, :],
                                    ALU.mult)
                    v.tensor_tensor(sv_ms[0:1, :], vm[0:1, :], sv_s[0:1, :],
                                    ALU.mult).then_inc(sem["nb5"], 1)
                    v.wait_ge(sem["sse"], 2 * fc + 2)
                    for k in range(8):
                        idx = 8 * fc + k
                        if idx >= 2:
                            v.wait_ge(sem["xnst"][k % 2],
                                      16 * (4 * fc + k // 2))
                        v.tensor_tensor(xtm5[:, :], x2cv(fc, k),
                                        sb5[:, 0:TC], ALU.mult)
                        v.tensor_tensor(i2v(xnc, k), xtm5[:, :],
                                        sb5[:, TC:2 * TC],
                                        ALU.subtract).then_inc(sem["xne"], 1)

            blk.sync(sy_p5)
            blk.tensor(t_p5)
            blk.scalar(s_p5)
            blk.vector(v_p5)

        with ExitStack() as e6:
            SB6 = lambda n, s, d: e6.enter_context(nc.sbuf_tensor(n, s, d))
            PS6 = lambda n, s, d: e6.enter_context(nc.psum_tensor(n, s, d))
            wtu_r = SB6("wtu_r", [128, 8 * DFF], BF16)
            wtd_r = SB6("wtd_r", [128, 32 * DM], BF16)
            xn6 = SB6("xn6", [128, 16 * TC2], BF16)
            x26 = SB6("x26", [128, 16 * TC2], F32)
            h6 = SB6("h6", [128, 32 * TC2], BF16)
            sqh = SB6("sqh", [128, 2 * TC2], BF16)
            w6a = SB6("w6a", [1, TC2], F32)
            w6b = SB6("w6b", [1, TC2], F32)
            w6c = SB6("w6c", [1, TC2], F32)
            w6d = SB6("w6d", [1, TC2], F32)
            w6e = SB6("w6e", [1, TC2], F32)
            w6f = SB6("w6f", [1, TC2], F32)
            sv_s2 = SB6("sv_s2", [1, TC2], F32)
            s2b6 = SB6("s2b6", [128, TC2], F32)
            x36 = SB6("x36", [128, 8 * TC2], F32)
            xtm6 = SB6("xtm6", [128, TC2], F32)
            ostg = SB6("ostg", [128, 2 * DM], F32)
            ps_u = PS6("ps_u", [128, 4 * TC2], F32)
            ps_d = PS6("ps_d", [128, 4 * TC2], F32)
            ps_h2 = PS6("ps_h2", [1, 2 * TC2], F32)
            ps_b6 = PS6("ps_b6", [128, 2 * TC2], F32)
            ps_to = PS6("ps_to", [128, 2 * 512], F32)

            def xn6v(fc, k):
                o = ((fc % 2) * 8 + k) * TC2
                return xn6[:, o:o + TC2]

            def x26v(fc, k):
                o = ((fc % 2) * 8 + k) * TC2
                return x26[:, o:o + TC2]

            def h6v(f):
                return h6[:, f * TC2:(f + 1) * TC2]

            def x36v(f):
                return x36[:, f * TC2:(f + 1) * TC2]

            def j2v(t_, i):
                return t_[:, (i % 2) * TC2:(i % 2) * TC2 + TC2]

            def jpv(ps, i):
                return ps[:, (i % 2) * 2 * TC2:(i % 2) * 2 * TC2 + TC2]

            def tov(g2):
                return ps_to[:, g2 * 512:(g2 + 1) * 512]

            def ostv(i):
                return ostg[:, i * DM:(i + 1) * DM]

            def sy_p6(sy):
                sy.wait_ge(sem["x2st"][0], 128 * (NF5 // 2))
                sy.wait_ge(sem["x2st"][1], 128 * (NF5 // 2))
                sy.wait_ge(sem["xnst"][0], 16 * 4 * NF5)
                sy.wait_ge(sem["xnst"][1], 16 * 4 * NF5)
                sy.wait_ge(sem["wtst"][0], 16 * (NPC // 2))
                sy.wait_ge(sem["wtst"][1], 16 * (NPC // 2))
                for k in range(8):
                    sy.dma_start(xn6v(0, k),
                                 d_xn[k * 128:(k + 1) * 128, 0:TC2]
                                 ).then_inc(sem["xnl"][0], 16)
                    sy.dma_start(x26v(0, k),
                                 d_x2[k * 128:(k + 1) * 128, 0:TC2]
                                 ).then_inc(sem["xnl"][0], 16)
                for j in range(8):
                    sy.dma_start(wtu_r[:, j * DFF:(j + 1) * DFF],
                                 d_wtu[j * 128:(j + 1) * 128, :]
                                 ).then_inc(sem["wtl"], 16)
                for j in range(32):
                    sy.dma_start(wtd_r[:, j * DM:(j + 1) * DM],
                                 d_wtd[j * 128:(j + 1) * 128, :]
                                 ).then_inc(sem["wtl"], 16)

                def ostores(fc):
                    for i in range(2):
                        sy.wait_ge(sem["tpe6"], 4 * fc + 2 * (i + 1))
                        sy.dma_start(
                            out[fc * TC2 + i * 128:
                                fc * TC2 + (i + 1) * 128, :],
                            ostv(i)).then_inc(sem["ost"][i], 16)
                for fc in range(1, NF6 + 1):
                    if fc >= 2:
                        sy.wait_ge(sem["upm"], 32 * (fc - 1))
                        sy.wait_ge(sem["x3e"], 8 * (fc - 1))
                    if fc < NF6:
                        for k in range(8):
                            sy.dma_start(xn6v(fc, k),
                                         d_xn[k * 128:(k + 1) * 128,
                                              fc * TC2:(fc + 1) * TC2]
                                         ).then_inc(sem["xnl"][fc % 2], 16)
                            sy.dma_start(x26v(fc, k),
                                         d_x2[k * 128:(k + 1) * 128,
                                              fc * TC2:(fc + 1) * TC2]
                                         ).then_inc(sem["xnl"][fc % 2], 16)
                    ostores(fc - 1)
                sy.wait_ge(sem["ost"][0], 16 * NF6)
                sy.wait_ge(sem["ost"][1], 16 * NF6)

            def t_p6(t):
                t.wait_ge(sem["wtl"], 16 * 8)
                for fc in range(NF6):
                    t.wait_ge(sem["xnl"][fc % 2], 256 * (fc // 2 + 1))
                    if fc >= 1:
                        t.wait_ge(sem["dnm"], 8 * fc)
                    for f in range(32):
                        idx = 32 * fc + f
                        if idx >= 2:
                            t.wait_ge(sem["gel"], idx - 1)
                        for k in range(8):
                            mm = t.matmul(jpv(ps_u, f),
                                          wtu_r[:, k * DFF + f * 128:
                                                k * DFF + (f + 1) * 128],
                                          xn6v(fc, k),
                                          start=(k == 0), stop=(k == 7))
                        mm.then_inc(sem["upm"], 1)
                        if f >= 2:
                            g2 = f - 2
                            if g2 == 0:
                                t.wait_ge(sem["s2v"], fc)
                            t.wait_ge(sem["hsq"], 32 * fc + g2 + 1)
                            t.matmul(ps_h2[0:1, 0:TC2], ones_b[:, 0:1],
                                     j2v(sqh, g2),
                                     start=(g2 == 0),
                                     stop=False).then_inc(sem["h2m"], 1)
                    for g2 in (30, 31):
                        t.wait_ge(sem["hsq"], 32 * fc + g2 + 1)
                        t.matmul(ps_h2[0:1, 0:TC2], ones_b[:, 0:1], j2v(sqh, g2),
                                 start=False,
                                 stop=(g2 == 31)).then_inc(sem["h2m"], 1)
                    t.wait_ge(sem["s2v"], fc + 1)
                    if fc >= 1:
                        t.wait_ge(sem["sbe"], fc)
                    t.matmul(ps_b6[:, 0:TC2], ones_f[0:1, :], sv_s2[0:1, :],
                             start=True, stop=True).then_inc(sem["opb"], 1)
                    if fc == 0:
                        t.wait_ge(sem["wtl"], 16 * 40)
                    t.wait_ge(sem["gel"], 32 * (fc + 1))
                    for fo in range(8):
                        idx = 8 * fc + fo
                        if idx >= 2:
                            t.wait_ge(sem["x3e"], idx - 1)
                        for k in range(32):
                            mm = t.matmul(jpv(ps_d, fo),
                                          wtd_r[:, k * DM + fo * 128:
                                                k * DM + (fo + 1) * 128],
                                          h6v(k),
                                          start=(k == 0), stop=(k == 31))
                        mm.then_inc(sem["dnm"], 1)
                    for i in range(2):
                        for g2 in range(2):
                            tpidx = 4 * fc + 2 * i + g2
                            if tpidx >= 2:
                                t.wait_ge(sem["tpe6"], tpidx - 1)
                            for q in range(4):
                                dt = g2 * 4 + q
                                t.wait_ge(sem["x3e"], 8 * fc + dt + 1)
                                tr = t.transpose(
                                    tov(g2)[:, q * 128:(q + 1) * 128],
                                    x36v(dt)[:, i * 128:(i + 1) * 128],
                                    identf[:, :])
                            tr.then_inc(sem["tp6"], 1)

            def s_p6(s):
                for fc in range(NF6):
                    for f in range(32):
                        idx = 32 * fc + f
                        if f == 0 and fc >= 1:
                            s.wait_ge(sem["dnm"], 8 * fc)
                        s.wait_ge(sem["upm"], idx + 1)
                        s.activation(h6v(f), jpv(ps_u, f), AF.Gelu,
                                     bias=0.0, scale=1.0).then_inc(sem["gel"], 1)
                        if f >= 2:
                            s.wait_ge(sem["h2m"], 32 * fc + f - 1)
                        s.activation(j2v(sqh, f), h6v(f), AF.Square,
                                     bias=0.0, scale=1.0).then_inc(sem["hsq"], 1)
                    s.wait_ge(sem["lnv"], 4 * NCH + fc + 1)
                    s.activation(w6d[0:1, :], w6a[0:1, :], AF.Ln, bias=1e-35,
                                 scale=1.0)
                    s.activation(w6c[0:1, :], w6d[0:1, :], AF.Exp, bias=0.0,
                                 scale=0.5).then_inc(sem["lns"], 1)
                    s.wait_ge(sem["opb"], NF5 + fc + 1)
                    if fc >= 1:
                        s.wait_ge(sem["x3e"], 8 * fc)
                    s.copy(s2b6[:, :], ps_b6[:, 0:TC2]).then_inc(sem["sbe"], 1)
                    for i in range(2):
                        for g2 in range(2):
                            tpidx = 4 * fc + 2 * i + g2
                            s.wait_ge(sem["tp6"], tpidx + 1)
                            if fc >= 1:
                                s.wait_ge(sem["ost"][i], 16 * fc)
                            s.copy(ostv(i)[:, g2 * 512:(g2 + 1) * 512],
                                   tov(g2)).then_inc(sem["tpe6"], 1)

            def v_p6(v):
                for fc in range(NF6):
                    v.wait_ge(sem["h2m"], 32 * (fc + 1))
                    v.tensor_copy(w6a[0:1, :],
                                  ps_h2[0:1, 0:TC2]).then_inc(sem["lnv"], 1)
                    v.wait_ge(sem["lns"], 4 * NCH + fc + 1)
                    v.tensor_single_scalar(w6b[0:1, :], w6c[0:1, :],
                                           1.0 / 64.0, ALU.mult)
                    v.tensor_single_scalar(w6b[0:1, :], w6b[0:1, :], 1e-8,
                                           ALU.add)
                    v.reciprocal(sv_s2[0:1, :],
                                 w6b[0:1, :]).then_inc(sem["s2v"], 1)
                    v.wait_ge(sem["sbe"], fc + 1)
                    for fo in range(8):
                        idx = 8 * fc + fo
                        v.wait_ge(sem["dnm"], idx + 1)
                        if fc >= 1 and fo == 0:
                            v.wait_ge(sem["tp6"], 4 * fc)
                        v.tensor_tensor(xtm6[:, :], jpv(ps_d, fo),
                                        s2b6[:, :], ALU.mult)
                        v.tensor_tensor(x36v(fo), xtm6[:, :], x26v(fc, fo),
                                        ALU.add).then_inc(sem["x3e"], 1)

            blk.sync(sy_p6)
            blk.tensor(t_p6)
            blk.scalar(s_p6)
            blk.vector(v_p6)

    return nc


_CACHE = {}


def kernel(L0_attn_hidden_states, L0_attn_in_proj_w, L0_attn_conv_w,
           L0_attn_conv_b, L0_attn_x_proj_w, L0_attn_dt_proj_w,
           L0_attn_dt_proj_b, L0_attn_A_log, L0_attn_D_skip,
           L0_attn_out_proj_w, L0_ffn_w_up, L0_ffn_w_down):
    h = np.asarray(L0_attn_hidden_states, dtype=np.float32)
    ipw = np.asarray(L0_attn_in_proj_w, dtype=np.float32)
    cw = np.asarray(L0_attn_conv_w, dtype=np.float32)
    cb = np.asarray(L0_attn_conv_b, dtype=np.float32)
    xpw = np.asarray(L0_attn_x_proj_w, dtype=np.float32)
    dtw = np.asarray(L0_attn_dt_proj_w, dtype=np.float32)
    dtb = np.asarray(L0_attn_dt_proj_b, dtype=np.float32)
    alog = np.asarray(L0_attn_A_log, dtype=np.float32)
    dsk = np.asarray(L0_attn_D_skip, dtype=np.float32)
    opw = np.asarray(L0_attn_out_proj_w, dtype=np.float32)
    wup = np.asarray(L0_ffn_w_up, dtype=np.float32)
    wdn = np.asarray(L0_ffn_w_down, dtype=np.float32)

    B = h.shape[0]
    eyem = np.eye(128, dtype=np.float32)
    ones = np.ones((128, 128), dtype=np.float32)
    selm = np.zeros((32, 32 * 128), dtype=np.float32)
    for n in range(32):
        selm[n, n * 128:(n + 1) * 128] = 1.0
    wupt = np.ascontiguousarray(wup.T)
    wdnt = np.ascontiguousarray(wdn.T)

    if "nc" not in _CACHE:
        _CACHE["nc"] = build()
    ncp = _CACHE["nc"]

    def dslice(vec, r):
        return np.ascontiguousarray(
            vec[r * DH:(r + 1) * DH].reshape(8, 128).T)

    in_maps = []
    for c in range(8):
        b, rr = c >> 1, c & 1
        in_maps.append({
            "x_seq": np.ascontiguousarray(h[b]),
            "x_own": np.ascontiguousarray(h[b, rr * TOWN:(rr + 1) * TOWN]),
            "w_in": np.ascontiguousarray(np.concatenate(
                [ipw[:, rr * DH:(rr + 1) * DH],
                 ipw[:, DI + rr * DH:DI + (rr + 1) * DH]], axis=1)),
            "w_xp": np.ascontiguousarray(xpw[rr * DH:(rr + 1) * DH, :]),
            "w_dt": np.ascontiguousarray(dtw[:, rr * DH:(rr + 1) * DH]),
            "b_dt": dslice(dtb, rr),
            "a_log": np.ascontiguousarray(
                alog[rr * DH:(rr + 1) * DH].reshape(8, 128, NST)
                .transpose(1, 0, 2).reshape(128, 128)),
            "conv_w4": np.ascontiguousarray(
                cw[rr * DH:(rr + 1) * DH].reshape(8, 128, 4)
                .transpose(1, 0, 2).reshape(128, 32)),
            "conv_b8": dslice(cb, rr),
            "dsk8": dslice(dsk, rr),
            "w_out": opw,
            "w_up_t": wupt,
            "w_dn_t": wdnt,
            "eye": eyem,
            "onesm": ones,
            "selbc": selm,
            "roff": np.array([[2 * rr, 2 * rr + 1]], dtype=np.int32),
            "cvals": np.tile(np.array(
                [[1.0, 1e-5, 1e-35, -1.0, 0.5, -0.5, 0.0, 0.0]],
                dtype=np.float32), (128, 1)),
        })

    res = run_bass_kernel_spmd(ncp, in_maps, core_ids=list(range(8)))
    _CACHE["res"] = res

    out_full = np.empty((B, L, DM), dtype=np.float32)
    for c in range(8):
        b, rr = c >> 1, c & 1
        out_full[b, rr * TOWN:(rr + 1) * TOWN, :] = res.results[c]["out"]
    return out_full
